# version 05
# baseline (speedup 1.0000x reference)
import numpy as np
import concourse.bass as bass
import concourse.mybir as mybir
from concourse.bass_utils import run_bass_kernel_spmd

F32 = mybir.dt.float32
BF16 = mybir.dt.bfloat16
AF = mybir.ActivationFunctionType
ALU = mybir.AluOpType

C = 1024
NCT = 8
T = 4096
HALF = 2048
CH = 256
NCH = 8
FFND = 4096
NFT = 32
NL = 4
HEADS = 16
DH = 64
EPS = 1e-5
INV_C = 1.0 / C


SERIALIZE = False
SER_SAME_ENGINES = ("vector", "scalar", "gpsimd")
DETECT_RACES = False
FFN_TRUNC = None

ENG_SEM = {"tensor": "t", "vector": "v", "scalar": "s", "gpsimd": "g",
           "sync": "sd"}


class Sched:
    ENGINES = ("tensor", "vector", "scalar", "gpsimd", "sync")

    def __init__(self, nc):
        self.nc = nc
        self.ops = {e: [] for e in self.ENGINES}
        self.sem = {}
        self.count = {}
        self.seen = {e: {} for e in self.ENGINES}

    def add_sem(self, name, handle):
        self.sem[name] = handle
        self.count[name] = 0
        for e in self.ENGINES:
            self.seen[e][name] = 0

    def op(self, eng, fn, waits=(), inc=None, inc_n=1):
        for s, v in waits:
            assert v <= self.count[s], f"wait {s}>={v} but only {self.count[s]} emitted"
        if SERIALIZE:
            waits = list(waits) + [(s, self.count[s]) for s in self.count
                                   if s != ENG_SEM[eng]]
            if inc is None:
                inc = ENG_SEM[eng]
                inc_n = 16 if inc in ("sd", "gd") else 1
        elif eng in SER_SAME_ENGINES:
            own = ENG_SEM[eng]
            waits = list(waits) + [(own, self.count[own])]
            if inc is None:
                inc = own
                inc_n = 16 if inc in ("sd", "gd") else 1
        if inc is not None:
            self.count[inc] += inc_n
        sems = self.sem
        seen = self.seen[eng]
        eff = {}
        for s, v in waits:
            if v > seen[s] and v > eff.get(s, 0):
                eff[s] = v
        for s, v in eff.items():
            seen[s] = v
        waits = tuple(eff.items())

        def closure(e, fn=fn, waits=waits, inc=inc, inc_n=inc_n):
            for s, v in waits:
                e.wait_ge(sems[s], v)
            inst = fn(e)
            if inc is not None and inst is not None:
                inst.then_inc(sems[inc], inc_n)
        self.ops[eng].append(closure)
        return self.count[inc] if inc is not None else None

    def m(self, sem):
        return self.count[sem]


def r3(ap, nb):
    return ap.rearrange("p (b w) -> p b w", b=nb)


def _par_map():
    m = {}
    col = 0

    def add(name, n=8):
        nonlocal col
        m[name] = col
        col += n
    for l in range(NL):
        for nm in ("Ak", "Bk", "Av", "Bv", "Ar", "Br",
                   "Ack", "Bck", "Acr", "Bcr", "adec", "Eu"):
            add(f"{nm}{l}")
    for nm in ("fAr", "fBr", "fAk", "fBk", "lnfw", "bq", "bk", "bo", "bv"):
        add(nm)
    add("G", NL * 8)
    add("H", NL * 8)
    add("halo", 10 * 8)
    add("epsc", 8)
    add("onesf", 128)
    assert col <= 768, col
    return m, col


PMAP, PAR_USED = _par_map()

VEC_ORDER = ([f"raw_{nm}{l}" for l in range(NL)
              for nm in ("ln1w", "ln2w", "mk", "mv", "mr", "cmk", "cmr", "td", "tf")]
             + ["raw_mur", "raw_muk", "lnfw", "bq", "bk", "bo", "bv"])
VEC_COLS = len(VEC_ORDER) * 8

PB_RSTD = 0
PB_MU = 2 * CH
PB_ONESROW = 4 * CH
PB_ONESCOL = 4 * CH + 128
PARB_COLS = 4 * CH + 132


def pcol(name, l=None):
    return PMAP[name if l is None else f"{name}{l}"]


def build_program(debug_stage=None):
    nc = bass.Bass("TRN2", target_bir_lowering=False,
                   detect_race_conditions=DETECT_RACES)

    xt = nc.dram_tensor("xt", [C, T], F32, kind="ExternalInput").ap()
    out = nc.dram_tensor("out", [C, T], F32, kind="ExternalOutput").ap()
    tmkT = nc.dram_tensor("tmkT", [NL, C, C], F32, kind="ExternalInput").ap()
    tmvT = nc.dram_tensor("tmvT", [NL, C, C], F32, kind="ExternalInput").ap()
    tmrT = nc.dram_tensor("tmrT", [NL, C, C], F32, kind="ExternalInput").ap()
    tmoT = nc.dram_tensor("tmoT", [NL, C, C], F32, kind="ExternalInput").ap()
    cmkT = nc.dram_tensor("cmkT", [NL, C, FFND], F32, kind="ExternalInput").ap()
    cmvT = nc.dram_tensor("cmvT", [NL, FFND, C], F32, kind="ExternalInput").ap()
    cmrT = nc.dram_tensor("cmrT", [NL, C, C], F32, kind="ExternalInput").ap()
    wqT = nc.dram_tensor("wqT", [C, C], F32, kind="ExternalInput").ap()
    wkT = nc.dram_tensor("wkT", [C, C], F32, kind="ExternalInput").ap()
    wvT = nc.dram_tensor("wvT", [C, C], F32, kind="ExternalInput").ap()
    woT = nc.dram_tensor("woT", [C, C], F32, kind="ExternalInput").ap()
    frT = nc.dram_tensor("frT", [C, C], F32, kind="ExternalInput").ap()
    fkT = nc.dram_tensor("fkT", [C, C], F32, kind="ExternalInput").ap()
    fvT = nc.dram_tensor("fvT", [C, C], F32, kind="ExternalInput").ap()
    vecs = nc.dram_tensor("vecs", [128, VEC_COLS], F32, kind="ExternalInput").ap()
    cmk_s = nc.dram_tensor("cmk_s", [NL, C, FFND], BF16, kind="Internal").ap()
    cmv_s = nc.dram_tensor("cmv_s", [NL, FFND, C], BF16, kind="Internal").ap()
    ksp = nc.dram_tensor("ksp", [C, T], BF16, kind="Internal").ap()
    vtsp = nc.dram_tensor("vtsp", [128, 8 * 32 * 130], BF16, kind="Internal").ap()
    qsp = nc.dram_tensor("qsp", [C, T], BF16, kind="Internal").ap()

    import contextlib
    with contextlib.ExitStack() as stack:
        ent = stack.enter_context
        XRES = ent(nc.sbuf_tensor("XRES", [128, NCT * HALF], F32))
        ARENA = ent(nc.sbuf_tensor("ARENA", [128, 4 * NCT * 1024], BF16))
        B0 = ent(nc.sbuf_tensor("B0", [128, NCT * CH], F32))
        B1 = ent(nc.sbuf_tensor("B1", [128, NCT * CH], F32))
        B2 = ent(nc.sbuf_tensor("B2", [128, 2080], F32))
        B3 = ent(nc.sbuf_tensor("B3", [128, 2080], F32))
        XHATp = [ent(nc.sbuf_tensor(f"XHAT{i}", [128, NCT * (CH + 1)], BF16))
                 for i in range(2)]
        RWB = ent(nc.sbuf_tensor("RWB", [128, NCT * CH], BF16))
        SQB = ent(nc.sbuf_tensor("SQB", [128, NCT * CH], BF16))
        MXAp = [ent(nc.sbuf_tensor(f"MXA{i}", [128, NCT * CH], BF16))
                for i in range(2)]
        MXBp = [ent(nc.sbuf_tensor(f"MXB{i}", [128, NCT * CH], BF16))
                for i in range(2)]
        MXCp = [ent(nc.sbuf_tensor(f"MXC{i}", [128, NCT * CH], BF16))
                for i in range(2)]
        BCB = ent(nc.sbuf_tensor("BCB", [128, 2 * CH], BF16))
        PAR = ent(nc.sbuf_tensor("PAR", [128, PAR_USED], F32))
        PARB = ent(nc.sbuf_tensor("PARB", [128, PARB_COLS], BF16))
        PS = ent(nc.psum_tensor("PS", [128, 4096], F32))
        sem_t = ent(nc.semaphore("sem_t"))
        sem_v = ent(nc.semaphore("sem_v"))
        sem_s = ent(nc.semaphore("sem_s"))
        sem_g = ent(nc.semaphore("sem_g"))
        sem_gd = ent(nc.semaphore("sem_gd"))
        sem_sd = ent(nc.semaphore("sem_sd"))
        blk = ent(nc.Block())

        S = Sched(nc)
        for nm, h in [("t", sem_t), ("v", sem_v), ("s", sem_s),
                      ("g", sem_g), ("gd", sem_gd), ("sd", sem_sd)]:
            S.add_sem(nm, h)

        cx = dict(
            xt=xt, out=out, tmkT=tmkT, tmvT=tmvT, tmrT=tmrT, tmoT=tmoT,
            cmkT=cmkT, cmvT=cmvT, cmrT=cmrT, wqT=wqT, wkT=wkT, wvT=wvT,
            woT=woT, frT=frT, fkT=fkT, fvT=fvT, vecs=vecs,
            cmk_s=cmk_s, cmv_s=cmv_s, ksp=ksp, vtsp=vtsp, qsp=qsp,
            XRES=XRES, ARENA=ARENA, B0=B0, B1=B1, B2=B2, B3=B3,
            XHATp=XHATp, RWB=RWB, SQB=SQB, MXAp=MXAp, MXBp=MXBp,
            MXCp=MXCp, BCB=BCB, PAR=PAR, PARB=PARB, PS=PS,
            free={},
            bankfree=[("v", 0)] * 8,
        )
        emit_all(S, cx, debug_stage)

        @blk.tensor
        def _(e):
            for f in S.ops["tensor"]:
                f(e)

        @blk.vector
        def _(e):
            for f in S.ops["vector"]:
                f(e)

        @blk.scalar
        def _(e):
            for f in S.ops["scalar"]:
                f(e)

        @blk.gpsimd
        def _(e):
            for f in S.ops["gpsimd"]:
                f(e)

        @blk.sync
        def _(e):
            for f in S.ops["sync"]:
                f(e)

    return nc



def fget(cx, name):
    return cx["free"].get(name, ("v", 0))


def fset(cx, name, sem, val):
    cx["free"][name] = (sem, val)


def emit_all(S, cx, debug_stage):
    prologue(S, cx)
    passes = (0,)
    if debug_stage and debug_stage.startswith("1"):
        debug_stage = debug_stage[1:]
    else:
        passes = (0, 1)
    for p in passes:
        pass_begin(S, cx, p)
        for c in range(NCH):
            emit_ln_fp32_inplace(S, cx, c)
        if debug_stage == "h0":
            pass_store(S, cx, p)
            continue
        stop = emit_rwkv_stack(S, cx, p, debug_stage)
        if stop:
            pass_store(S, cx, p)
            continue
        emit_lnf_add(S, cx, p)
        if debug_stage == "x1":
            pass_store(S, cx, p)
            continue
        emit_ffn(S, cx, p, site=8)
        if debug_stage == "x2":
            pass_store(S, cx, p)
            continue
        emit_attn(S, cx, p)
        if debug_stage == "x3":
            pass_store(S, cx, p)
            continue
        emit_ffn(S, cx, p, site=9)
        pass_store(S, cx, p)

    S.op("sync", lambda e: None, waits=[("sd", S.m("sd"))])


def prologue(S, cx):
    PAR, PARB, B0 = cx["PAR"], cx["PARB"], cx["B0"]

    S.op("sync", lambda e: e.dma_start(B0[:, 0:VEC_COLS], cx["vecs"]),
         inc="sd", inc_n=16)
    lded = S.m("sd")

    def rawc(name, l=None):
        i = VEC_ORDER.index(name if l is None else f"{name}{l}")
        return B0[:, i * 8:(i + 1) * 8]

    def c(name, l=None):
        col = pcol(name, l)
        return PAR[:, col:col + 8]

    for pp in (0, 32, 64):
        S.op("vector", lambda e, pp=pp: e.memset(
            PARB[pp:pp + 1, PB_ONESROW:PB_ONESROW + 128], 1.0), inc="v")
    S.op("vector", lambda e: e.memset(PARB[:, PB_ONESCOL:PB_ONESCOL + 1], 1.0),
         inc="v")
    of = PMAP["onesf"]
    S.op("vector", lambda e: e.memset(PAR[0:1, of:of + 128], 1.0), inc="v")
    S.op("vector", lambda e: e.memset(PAR[32:33, of:of + 128], 1.0), inc="v")
    ec = PMAP["epsc"]
    S.op("vector", lambda e: e.memset(PAR[:, ec:ec + 1], EPS), inc="v")
    S.op("vector", lambda e: e.memset(PAR[:, PMAP["G"]:PMAP["G"] + NL * 8], 0.0),
         inc="v")
    S.op("vector", lambda e: e.memset(PAR[:, PMAP["H"]:PMAP["H"] + NL * 8], 0.0),
         inc="v")
    S.op("vector", lambda e: e.memset(PAR[:, PMAP["halo"]:PMAP["halo"] + 80], 0.0),
         inc="v")

    for l in range(NL):
        for (a, b, mix, w) in [("Ak", "Bk", "raw_mk", "raw_ln1w"),
                               ("Av", "Bv", "raw_mv", "raw_ln1w"),
                               ("Ar", "Br", "raw_mr", "raw_ln1w"),
                               ("Ack", "Bck", "raw_cmk", "raw_ln2w"),
                               ("Acr", "Bcr", "raw_cmr", "raw_ln2w")]:
            S.op("vector", lambda e, a=a, mix=mix, w=w, l=l: e.tensor_tensor(
                c(a, l), rawc(mix, l), rawc(w, l), ALU.mult),
                waits=[("sd", lded)], inc="v")
            S.op("vector", lambda e, a=a, b=b, w=w, l=l: e.tensor_tensor(
                c(b, l), rawc(w, l), c(a, l), ALU.subtract), inc="v")
        S.op("scalar", lambda e, l=l: e.activation(
            c("adec", l), rawc("raw_td", l), AF.Exp), waits=[("sd", lded)],
            inc="s")
        S.op("scalar", lambda e, l=l: e.activation(
            c("adec", l), c("adec", l), AF.Exp, scale=-1.0), inc="s")
        S.op("scalar", lambda e, l=l: e.activation(
            c("Eu", l), rawc("raw_tf", l), AF.Exp, scale=-1.0), inc="s")
    for (a, b, raw) in [("fAr", "fBr", "raw_mur"), ("fAk", "fBk", "raw_muk")]:
        S.op("vector", lambda e, a=a, raw=raw: e.tensor_copy(c(a), rawc(raw)),
             waits=[("sd", lded)], inc="v")
        S.op("vector", lambda e, b=b, raw=raw: e.tensor_scalar(
            c(b), rawc(raw), -1.0, 1.0, ALU.mult, ALU.add), inc="v")
    for nm in ("lnfw", "bq", "bk", "bo", "bv"):
        S.op("vector", lambda e, nm=nm: e.tensor_copy(c(nm), rawc(nm)),
             waits=[("sd", lded)], inc="v")
    fset(cx, "B0", "v", S.m("v"))
    cx["params_v"] = S.m("v")
    cx["params_s"] = S.m("s")

    cx["cmprep"] = {}
    for l in range(NL):
        S.op("gpsimd", lambda e, l=l: e.dma_start(cx["cmk_s"][l], cx["cmkT"][l]),
             inc="gd", inc_n=16)
        S.op("gpsimd", lambda e, l=l: e.dma_start(cx["cmv_s"][l], cx["cmvT"][l]),
             inc="gd", inc_n=16)
        cx["cmprep"][l] = S.m("gd")


def pass_begin(S, cx, p):
    XRES = cx["XRES"]
    cx["pass"] = p
    src = cx["xt"].rearrange("(ct p) t -> p ct t", p=128)[:, :, p * HALF:(p + 1) * HALF]
    waits = [fget(cx, "XRES")]
    S.op("sync", lambda e: e.dma_start(r3(XRES[:], NCT), src),
         waits=waits, inc="sd", inc_n=16)
    cx["xres_ready"] = [("sd", S.m("sd"))]


def pass_store(S, cx, p):
    XRES = cx["XRES"]
    dst = cx["out"].rearrange("(ct p) t -> p ct t", p=128)[:, :, p * HALF:(p + 1) * HALF]
    S.op("sync", lambda e: e.dma_start(dst, r3(XRES[:], NCT)),
         waits=list(cx["xres_ready"]), inc="sd", inc_n=16)
    fset(cx, "XRES", "sd", S.m("sd"))


def emit_stats(S, cx, src3, src_waits, pbank):
    PS, SQB, PARB = cx["PS"], cx["SQB"], cx["PARB"]
    par = pbank - 6
    XHAT = cx["XHATp"][par]
    xhv = r3(XHAT[:], NCT)[:, :, 1:1 + CH]
    SQF = SQB[:].bitcast(F32)
    bank = PS[:, pbank * 512:(pbank + 1) * 512]

    S.op("scalar", lambda e: e.activation(xhv, src3, AF.Copy),
         waits=list(src_waits) + [fget(cx, f"XHAT{par}")], inc="s")
    cast_s = S.m("s")
    S.op("vector", lambda e: e.tensor_tensor(r3(SQB[:], NCT), xhv, xhv,
                                             ALU.mult),
         waits=[("s", cast_s), fget(cx, "SQB")], inc="v")
    sq_v = S.m("v")

    onescol = PARB[:, PB_ONESCOL:PB_ONESCOL + 1]

    def fsum(t):
        inst = None
        for ct in range(NCT):
            inst = t.matmul(bank[0:1, 0:CH], onescol,
                            r3(XHAT[:], NCT)[:, ct, 1:1 + CH],
                            start=(ct == 0), stop=(ct == NCT - 1))
        for ct in range(NCT):
            inst = t.matmul(bank[0:1, CH:2 * CH], onescol,
                            r3(SQB[:], NCT)[:, ct],
                            start=(ct == 0), stop=(ct == NCT - 1))
        return inst
    S.op("tensor", fsum, waits=[("s", cast_s), ("v", sq_v),
                                cx["bankfree"][pbank],
                                ("v", cx["params_v"]), ("s", cx["params_s"])],
         inc="t")
    stats_t = S.m("t")
    fset(cx, f"XHAT{par}", "t", stats_t)
    fset(cx, "SQB", "t", stats_t)

    mu_f = SQF[0:1, 0:CH]
    var_f = SQF[0:1, CH:2 * CH]
    rstd_f = SQF[0:1, 2 * CH:3 * CH]
    scr_f = SQF[0:1, 3 * CH:4 * CH]
    S.op("vector", lambda e: e.tensor_scalar(mu_f, bank[0:1, 0:CH], INV_C,
                                             None, ALU.mult),
         waits=[("t", stats_t)], inc="v")
    S.op("vector", lambda e: e.tensor_scalar(var_f, bank[0:1, CH:2 * CH],
                                             INV_C, None, ALU.mult), inc="v")
    S.op("vector", lambda e: e.tensor_tensor(scr_f, mu_f, mu_f, ALU.mult),
         inc="v")
    S.op("vector", lambda e: e.tensor_tensor(var_f, var_f, scr_f,
                                             ALU.subtract), inc="v")
    fin_v = S.m("v")

    epsap = cx["PAR"][0:1, PMAP["epsc"]:PMAP["epsc"] + 1]
    S.op("scalar", lambda e: e.activation(scr_f, var_f, AF.Ln, bias=epsap),
         waits=[("v", fin_v)], inc="s")
    S.op("scalar", lambda e: e.activation(rstd_f, scr_f, AF.Exp, scale=-0.5),
         inc="s")
    S.op("scalar", lambda e: e.activation(PARB[0:1, PB_RSTD + par * CH:
                                               PB_RSTD + par * CH + CH],
                                          rstd_f, AF.Copy), inc="s")
    S.op("scalar", lambda e: e.activation(PARB[0:1, PB_MU + par * CH:
                                               PB_MU + par * CH + CH],
                                          mu_f, AF.Copy), inc="s")
    rows_s = S.m("s")
    fset(cx, "SQB", "s", rows_s)
    return dict(rows_s=rows_s, fin_v=fin_v, pbank=pbank, par=par,
                mu_f=mu_f, rstd_f=rstd_f)


def emit_bcast(S, cx, st, fp32=False):
    PS, PARB, PAR = cx["PS"], cx["PARB"], cx["PAR"]
    par = st["par"]
    bank = PS[:, st["pbank"] * 512:(st["pbank"] + 1) * 512]
    if fp32:
        lhs_r = PAR[0:1, PMAP["onesf"]:PMAP["onesf"] + 128]
        lhs_m = lhs_r
        rstd_src = st["rstd_f"]
        mu_src = st["mu_f"]
        w = [("s", st["rows_s"]), ("v", st["fin_v"])]
    else:
        lhs_r = PARB[0:1, PB_ONESROW:PB_ONESROW + 128]
        lhs_m = lhs_r
        rstd_src = PARB[0:1, PB_RSTD + par * CH:PB_RSTD + par * CH + CH]
        mu_src = PARB[0:1, PB_MU + par * CH:PB_MU + par * CH + CH]
        w = [("s", st["rows_s"])]

    def fbc(e):
        e.matmul(bank[:, 0:CH], lhs_r, rstd_src, start=True, stop=True)
        return e.matmul(bank[:, CH:2 * CH], lhs_m, mu_src, start=True,
                        stop=True)
    S.op("tensor", fbc, waits=w, inc="t")
    return S.m("t")


def emit_ln_fp32_inplace(S, cx, c):
    XRES, B0, PS = cx["XRES"], cx["B0"], cx["PS"]
    pbank = 6 + c % 2
    xr = r3(XRES[:], NCT)[:, :, c * CH:(c + 1) * CH]
    st = emit_stats(S, cx, xr, cx["xres_ready"], pbank)
    bc_t = emit_bcast(S, cx, st, fp32=True)
    fset(cx, "SQB", "t", bc_t)
    bank = PS[:, pbank * 512:(pbank + 1) * 512]
    mu_bc = bank[:, CH:2 * CH].unsqueeze(1).broadcast_to([128, NCT, CH])
    rstd_bc = bank[:, 0:CH].unsqueeze(1).broadcast_to([128, NCT, CH])
    b0r = r3(B0[:], NCT)
    S.op("vector", lambda e: e.tensor_tensor(b0r, xr, mu_bc, ALU.subtract),
         waits=[("t", bc_t), fget(cx, "B0")], inc="v")
    S.op("vector", lambda e: e.tensor_tensor(xr, b0r, rstd_bc, ALU.mult),
         inc="v")
    done_v = S.m("v")
    cx["bankfree"][pbank] = ("v", done_v)
    fset(cx, "B0", "v", done_v)
    cx["xres_ready"] = [("v", done_v)]


def emit_ln_prep(S, cx, c, site, mixes):
    XRES, PAR, PS = cx["XRES"], cx["PAR"], cx["PS"]
    par = c % 2
    pbank = 6 + par
    XHAT = cx["XHATp"][par]
    BCB = cx["BCB"]
    xr = r3(XRES[:], NCT)[:, :, c * CH:(c + 1) * CH]
    st = emit_stats(S, cx, xr, cx["xres_ready"], pbank)
    bc_t = emit_bcast(S, cx, st, fp32=False)
    bcsrc = r3(PS[:, pbank * 512:(pbank + 1) * 512][:, 0:2 * CH], 2)
    S.op("scalar", lambda e: e.activation(r3(BCB[:], 2), bcsrc, AF.Copy),
         waits=[("t", bc_t), fget(cx, "BCB")], inc="s")
    bcb_s = S.m("s")
    cx["bankfree"][pbank] = ("s", bcb_s)

    xh = r3(XHAT[:], NCT)
    xh_cur = xh[:, :, 1:1 + CH]
    mu_bc = BCB[:, CH:2 * CH].unsqueeze(1).broadcast_to([128, NCT, CH])
    rstd_bc = BCB[:, 0:CH].unsqueeze(1).broadcast_to([128, NCT, CH])
    if site is not None:
        hc = PMAP["halo"] + site * 8
        S.op("vector", lambda e: e.tensor_copy(
            xh[:, :, 0:1], PAR[:, hc:hc + 8].bitcast(BF16)[:, 0:NCT]
            .rearrange("p (b w) -> p b w", b=NCT)), inc="v")
    S.op("vector", lambda e: e.tensor_tensor(xh_cur, xh_cur, mu_bc,
                                             ALU.subtract),
         waits=[("s", bcb_s)], inc="v")
    S.op("vector", lambda e: e.tensor_tensor(xh_cur, xh_cur, rstd_bc,
                                             ALU.mult), inc="v")
    xhat_v = S.m("v")
    fset(cx, "BCB", "v", xhat_v)
    if site is not None:
        hc = PMAP["halo"] + site * 8
        S.op("vector", lambda e: e.tensor_copy(
            PAR[:, hc:hc + 8].bitcast(BF16)[:, 0:NCT]
            .rearrange("p (b w) -> p b w", b=NCT), xh[:, :, CH:CH + 1]),
            inc="v")

    rec = dict(par=par, xhat_v=xhat_v, xh=xh, mix={})
    slot = {"A": "MXAp", "B": "MXBp", "C": "MXCp"}
    for (Aname, Bname, l, which) in mixes:
        dst = r3(cx[slot[which]][par][:], NCT)
        mv = emit_mix(S, cx, Aname, Bname, l, dst, f"MX{which}{par}", xhat_v,
                      xh)
        rec["mix"][which] = (dst, mv)
    return rec


def emit_mix(S, cx, Aname, Bname, l, dst, dst_name, xhat_v, xh):
    PAR = cx["PAR"]
    Ac = pcol(Aname, l) if l is not None else pcol(Aname)
    Bc = pcol(Bname, l) if l is not None else pcol(Bname)
    A_bc = r3(PAR[:, Ac:Ac + 8], NCT).broadcast_to([128, NCT, CH])
    S.op("vector", lambda e: e.tensor_tensor(dst, xh[:, :, 1:1 + CH], A_bc,
                                             ALU.mult),
         waits=[fget(cx, dst_name), ("v", xhat_v)], inc="v")

    def fb(e):
        inst = None
        for ct in range(NCT):
            inst = e.scalar_tensor_tensor(
                dst[:, ct], xh[:, ct, 0:CH], PAR[:, Bc + ct:Bc + ct + 1],
                dst[:, ct], ALU.mult, ALU.add)
        return inst
    S.op("vector", fb, inc="v")
    return S.m("v")


def emit_lnf_add(S, cx, p):
    XRES, B0, B3, PAR, PS = cx["XRES"], cx["B0"], cx["B3"], cx["PAR"], cx["PS"]
    lnfw = pcol("lnfw")
    for c in range(NCH):
        pbank = 6 + c % 2
        xr = r3(XRES[:], NCT)[:, :, c * CH:(c + 1) * CH]
        st = emit_stats(S, cx, xr, cx["xres_ready"], pbank)
        bc_t = emit_bcast(S, cx, st, fp32=True)
        fset(cx, "SQB", "t", bc_t)
        bank = PS[:, pbank * 512:(pbank + 1) * 512]
        mu_bc = bank[:, CH:2 * CH].unsqueeze(1).broadcast_to([128, NCT, CH])
        rstd_bc = bank[:, 0:CH].unsqueeze(1).broadcast_to([128, NCT, CH])
        src = cx["xt"].rearrange("(ct p) t -> p ct t", p=128)[
            :, :, p * HALF + c * CH: p * HALF + (c + 1) * CH]
        b3r = r3(B3[:, 0:NCT * CH], NCT)
        S.op("sync", lambda e, src=src, b3r=b3r: e.dma_start(b3r, src),
             waits=[fget(cx, "B3")], inc="sd", inc_n=16)
        xin_d = S.m("sd")
        b0r = r3(B0[:], NCT)
        S.op("vector", lambda e, b0r=b0r, xr=xr, mu_bc=mu_bc: e.tensor_tensor(
            b0r, xr, mu_bc, ALU.subtract),
            waits=[("t", bc_t), fget(cx, "B0")], inc="v")
        S.op("vector", lambda e, b0r=b0r, rstd_bc=rstd_bc: e.tensor_tensor(
            b0r, b0r, rstd_bc, ALU.mult), inc="v")
        bcd = S.m("v")
        cx["bankfree"][pbank] = ("v", bcd)

        def fn(e, b0r=b0r, xr=xr, b3r=b3r):
            inst = None
            for ct in range(NCT):
                inst = e.scalar_tensor_tensor(
                    xr[:, ct], b0r[:, ct], PAR[:, lnfw + ct:lnfw + ct + 1],
                    b3r[:, ct], ALU.mult, ALU.add)
            return inst
        S.op("vector", fn, waits=[("sd", xin_d)], inc="v")
        done_v = S.m("v")
        fset(cx, "B0", "v", done_v)
        fset(cx, "B3", "v", done_v)
        cx["xres_ready"] = [("v", done_v)]


PROJ_ROUNDS = [(0, 3, 0), (3, 6, 3), (6, 8, 0)]


def ps_banks(cx, blo, n):
    return r3(cx["PS"][:, blo * 512:(blo + n) * 512], n)[:, :, 0:CH]


def emit_proj(S, cx, warena_m, rhs3, rhs_wait, evac, wa3=None):
    ARENA, PS = cx["ARENA"], cx["PS"]
    if wa3 is None:
        wa3 = r3(ARENA[:], 4)[:, warena_m].rearrange("p (ct o) -> p ct o",
                                                     ct=NCT)
    for r, (glo, ghi, blo) in enumerate(PROJ_ROUNDS):
        def fn(t, glo=glo, ghi=ghi, blo=blo):
            inst = None
            for g in range(glo, ghi):
                bank = PS[:, (blo + g - glo) * 512:(blo + g - glo) * 512 + CH]
                for ct in range(NCT):
                    inst = t.matmul(bank, wa3[:, ct, g * 128:(g + 1) * 128],
                                    rhs3[:, ct], start=(ct == 0),
                                    stop=(ct == NCT - 1))
            return inst
        waits = [rhs_wait, cx["arena_ready"]] + [cx["bankfree"][blo + i]
                                                 for i in range(ghi - glo)]
        S.op("tensor", fn, waits=waits, inc="t")
        evac(r, glo, ghi, blo, S.m("t"))


def _dmp(S, cx, region, src, waits, cast=False, nct=NCT):
    dst = cx["out"].rearrange("(ct p) t -> p ct t", p=128)[
        :, 0:nct, region * CH:(region + 1) * CH]
    eng, inc = ("gpsimd", "gd") if cast else ("sync", "sd")
    S.op(eng, lambda e, dst=dst, src=src: e.dma_start(dst, src),
         waits=waits, inc=inc, inc_n=16)


def emit_rwkv_stack(S, cx, p, debug_stage):
    for l in range(NL):
        emit_rwkv_tm(S, cx, p, l)
        if debug_stage == f"L{l}tm":
            return True
        emit_rwkv_cm(S, cx, p, l)
        if debug_stage == f"L{l}":
            return True
    return False


def load_tm_arena(S, cx, l):
    ARENA = cx["ARENA"]
    waits = [("t", S.m("t"))]
    for m, w in enumerate([cx["tmkT"], cx["tmvT"], cx["tmrT"], cx["tmoT"]]):
        srcw = w[l].rearrange("(ct p) o -> p ct o", p=128)
        dst = r3(ARENA[:], 4)[:, m].rearrange("p (ct o) -> p ct o", ct=NCT)
        S.op("gpsimd", lambda e, dst=dst, srcw=srcw: e.dma_start(dst, srcw),
             waits=waits if m == 0 else [], inc="gd", inc_n=16)
    cx["arena_ready"] = ("gd", S.m("gd"))


def emit_rwkv_tm(S, cx, p, l):
    PS, PAR = cx["PS"], cx["PAR"]
    B0, B1, B2, B3 = cx["B0"], cx["B1"], cx["B2"], cx["B3"]
    XRES = cx["XRES"]
    load_tm_arena(S, cx, l)

    b0r = r3(B0[:], NCT)
    b1r = r3(B1[:], NCT)
    b2r = r3(B2[:, 0:NCT * (CH + 1)], NCT)
    b3r = r3(B3[:, 0:NCT * (CH + 1)], NCT)
    rw = r3(cx["RWB"][:], NCT)

    mixes = [("Ak", "Bk", l, "A"), ("Av", "Bv", l, "B"), ("Ar", "Br", l, "C")]
    preps = [emit_ln_prep(S, cx, 0, l, mixes)]
    for c in range(NCH):
        if c + 1 < NCH:
            preps.append(emit_ln_prep(S, cx, c + 1, l, mixes))
        P = preps[c]
        par = P["par"]
        mxa, mixk_v = P["mix"]["A"]
        mxb, mixv_v = P["mix"]["B"]
        mxc, mixr_v = P["mix"]["C"]

        def ev_ek(r, glo, ghi, blo, mm_t):
            S.op("scalar", lambda e, glo=glo, ghi=ghi, blo=blo: e.activation(
                b0r[:, glo:ghi], ps_banks(cx, blo, ghi - glo), AF.Exp),
                waits=[("t", mm_t), fget(cx, "B0")], inc="s")
            for i in range(ghi - glo):
                cx["bankfree"][blo + i] = ("s", S.m("s"))
        emit_proj(S, cx, 0, mxa, ("v", mixk_v), ev_ek)
        ek_s = S.m("s")
        fset(cx, "B0", "s", ek_s)
        fset(cx, f"MXA{par}", "t", S.m("t"))

        def ev_v(r, glo, ghi, blo, mm_t):
            S.op("scalar", lambda e, glo=glo, ghi=ghi, blo=blo: e.activation(
                b1r[:, glo:ghi], ps_banks(cx, blo, ghi - glo), AF.Copy),
                waits=[("t", mm_t), fget(cx, "B1")], inc="s")
            for i in range(ghi - glo):
                cx["bankfree"][blo + i] = ("s", S.m("s"))
        emit_proj(S, cx, 1, mxb, ("v", mixv_v), ev_v)
        v_s = S.m("s")
        fset(cx, "B1", "s", v_s)
        fset(cx, f"MXB{par}", "t", S.m("t"))

        def ev_r(r, glo, ghi, blo, mm_t):
            S.op("scalar", lambda e, glo=glo, ghi=ghi, blo=blo: e.activation(
                b2r[:, glo:ghi, 0:CH], ps_banks(cx, blo, ghi - glo), AF.Exp,
                scale=-1.0),
                waits=[("t", mm_t), fget(cx, "B2")], inc="s")
            for i in range(ghi - glo):
                cx["bankfree"][blo + i] = ("s", S.m("s"))

        S.op("vector", lambda e: e.tensor_tensor(b1r, b0r, b1r, ALU.mult),
             waits=[("s", ek_s), ("s", v_s)], inc="v")
        c1_v = S.m("v")
        gcol = PMAP["G"] + l * 8
        hcol = PMAP["H"] + l * 8
        S.op("vector", lambda e: e.tensor_copy(
            b2r[:, :, 0:1], r3(PAR[:, gcol:gcol + 8], NCT)),
            waits=[fget(cx, "B2")], inc="v")
        S.op("vector", lambda e: e.tensor_copy(
            b3r[:, :, 0:1], r3(PAR[:, hcol:hcol + 8], NCT)),
            waits=[fget(cx, "B3")], inc="v")

        def scans(e):
            inst = None
            for ct in range(NCT):
                a_ap = PAR[:, pcol("adec", l) + ct: pcol("adec", l) + ct + 1]
                inst = e.tensor_tensor_scan(
                    b2r[:, ct, 1:1 + CH], a_ap.broadcast_to([128, CH]),
                    b1r[:, ct], PAR[:, gcol + ct:gcol + ct + 1],
                    ALU.mult, ALU.add)
            return inst
        S.op("vector", scans, waits=[("v", c1_v)], inc="v")

        def scand(e):
            inst = None
            for ct in range(NCT):
                a_ap = PAR[:, pcol("adec", l) + ct: pcol("adec", l) + ct + 1]
                inst = e.tensor_tensor_scan(
                    b3r[:, ct, 1:1 + CH], a_ap.broadcast_to([128, CH]),
                    b0r[:, ct], PAR[:, hcol + ct:hcol + ct + 1],
                    ALU.mult, ALU.add)
            return inst
        S.op("vector", scand, inc="v")
        S.op("vector", lambda e: e.tensor_copy(
            r3(PAR[:, gcol:gcol + 8], NCT), b2r[:, :, CH:CH + 1]), inc="v")
        S.op("vector", lambda e: e.tensor_copy(
            r3(PAR[:, hcol:hcol + 8], NCT), b3r[:, :, CH:CH + 1]), inc="v")

        def num(e):
            inst = None
            for ct in range(NCT):
                eu = PAR[:, pcol("Eu", l) + ct: pcol("Eu", l) + ct + 1]
                inst = e.scalar_tensor_tensor(b1r[:, ct], b2r[:, ct, 0:CH], eu,
                                              b1r[:, ct], ALU.mult, ALU.add)
            return inst
        S.op("vector", num, inc="v")
        num_v = S.m("v")

        def den(e):
            inst = None
            for ct in range(NCT):
                eu = PAR[:, pcol("Eu", l) + ct: pcol("Eu", l) + ct + 1]
                inst = e.scalar_tensor_tensor(b0r[:, ct], b3r[:, ct, 0:CH], eu,
                                              b0r[:, ct], ALU.mult, ALU.add)
            return inst
        S.op("vector", den, inc="v")
        den_v = S.m("v")
        fset(cx, "B2", "v", num_v)
        fset(cx, "B3", "v", den_v)

        emit_proj(S, cx, 2, mxc, ("v", mixr_v), ev_r)
        esm_s = S.m("s")
        fset(cx, "B2", "s", esm_s)
        fset(cx, f"MXC{par}", "t", S.m("t"))

        S.op("vector", lambda e: e.tensor_scalar(
            b2r[:, :, 0:CH], b2r[:, :, 0:CH], 1.0, None, ALU.add),
            waits=[("s", esm_s)], inc="v")
        S.op("vector", lambda e: e.tensor_tensor(
            b0r, b0r, b2r[:, :, 0:CH], ALU.mult),
            waits=[("v", den_v)], inc="v")
        d2_v = S.m("v")
        fset(cx, "B2", "v", d2_v)
        S.op("scalar", lambda e: e.activation(b3r[:, :, 0:CH], b0r, AF.Ln),
             waits=[("v", d2_v)], inc="s")
        S.op("scalar", lambda e: e.activation(b0r, b3r[:, :, 0:CH], AF.Exp,
                                              scale=-1.0), inc="s")
        rec_s = S.m("s")
        fset(cx, "B3", "s", rec_s)
        S.op("vector", lambda e: e.tensor_tensor(rw, b1r, b0r, ALU.mult),
             waits=[("s", rec_s), ("v", num_v), fget(cx, "RWB")], inc="v")
        rw_v = S.m("v")
        fset(cx, "RWB", "v", rw_v)
        fset(cx, "B0", "v", rw_v)
        fset(cx, "B1", "v", rw_v)

        xr = r3(XRES[:], NCT)[:, :, c * CH:(c + 1) * CH]

        def ev_o(r, glo, ghi, blo, mm_t, xr=xr):
            S.op("vector", lambda e, glo=glo, ghi=ghi, blo=blo, xr=xr:
                 e.tensor_tensor(xr[:, glo:ghi], xr[:, glo:ghi],
                                 ps_banks(cx, blo, ghi - glo), ALU.add),
                 waits=[("t", mm_t)] + cx["xres_ready"], inc="v")
            for i in range(ghi - glo):
                cx["bankfree"][blo + i] = ("v", S.m("v"))
        emit_proj(S, cx, 3, rw, ("v", rw_v), ev_o)
        oadd_v = S.m("v")
        fset(cx, "RWB", "t", S.m("t"))
        cx["xres_ready"] = [("v", oadd_v)]


def emit_rwkv_cm(S, cx, p, l):
    PS, PAR = cx["PS"], cx["PAR"]
    B0, B1, B2, B3 = cx["B0"], cx["B1"], cx["B2"], cx["B3"]
    ARENA, XRES = cx["ARENA"], cx["XRES"]

    cmr_dst = ARENA[:, 24576:32768].rearrange("p (ct o) -> p ct o", ct=NCT)
    cmr3 = ARENA[:, 24576:32768].rearrange("p (ct o) -> p ct o", ct=NCT)
    tm_done_t = S.m("t")
    S.op("gpsimd", lambda e: e.dma_start(
        cmr_dst, cx["cmrT"][l].rearrange("(ct p) o -> p ct o", p=128)),
        waits=[("t", tm_done_t)], inc="gd", inc_n=16)
    cmr_gd = S.m("gd")
    for nm in ("CKSLOT0", "CKSLOT1", "CV0", "CV1"):
        fset(cx, nm, "t", tm_done_t)

    b0r = r3(B0[:], NCT)
    b1r = r3(B1[:], NCT)
    b2r = r3(B2[:, 0:NCT * CH], NCT)
    sq_a = cx["B2"][:, 0:NCT * CH].bitcast(BF16)
    sq_b = cx["B3"][:, 0:NCT * CH].bitcast(BF16)

    def sq_slice(s):
        buf = sq_a if s < 4 else sq_b
        return r3(buf[:, (s % 4) * 1024:(s % 4) * 1024 + 1024], 4)

    prep = cx["cmprep"][l]
    fset(cx, "SQ", "s", S.m("s"))
    mixes = [("Ack", "Bck", l, "A"), ("Acr", "Bcr", l, "C")]
    preps = [emit_ln_prep(S, cx, 0, 4 + l, mixes)]
    for c in range(NCH):
        if c + 1 < NCH:
            preps.append(emit_ln_prep(S, cx, c + 1, 4 + l, mixes))
        P = preps[c]
        par = P["par"]
        mxa, mixk_v = P["mix"]["A"]
        mxc, mixr_v = P["mix"]["C"]

        last_sq_v = 0
        for s in range(8):
            slot = s % 2
            slot3 = r3(ARENA[:, slot * 4096:(slot + 1) * 4096], NCT)
            srcw = cx["cmk_s"][l].rearrange("(ct p) f -> p ct f", p=128)[
                :, :, s * 512:(s + 1) * 512]
            S.op("sync", lambda e, slot3=slot3, srcw=srcw: e.dma_start(
                slot3, srcw),
                waits=[("gd", prep), fget(cx, f"CKSLOT{slot}")],
                inc="sd", inc_n=16)
            sl_sd = S.m("sd")
            blo = 3 * (s % 2)

            def fkk(t, slot3=slot3, blo=blo, mxa=mxa):
                inst = None
                for fi in range(4):
                    if fi < 3:
                        bank = PS[:, (blo + fi) * 512:(blo + fi) * 512 + CH]
                    else:
                        bank = PS[:, blo * 512 + CH:blo * 512 + 2 * CH]
                    for ct in range(NCT):
                        st = (ct == 0)
                        inst = t.matmul(bank,
                                        slot3[:, ct, fi * 128:(fi + 1) * 128],
                                        mxa[:, ct], start=st,
                                        stop=(ct == NCT - 1))
                return inst
            S.op("tensor", fkk, waits=[("sd", sl_sd), ("v", mixk_v)] +
                 [cx["bankfree"][blo + i] for i in range(3)], inc="t")
            kk_t = S.m("t")
            fset(cx, f"CKSLOT{slot}", "t", kk_t)
            if cx.get("dbgcm") == (l, c) and s == 0:
                stg = r3(cx["RWB"][:], NCT)
                S.op("scalar", lambda e, stg=stg: e.activation(
                    stg[:, 0:3], r3(PS[:, 0:3 * 512], 3)[:, :, 0:CH],
                    AF.Copy), waits=[("t", kk_t)], inc="s")
                S.op("scalar", lambda e, stg=stg: e.activation(
                    stg[:, 3], PS[:, CH:2 * CH], AF.Copy), inc="s")
                _dmp(S, cx, 14, stg[:, 0:4], [("s", S.m("s"))], cast=True,
                     nct=4)
                for i in range(3):
                    cx["bankfree"][i] = ("gd", S.m("gd"))
            sqv = sq_slice(s)

            def evk(e, sqv=sqv, blo=blo):
                src = r3(PS[:, blo * 512:(blo + 3) * 512], 3)[:, :, 0:CH]
                e.tensor_scalar(sqv[:, 0:3], src, 0.0, None, ALU.max)
                return e.tensor_scalar(
                    sqv[:, 3], PS[:, blo * 512 + CH:blo * 512 + 2 * CH], 0.0,
                    None, ALU.max)
            S.op("vector", evk, waits=[("t", kk_t), fget(cx, "SQ")], inc="v")
            S.op("vector", lambda e, sqv=sqv: e.tensor_tensor(
                sqv, sqv, sqv, ALU.mult), inc="v")
            last_sq_v = S.m("v")
            for i in range(3):
                cx["bankfree"][blo + i] = ("v", last_sq_v)

        def cv_slot(sl):
            return ARENA[:, 8192 + sl * 8192:8192 + (sl + 1) * 8192]\
                .rearrange("p (ft o) -> p ft o", ft=8)

        def cv_load(s):
            sl = s % 2
            srcw = cx["cmv_s"][l][s * 1024:(s + 1) * 1024].rearrange(
                "(ft p) o -> p ft o", p=128)
            S.op("sync", lambda e, sl=sl, srcw=srcw: e.dma_start(
                cv_slot(sl), srcw),
                waits=[("gd", prep), fget(cx, f"CV{sl}")], inc="sd", inc_n=16)
            return S.m("sd")
        cv_sd = {0: cv_load(0), 1: cv_load(1)}
        for s in range(4):
            slot3 = cv_slot(s % 2)
            for fi in range(8):
                ft = s * 8 + fi
                buf = sq_a if ft < 16 else sq_b
                ft_sq = buf[:, (ft % 16) * CH:(ft % 16) * CH + CH]

                def fvv(t, slot3=slot3, fi=fi, ft=ft, ft_sq=ft_sq):
                    inst = None
                    for g in range(8):
                        inst = t.matmul(
                            PS[:, g * 512:g * 512 + CH],
                            slot3[:, fi, g * 128:(g + 1) * 128],
                            ft_sq, start=(ft == 0), stop=(ft == NFT - 1))
                    return inst
                wv = [("sd", cv_sd[s]), ("v", last_sq_v)]
                if ft == 0:
                    wv += [cx["bankfree"][i] for i in range(8)]
                S.op("tensor", fvv, waits=wv, inc="t")
            fset(cx, f"CV{s % 2}", "t", S.m("t"))
            if s + 2 < 4:
                cv_sd[s + 2] = cv_load(s + 2)
        vv_t = S.m("t")
        S.op("scalar", lambda e: e.activation(
            b1r, r3(PS[:], 8)[:, :, 0:CH], AF.Copy),
            waits=[("t", vv_t), fget(cx, "B1")], inc="s")
        vv_s = S.m("s")
        for i in range(8):
            cx["bankfree"][i] = ("s", vv_s)
        fset(cx, "B1", "s", vv_s)
        fset(cx, "B2", "t", vv_t)
        fset(cx, "B3", "t", vv_t)

        dwait1, dwait2 = [], []
        if cx.get("dbgcm") == (l, c):
            _dmp(S, cx, 8, b1r, [("s", vv_s)])
            dwait1 = [("sd", S.m("sd"))]
            _dmp(S, cx, 12, r3(sq_a[:, 0:2048], NCT), [("v", last_sq_v)],
                 cast=True)
            _dmp(S, cx, 10, mxa, [("v", mixk_v)], cast=True)
            _dmp(S, cx, 11, mxc, [("v", mixr_v)], cast=True)
            _dmp(S, cx, 13, P["xh"][:, :, 1:1 + CH], [("v", P["xhat_v"])],
                 cast=True)

        def ev_rr(r, glo, ghi, blo, mm_t):
            S.op("scalar", lambda e, glo=glo, ghi=ghi, blo=blo: e.activation(
                b0r[:, glo:ghi], ps_banks(cx, blo, ghi - glo), AF.Exp,
                scale=-1.0),
                waits=[("t", mm_t), fget(cx, "B0")], inc="s")
            for i in range(ghi - glo):
                cx["bankfree"][blo + i] = ("s", S.m("s"))
        cx["arena_ready"] = ("gd", cmr_gd)
        emit_proj(S, cx, None, mxc, ("v", mixr_v), ev_rr, wa3=cmr3)
        e2m_s = S.m("s")
        fset(cx, "B0", "s", e2m_s)
        fset(cx, f"MXC{par}", "t", S.m("t"))
        if cx.get("dbgcm") == (l, c):
            _dmp(S, cx, 9, b0r, [("s", e2m_s)])
            dwait2 = [("sd", S.m("sd"))]

        S.op("vector", lambda e: e.tensor_scalar(b0r, b0r, 1.0, None, ALU.add),
             waits=[("s", e2m_s)] + dwait2, inc="v")
        d_v = S.m("v")
        S.op("scalar", lambda e: e.activation(b2r, b0r, AF.Ln),
             waits=[("v", d_v), fget(cx, "B2")], inc="s")
        S.op("scalar", lambda e: e.activation(b0r, b2r, AF.Exp, scale=-1.0),
             inc="s")
        rec_s = S.m("s")
        S.op("vector", lambda e: e.tensor_tensor(b1r, b1r, b0r, ALU.mult),
             waits=[("s", rec_s), ("s", vv_s)] + dwait1, inc="v")
        xr = r3(XRES[:], NCT)[:, :, c * CH:(c + 1) * CH]
        S.op("vector", lambda e, xr=xr: e.tensor_tensor(xr, xr, b1r, ALU.add),
             waits=list(cx["xres_ready"]), inc="v")
        done_v = S.m("v")
        fset(cx, "B0", "v", done_v)
        fset(cx, "B1", "v", done_v)
        fset(cx, "B2", "s", rec_s)
        fset(cx, "B3", "s", rec_s)
        fset(cx, "SQ", "s", rec_s)
        fset(cx, f"MXA{par}", "t", S.m("t"))
        cx["xres_ready"] = [("v", done_v)]


def emit_ffn(S, cx, p, site):
    PS = cx["PS"]
    B0, B1, B2 = cx["B0"], cx["B1"], cx["B2"]
    ARENA, XRES = cx["ARENA"], cx["XRES"]
    waits = [("t", S.m("t"))]
    for m, w in enumerate([cx["frT"], cx["fkT"], cx["fvT"]]):
        srcw = w.rearrange("(ct p) o -> p ct o", p=128)
        dst = r3(ARENA[:], 4)[:, m].rearrange("p (ct o) -> p ct o", ct=NCT)
        S.op("gpsimd", lambda e, dst=dst, srcw=srcw: e.dma_start(dst, srcw),
             waits=waits if m == 0 else [], inc="gd", inc_n=16)
    cx["arena_ready"] = ("gd", S.m("gd"))

    b0r = r3(B0[:], NCT)
    b1r = r3(B1[:], NCT)
    b2f = r3(B2[:, 0:NCT * CH], NCT)
    sq3 = r3(B2[:, 0:NCT * CH].bitcast(BF16)[:, 0:NCT * CH], NCT)

    mixes = [("fAr", "fBr", None, "A"), ("fAk", "fBk", None, "B")]
    preps = [emit_ln_prep(S, cx, 0, site, mixes)]
    for c in range(NCH):
        if c + 1 < NCH:
            preps.append(emit_ln_prep(S, cx, c + 1, site, mixes))
        P = preps[c]
        par = P["par"]
        mxa, mixr_v = P["mix"]["A"]
        mxb, mixk_v = P["mix"]["B"]

        def ev_r(r, glo, ghi, blo, mm_t):
            S.op("scalar", lambda e, glo=glo, ghi=ghi, blo=blo: e.activation(
                b0r[:, glo:ghi], ps_banks(cx, blo, ghi - glo), AF.Exp,
                scale=-1.0),
                waits=[("t", mm_t), fget(cx, "B0")], inc="s")
            for i in range(ghi - glo):
                cx["bankfree"][blo + i] = ("s", S.m("s"))
        emit_proj(S, cx, 0, mxa, ("v", mixr_v), ev_r)
        erm_s = S.m("s")
        fset(cx, "B0", "s", erm_s)
        fset(cx, f"MXA{par}", "t", S.m("t"))

        def ev_k(r, glo, ghi, blo, mm_t):
            S.op("vector", lambda e, glo=glo, ghi=ghi, blo=blo: e.tensor_scalar(
                sq3[:, glo:ghi], ps_banks(cx, blo, ghi - glo), 0.0, None,
                ALU.max),
                waits=[("t", mm_t), fget(cx, "B2")], inc="v")
            S.op("vector", lambda e, glo=glo, ghi=ghi: e.tensor_tensor(
                sq3[:, glo:ghi], sq3[:, glo:ghi], sq3[:, glo:ghi], ALU.mult),
                inc="v")
            for i in range(ghi - glo):
                cx["bankfree"][blo + i] = ("v", S.m("v"))
        emit_proj(S, cx, 1, mxb, ("v", mixk_v), ev_k)
        sq_v = S.m("v")
        fset(cx, "B2", "v", sq_v)
        fset(cx, f"MXB{par}", "t", S.m("t"))

        def ev_v(r, glo, ghi, blo, mm_t):
            S.op("scalar", lambda e, glo=glo, ghi=ghi, blo=blo: e.activation(
                b1r[:, glo:ghi], ps_banks(cx, blo, ghi - glo), AF.Copy),
                waits=[("t", mm_t), fget(cx, "B1")], inc="s")
            for i in range(ghi - glo):
                cx["bankfree"][blo + i] = ("s", S.m("s"))
        emit_proj(S, cx, 2, sq3, ("v", sq_v), ev_v)
        vv_s = S.m("s")
        fset(cx, "B1", "s", vv_s)
        fset(cx, "B2", "t", S.m("t"))

        S.op("vector", lambda e: e.tensor_scalar(b0r, b0r, 1.0, None, ALU.add),
             waits=[("s", erm_s)], inc="v")
        d_v = S.m("v")
        S.op("scalar", lambda e: e.activation(b2f, b0r, AF.Ln),
             waits=[("v", d_v), fget(cx, "B2")], inc="s")
        S.op("scalar", lambda e: e.activation(b0r, b2f, AF.Exp, scale=-1.0),
             inc="s")
        rec_s = S.m("s")
        S.op("vector", lambda e: e.tensor_tensor(b1r, b1r, b0r, ALU.mult),
             waits=[("s", rec_s), ("s", vv_s)], inc="v")
        xr = r3(XRES[:], NCT)[:, :, c * CH:(c + 1) * CH]
        S.op("vector", lambda e, xr=xr: e.tensor_tensor(xr, xr, b1r, ALU.add),
             waits=list(cx["xres_ready"]), inc="v")
        done_v = S.m("v")
        fset(cx, "B0", "v", done_v)
        fset(cx, "B1", "v", done_v)
        fset(cx, "B2", "s", rec_s)
        cx["xres_ready"] = [("v", done_v)]


def emit_attn(S, cx, p):
    PS, RWB, PAR, PARB = cx["PS"], cx["RWB"], cx["PAR"], cx["PARB"]
    B0, B1, B2, B3 = cx["B0"], cx["B1"], cx["B2"], cx["B3"]
    ARENA, XRES = cx["ARENA"], cx["XRES"]
    ksp, vtsp, qsp = cx["ksp"], cx["vtsp"], cx["qsp"]
    poff = p * NCH

    waits = [("t", S.m("t"))]
    for m, w in enumerate([cx["wqT"], cx["wkT"], cx["wvT"], cx["woT"]]):
        srcw = w.rearrange("(ct p) o -> p ct o", p=128)
        dst = r3(ARENA[:], 4)[:, m].rearrange("p (ct o) -> p ct o", ct=NCT)
        S.op("gpsimd", lambda e, dst=dst, srcw=srcw: e.dma_start(dst, srcw),
             waits=waits if m == 0 else [], inc="gd", inc_n=16)
    cx["arena_ready"] = ("gd", S.m("gd"))

    rwb3 = r3(RWB[:], NCT)
    stg = B2[:].bitcast(BF16)[:, 0:8 * 2 * 130]
    stg4 = stg.rearrange("p (pr tt w) -> p pr tt w", pr=8, tt=2)

    S.op("vector", lambda e: e.memset(
        stg.rearrange("p (b w) -> p b w", b=16, w=130)[:, :, 64:130:65], 1.0),
        waits=[fget(cx, "B2")], inc="v")
    ones_v = S.m("v")
    preps = [emit_ln_prep(S, cx, 0, None, [])]
    for c in range(NCH):
        if c + 1 < NCH:
            preps.append(emit_ln_prep(S, cx, c + 1, None, []))
        P = preps[c]
        par = P["par"]
        xh3 = P["xh"][:, :, 1:1 + CH]
        xhat_v = P["xhat_v"]
        gcol = p * HALF + c * CH

        def ev_q(r, glo, ghi, blo, mm_t):
            for g in range(glo, ghi):
                bq = PAR[:, pcol("bq") + g:pcol("bq") + g + 1]
                bank = PS[:, (blo + g - glo) * 512:(blo + g - glo) * 512 + CH]
                S.op("scalar", lambda e, g=g, bank=bank, bq=bq: e.activation(
                    rwb3[:, g], bank, AF.Identity, bias=bq),
                    waits=[("t", mm_t), fget(cx, "RWB")], inc="s")
                cx["bankfree"][blo + g - glo] = ("s", S.m("s"))
        emit_proj(S, cx, 0, xh3, ("v", xhat_v), ev_q)
        q_s = S.m("s")
        S.op("sync", lambda e, gcol=gcol: e.dma_start(
            qsp.rearrange("(ct p) t -> p ct t", p=128)[:, :, gcol:gcol + CH],
            rwb3), waits=[("s", q_s)], inc="sd", inc_n=16)
        fset(cx, "RWB", "sd", S.m("sd"))

        def ev_k(r, glo, ghi, blo, mm_t):
            for g in range(glo, ghi):
                bk = PAR[:, pcol("bk") + g:pcol("bk") + g + 1]
                bank = PS[:, (blo + g - glo) * 512:(blo + g - glo) * 512 + CH]
                S.op("scalar", lambda e, g=g, bank=bank, bk=bk: e.activation(
                    rwb3[:, g], bank, AF.Identity, bias=bk),
                    waits=[("t", mm_t), fget(cx, "RWB")], inc="s")
                cx["bankfree"][blo + g - glo] = ("s", S.m("s"))
        emit_proj(S, cx, 1, xh3, ("v", xhat_v), ev_k)
        k_s = S.m("s")
        S.op("sync", lambda e, gcol=gcol: e.dma_start(
            ksp.rearrange("(ct p) t -> p ct t", p=128)[:, :, gcol:gcol + CH],
            rwb3), waits=[("s", k_s)], inc="sd", inc_n=16)
        fset(cx, "RWB", "sd", S.m("sd"))

        wv3 = r3(ARENA[:], 4)[:, 2].rearrange("p (ct o) -> p ct o", ct=NCT)
        vmm_t = 0
        for tt in range(2):
            for nh in range(2):
                bank = PS[:, nh * 512:nh * 512 + 512]

                def fmm(t, tt=tt, nh=nh, bank=bank, P=P):
                    inst = None
                    for ct in range(NCT):
                        inst = t.matmul(
                            bank,
                            P["xh"][:, ct, 1 + tt * 128:1 + tt * 128 + 128],
                            wv3[:, ct, nh * 512:(nh + 1) * 512],
                            start=(ct == 0), stop=(ct == NCT - 1))
                    return inst
                S.op("tensor", fmm, waits=[("v", xhat_v),
                                           cx["bankfree"][nh],
                                           cx["arena_ready"]], inc="t")
            vmm_t = S.m("t")

            def ev(e, tt=tt):
                src_ap = PS[:, 0:1024].rearrange(
                    "p (pr h w) -> p pr h w", pr=8, h=2, w=64)
                dst_ap = stg4[:, :, tt, :].rearrange(
                    "p pr (h z) -> p pr h z", h=2, z=65)[:, :, :, 0:64]
                return e.activation(dst_ap, src_ap, AF.Copy)
            S.op("scalar", ev, waits=[("t", vmm_t), ("v", ones_v),
                                      fget(cx, "B2")], inc="s")
            cx["bankfree"][0] = ("s", S.m("s"))
            cx["bankfree"][1] = ("s", S.m("s"))
        fset(cx, f"XHAT{par}", "t", vmm_t)
        vstg_s = S.m("s")
        dst = vtsp.rearrange("p (pr tt w) -> p pr tt w", pr=8, tt=32,
                             w=130)[:, :, 2 * (poff + c):2 * (poff + c) + 2, :]
        S.op("sync", lambda e, dst=dst: e.dma_start(dst, stg4),
             waits=[("s", vstg_s)], inc="sd", inc_n=16)
        fset(cx, "B2", "sd", S.m("sd"))
    spills_sd = S.m("sd")

    kslot = [B0[:].bitcast(BF16), B1[:].bitcast(BF16)]
    vslot = [B2[:].bitcast(BF16), B3[:].bitcast(BF16)]
    ptbuf = cx["XHATp"][0][:]
    slot_free = {("k", 0): [fget(cx, "B0")], ("k", 1): [fget(cx, "B1")],
                 ("v", 0): [fget(cx, "B2")], ("v", 1): [fget(cx, "B3")]}
    cx["free"]["PT0"] = fget(cx, "XHAT0")
    cx["free"]["PT1"] = fget(cx, "XHAT0")
    cx["free"]["OB0"] = fget(cx, "XHAT0")
    cx["free"]["OB1"] = fget(cx, "XHAT0")
    SQF = cx["SQB"][:].bitcast(F32)
    QB = cx["MXAp"][0]

    for c in range(NCH):
        q0 = p * HALF + c * CH
        nkt = (q0 + CH) // 128
        S.op("sync", lambda e, q0=q0: e.dma_start(
            r3(QB[:], NCT),
            qsp.rearrange("(ct p) t -> p ct t", p=128)[:, :, q0:q0 + CH]),
            waits=[("sd", spills_sd), fget(cx, "MXA0")], inc="sd", inc_n=16)
        qd = S.m("sd")

        for pr in range(8):
            ks = kslot[pr % 2]
            vs = vslot[pr % 2]
            S.op("sync", lambda e, nkt=nkt, ks=ks, pr=pr: e.dma_start(
                ks[:, 0:nkt * 128], ksp[pr * 128:(pr + 1) * 128, 0:nkt * 128]),
                waits=[("sd", spills_sd)] + slot_free[("k", pr % 2)],
                inc="sd", inc_n=16)
            kd = S.m("sd")
            S.op("sync", lambda e, nkt=nkt, vs=vs, pr=pr: e.dma_start(
                vs[:, 0:nkt * 130],
                vtsp.rearrange("p (pr w) -> p pr w", pr=8, w=32 * 130)[
                    :, pr, 0:nkt * 130]),
                waits=[("sd", spills_sd)] + slot_free[("v", pr % 2)],
                inc="sd", inc_n=16)
            vd = S.m("sd")

            for kt in range(nkt):
                sb = [0, 2][kt % 2]
                pt = kt % 2
                ptap = ptbuf[:, pt * 512:pt * 512 + 512]

                def fsc(t, kt=kt, sb=sb, ks=ks, pr=pr):
                    t.matmul(PS[:, sb * 512:sb * 512 + CH],
                             ks[0:64, kt * 128:(kt + 1) * 128],
                             r3(QB[:], NCT)[0:64, pr, :],
                             start=True, stop=True, tile_position=(0, 0))
                    return t.matmul(PS[:, (sb + 1) * 512:(sb + 1) * 512 + CH],
                                    ks[64:128, kt * 128:(kt + 1) * 128],
                                    r3(QB[:], NCT)[64:128, pr, :],
                                    start=True, stop=True,
                                    tile_position=(64, 0))
                S.op("tensor", fsc, waits=[("sd", kd), ("sd", qd),
                                           cx["bankfree"][sb],
                                           cx["bankfree"][sb + 1]], inc="t")
                sc_t = S.m("t")

                def fex(e, sb=sb, ptap=ptap):
                    src = PS[:, sb * 512:sb * 512 + 1024].rearrange(
                        "p (h w) -> p h w", h=2, w=512)[:, :, 0:CH]
                    return e.activation(
                        ptap.rearrange("p (h w) -> p h w", h=2, w=CH), src,
                        AF.Exp, scale=0.125)
                S.op("scalar", fex, waits=[("t", sc_t), fget(cx, f"PT{pt}")],
                     inc="s")
                ex_s = S.m("s")
                cx["bankfree"][sb] = ("s", ex_s)
                cx["bankfree"][sb + 1] = ("s", ex_s)
                avw = [("s", ex_s), ("sd", vd)]
                if kt * 128 > q0 - 128:
                    def fmask(e, kt=kt, ptap=ptap, q0=q0):
                        i = None
                        for h in range(2):
                            i = e.affine_select(
                                ptap[:, h * CH:(h + 1) * CH],
                                ptap[:, h * CH:(h + 1) * CH],
                                [[1, CH]], ALU.is_ge, 0.0,
                                base=q0 - kt * 128, channel_multiplier=-1)
                        return i
                    S.op("gpsimd", fmask, waits=[("s", ex_s)], inc="g")
                    avw = [("g", S.m("g")), ("sd", vd)]

                def fav(t, kt=kt, vs=vs, ptap=ptap, first=(kt == 0),
                        last=(kt == nkt - 1)):
                    t.matmul(PS[0:65, 4 * 512:4 * 512 + CH],
                             vs[:, kt * 130:kt * 130 + 65],
                             ptap[:, 0:CH], start=first, stop=last)
                    t.matmul(PS[64:128, 5 * 512:5 * 512 + CH],
                             vs[:, kt * 130 + 65:kt * 130 + 129],
                             ptap[:, CH:2 * CH], start=first, stop=last,
                             tile_position=(0, 64))
                    return t.matmul(PS[0:1, 5 * 512 + CH:5 * 512 + 2 * CH],
                                    vs[:, kt * 130 + 129:kt * 130 + 130],
                                    ptap[:, CH:2 * CH], start=first, stop=last)
                avw2 = avw + ([cx["bankfree"][4], cx["bankfree"][5]]
                              if kt == 0 else [])
                S.op("tensor", fav, waits=avw2, inc="t")
                fset(cx, f"PT{pt}", "t", S.m("t"))
            av_t = S.m("t")
            slot_free[("k", pr % 2)] = [("t", av_t)]
            slot_free[("v", pr % 2)] = [("t", av_t)]

            for h in range(2):
                pb = 64 if h == 0 else 0
                avbank = PS[0 + 64 * h:64 + 64 * h,
                            (4 + h) * 512:(4 + h) * 512 + CH]
                den_ap = (PS[64:65, 4 * 512:4 * 512 + CH] if h == 0 else
                          PS[0:1, 5 * 512 + CH:5 * 512 + 2 * CH])
                rrow = SQF[pb:pb + 1, 0:CH]
                rrow2 = SQF[pb:pb + 1, CH:2 * CH]
                brow = PARB[pb:pb + 1, PB_MU:PB_MU + CH]
                S.op("scalar", lambda e, den_ap=den_ap, rrow=rrow: e.activation(
                    rrow, den_ap, AF.Ln),
                    waits=[("t", av_t), fget(cx, f"DROW{h}")], inc="s")
                S.op("scalar", lambda e, rrow=rrow, rrow2=rrow2: e.activation(
                    rrow2, rrow, AF.Exp, scale=-1.0), inc="s")
                S.op("scalar", lambda e, rrow2=rrow2, brow=brow: e.activation(
                    brow, rrow2, AF.Copy), inc="s")
                row_s = S.m("s")
                bb = 6 + h

                def fbc(e, brow=brow, pb=pb, bb=bb, h=h):
                    ones = PARB[pb:pb + 1, PB_ONESROW:PB_ONESROW + 64]
                    if h == 0:
                        return e.matmul(PS[0:64, bb * 512:bb * 512 + CH],
                                        ones, brow, start=True, stop=True)
                    return e.matmul(PS[64:128, bb * 512:bb * 512 + CH],
                                    ones, brow, start=True, stop=True,
                                    tile_position=(0, 64))
                S.op("tensor", fbc, waits=[("s", row_s), cx["bankfree"][bb]],
                     inc="t")
                bc_t = S.m("t")
                fset(cx, f"DROW{h}", "t", bc_t)
                obuf = ptbuf[0 + 64 * h:64 + 64 * h, 1024:1024 + CH]
                S.op("scalar", lambda e, avbank=avbank, obuf=obuf: e.activation(
                    obuf, avbank, AF.Copy),
                    waits=[("t", av_t), fget(cx, f"OB{h}")], inc="s")
                ob_s = S.m("s")
                dst = rwb3[64 * h:64 * h + 64, pr, :]
                bvap = PAR[64 * h:64 * h + 64,
                           pcol("bv") + pr:pcol("bv") + pr + 1]
                bcread = (PS[0:64, bb * 512:bb * 512 + CH] if h == 0 else
                          PS[64:128, bb * 512:bb * 512 + CH])
                S.op("vector", lambda e, obuf=obuf, dst=dst, bcread=bcread:
                     e.tensor_tensor(dst, obuf, bcread, ALU.mult),
                     waits=[("s", ob_s), ("t", bc_t), fget(cx, "RWB")],
                     inc="v")
                S.op("vector", lambda e, dst=dst, bvap=bvap: e.tensor_scalar(
                    dst, dst, bvap, None, ALU.add), inc="v")
                fin_v = S.m("v")
                fset(cx, f"OB{h}", "v", fin_v)
                cx["bankfree"][bb] = ("v", fin_v)
                cx["bankfree"][4 + h] = ("v", fin_v)
        ofin_v = S.m("v")
        fset(cx, "RWB", "v", ofin_v)
        fset(cx, "MXA0", "t", S.m("t"))

        xr = r3(XRES[:], NCT)[:, :, c * CH:(c + 1) * CH]

        def ev_o(r, glo, ghi, blo, mm_t, xr=xr):
            def fn(e, glo=glo, ghi=ghi, blo=blo, xr=xr):
                inst = None
                for g in range(glo, ghi):
                    bank = PS[:, (blo + g - glo) * 512:
                              (blo + g - glo) * 512 + CH]
                    inst = e.scalar_tensor_tensor(
                        xr[:, g], bank, PAR[:, pcol("bo") + g:pcol("bo") + g + 1],
                        xr[:, g], ALU.add, ALU.add)
                return inst
            S.op("vector", fn, waits=[("t", mm_t)] + cx["xres_ready"], inc="v")
            for i in range(ghi - glo):
                cx["bankfree"][blo + i] = ("v", S.m("v"))
        emit_proj(S, cx, 3, rwb3, ("v", ofin_v), ev_o)
        oadd_v = S.m("v")
        fset(cx, "RWB", "t", S.m("t"))
        cx["xres_ready"] = [("v", oadd_v)]

    end_t = S.m("t")
    for nm in ("B0", "B1", "B2", "B3", "XHAT0"):
        fset(cx, nm, "t", end_t)
    fset(cx, "SQB", "s", S.m("s"))


def _prep_inputs(inputs):
    def tr(a):
        return np.ascontiguousarray(np.swapaxes(np.asarray(a), -1, -2))

    g = {k: np.asarray(v) for k, v in inputs.items()}
    com = {}
    com["tmkT"] = tr(g["L0_attn_a_tm_k_w"])
    com["tmvT"] = tr(g["L0_attn_a_tm_v_w"])
    com["tmrT"] = tr(g["L0_attn_a_tm_r_w"])
    com["tmoT"] = tr(g["L0_attn_a_tm_o_w"])
    com["cmkT"] = tr(g["L0_attn_a_cm_k_w"])
    com["cmvT"] = tr(g["L0_attn_a_cm_v_w"])
    com["cmrT"] = tr(g["L0_attn_a_cm_r_w"])
    wqkv = g["L0_attn_b_W_qkv"]
    com["wqT"] = tr(wqkv[0:1024])
    com["wkT"] = tr(wqkv[1024:2048])
    com["wvT"] = tr(wqkv[2048:3072])
    com["woT"] = tr(g["L0_attn_b_W_o"])
    com["frT"] = tr(g["L0_ffn_W_r"])
    com["fkT"] = tr(g["L0_ffn_W_k"])
    com["fvT"] = tr(g["L0_ffn_W_v"])

    def col8(v):
        return np.ascontiguousarray(np.asarray(v).reshape(8, 128).T)

    cols = {}
    for l in range(NL):
        cols[f"raw_ln1w{l}"] = col8(g["L0_attn_a_ln1_w"][l])
        cols[f"raw_ln2w{l}"] = col8(g["L0_attn_a_ln2_w"][l])
        cols[f"raw_mk{l}"] = col8(g["L0_attn_a_tm_mix_k"][l, 0, 0])
        cols[f"raw_mv{l}"] = col8(g["L0_attn_a_tm_mix_v"][l, 0, 0])
        cols[f"raw_mr{l}"] = col8(g["L0_attn_a_tm_mix_r"][l, 0, 0])
        cols[f"raw_cmk{l}"] = col8(g["L0_attn_a_cm_mix_k"][l, 0, 0])
        cols[f"raw_cmr{l}"] = col8(g["L0_attn_a_cm_mix_r"][l, 0, 0])
        cols[f"raw_td{l}"] = col8(g["L0_attn_a_time_decay"][l])
        cols[f"raw_tf{l}"] = col8(g["L0_attn_a_time_first"][l])
    cols["raw_mur"] = col8(g["L0_ffn_mu_r"])
    cols["raw_muk"] = col8(g["L0_ffn_mu_k"])
    cols["lnfw"] = col8(g["L0_attn_a_lnf_w"])
    bqkv = g["L0_attn_b_b_qkv"]
    cols["bq"] = col8(bqkv[0:1024])
    cols["bk"] = col8(bqkv[1024:2048])
    cols["bv"] = col8(bqkv[2048:3072])
    cols["bo"] = col8(g["L0_attn_b_b_o"])
    vec = np.concatenate([cols[n] for n in VEC_ORDER], axis=1).astype(np.float32)
    com["vecs"] = np.ascontiguousarray(vec)

    x = g["L0_attn_a_x"]
    per_core = []
    for b in range(8):
        m = dict(com)
        m["xt"] = np.ascontiguousarray(x[b].T.astype(np.float32))
        per_core.append(m)
    return per_core


def kernel(**inputs):
    in_maps = _prep_inputs(inputs)
    nc = build_program()
    res = run_bass_kernel_spmd(nc, in_maps, core_ids=list(range(8)))
    outs = [np.ascontiguousarray(np.asarray(r["out"]).T) for r in res.results]
    return np.stack(outs, axis=0).astype(np.float32)


# version 06
# speedup vs baseline: 1.1819x; 1.1819x over previous
import numpy as np
import concourse.bass as bass
import concourse.mybir as mybir
from concourse.bass_utils import run_bass_kernel_spmd

F32 = mybir.dt.float32
BF16 = mybir.dt.bfloat16
AF = mybir.ActivationFunctionType
ALU = mybir.AluOpType

C = 1024
NCT = 8
T = 4096
HALF = 2048
CH = 256
NCH = 8
FFND = 4096
NFT = 32
NL = 4
HEADS = 16
DH = 64
EPS = 1e-5
INV_C = 1.0 / C


SERIALIZE = False
SER_SAME_ENGINES = ("vector", "scalar", "gpsimd")
DETECT_RACES = False
FFN_TRUNC = None

ENG_SEM = {"tensor": "t", "vector": "v", "scalar": "s", "gpsimd": "g",
           "sync": "sd"}


class Sched:
    ENGINES = ("tensor", "vector", "scalar", "gpsimd", "sync")

    def __init__(self, nc):
        self.nc = nc
        self.ops = {e: [] for e in self.ENGINES}
        self.sem = {}
        self.count = {}
        self.seen = {e: {} for e in self.ENGINES}

    def add_sem(self, name, handle):
        self.sem[name] = handle
        self.count[name] = 0
        for e in self.ENGINES:
            self.seen[e][name] = 0

    def op(self, eng, fn, waits=(), inc=None, inc_n=1):
        for s, v in waits:
            assert v <= self.count[s], f"wait {s}>={v} but only {self.count[s]} emitted"
        if SERIALIZE:
            waits = list(waits) + [(s, self.count[s]) for s in self.count
                                   if s != ENG_SEM[eng]]
            if inc is None:
                inc = ENG_SEM[eng]
                inc_n = 16 if inc in ("sd", "gd") else 1
        elif eng in SER_SAME_ENGINES:
            own = ENG_SEM[eng]
            waits = list(waits) + [(own, self.count[own])]
            if inc is None:
                inc = own
                inc_n = 16 if inc in ("sd", "gd") else 1
        if inc is not None:
            self.count[inc] += inc_n
        sems = self.sem
        seen = self.seen[eng]
        eff = {}
        for s, v in waits:
            if v > seen[s] and v > eff.get(s, 0):
                eff[s] = v
        for s, v in eff.items():
            seen[s] = v
        waits = tuple(eff.items())

        def closure(e, fn=fn, waits=waits, inc=inc, inc_n=inc_n):
            for s, v in waits:
                e.wait_ge(sems[s], v)
            inst = fn(e)
            if inc is not None and inst is not None:
                inst.then_inc(sems[inc], inc_n)
        self.ops[eng].append(closure)
        return self.count[inc] if inc is not None else None

    def m(self, sem):
        return self.count[sem]


def r3(ap, nb):
    return ap.rearrange("p (b w) -> p b w", b=nb)


def _par_map():
    m = {}
    col = 0

    def add(name, n=8):
        nonlocal col
        m[name] = col
        col += n
    for l in range(NL):
        for nm in ("Ak", "Bk", "Av", "Bv", "Ar", "Br",
                   "Ack", "Bck", "Acr", "Bcr", "adec", "Eu"):
            add(f"{nm}{l}")
    for nm in ("fAr", "fBr", "fAk", "fBk", "lnfw", "bq", "bk", "bo", "bv"):
        add(nm)
    add("G", NL * 8)
    add("H", NL * 8)
    add("halo", 10 * 8)
    add("epsc", 8)
    add("onesf", 128)
    assert col <= 768, col
    return m, col


PMAP, PAR_USED = _par_map()

VEC_ORDER = ([f"raw_{nm}{l}" for l in range(NL)
              for nm in ("ln1w", "ln2w", "mk", "mv", "mr", "cmk", "cmr", "td", "tf")]
             + ["raw_mur", "raw_muk", "lnfw", "bq", "bk", "bo", "bv"])
VEC_COLS = len(VEC_ORDER) * 8

PB_RSTD = 0
PB_MU = 2 * CH
PB_ONESROW = 4 * CH
PB_ONESCOL = 4 * CH + 128
PARB_COLS = 4 * CH + 132


def pcol(name, l=None):
    return PMAP[name if l is None else f"{name}{l}"]


def build_program(debug_stage=None):
    nc = bass.Bass("TRN2", target_bir_lowering=False,
                   detect_race_conditions=DETECT_RACES)

    xt = nc.dram_tensor("xt", [C, T], F32, kind="ExternalInput").ap()
    out = nc.dram_tensor("out", [C, T], F32, kind="ExternalOutput").ap()
    tmkT = nc.dram_tensor("tmkT", [NL, C, C], F32, kind="ExternalInput").ap()
    tmvT = nc.dram_tensor("tmvT", [NL, C, C], F32, kind="ExternalInput").ap()
    tmrT = nc.dram_tensor("tmrT", [NL, C, C], F32, kind="ExternalInput").ap()
    tmoT = nc.dram_tensor("tmoT", [NL, C, C], F32, kind="ExternalInput").ap()
    cmkT = nc.dram_tensor("cmkT", [NL, C, FFND], F32, kind="ExternalInput").ap()
    cmvT = nc.dram_tensor("cmvT", [NL, FFND, C], F32, kind="ExternalInput").ap()
    cmrT = nc.dram_tensor("cmrT", [NL, C, C], F32, kind="ExternalInput").ap()
    wqT = nc.dram_tensor("wqT", [C, C], F32, kind="ExternalInput").ap()
    wkT = nc.dram_tensor("wkT", [C, C], F32, kind="ExternalInput").ap()
    wvT = nc.dram_tensor("wvT", [C, C], F32, kind="ExternalInput").ap()
    woT = nc.dram_tensor("woT", [C, C], F32, kind="ExternalInput").ap()
    frT = nc.dram_tensor("frT", [C, C], F32, kind="ExternalInput").ap()
    fkT = nc.dram_tensor("fkT", [C, C], F32, kind="ExternalInput").ap()
    fvT = nc.dram_tensor("fvT", [C, C], F32, kind="ExternalInput").ap()
    vecs = nc.dram_tensor("vecs", [128, VEC_COLS], F32, kind="ExternalInput").ap()
    cmk_s = nc.dram_tensor("cmk_s", [NL, C, FFND], BF16, kind="Internal").ap()
    cmv_s = nc.dram_tensor("cmv_s", [NL, FFND, C], BF16, kind="Internal").ap()
    ksp = nc.dram_tensor("ksp", [C, T], BF16, kind="Internal").ap()
    vtsp = nc.dram_tensor("vtsp", [128, 8 * 32 * 130], BF16, kind="Internal").ap()
    qsp = nc.dram_tensor("qsp", [C, T], BF16, kind="Internal").ap()

    import contextlib
    with contextlib.ExitStack() as stack:
        ent = stack.enter_context
        XRES = ent(nc.sbuf_tensor("XRES", [128, NCT * HALF], F32))
        ARENA = ent(nc.sbuf_tensor("ARENA", [128, 4 * NCT * 1024], BF16))
        B0 = ent(nc.sbuf_tensor("B0", [128, NCT * CH], F32))
        B1 = ent(nc.sbuf_tensor("B1", [128, NCT * CH], F32))
        B2 = ent(nc.sbuf_tensor("B2", [128, 2080], F32))
        B3 = ent(nc.sbuf_tensor("B3", [128, 2080], F32))
        XHATp = [ent(nc.sbuf_tensor(f"XHAT{i}", [128, NCT * (CH + 1)], BF16))
                 for i in range(2)]
        RWB = ent(nc.sbuf_tensor("RWB", [128, NCT * CH], BF16))
        SQB = ent(nc.sbuf_tensor("SQB", [128, NCT * CH], BF16))
        MXAp = [ent(nc.sbuf_tensor(f"MXA{i}", [128, NCT * CH], BF16))
                for i in range(2)]
        MXBp = [ent(nc.sbuf_tensor(f"MXB{i}", [128, NCT * CH], BF16))
                for i in range(2)]
        MXCp = [ent(nc.sbuf_tensor(f"MXC{i}", [128, NCT * CH], BF16))
                for i in range(2)]
        BCB = ent(nc.sbuf_tensor("BCB", [128, 2 * CH], BF16))
        PAR = ent(nc.sbuf_tensor("PAR", [128, PAR_USED], F32))
        PARB = ent(nc.sbuf_tensor("PARB", [128, PARB_COLS], BF16))
        PS = ent(nc.psum_tensor("PS", [128, 4096], F32))
        sem_t = ent(nc.semaphore("sem_t"))
        sem_v = ent(nc.semaphore("sem_v"))
        sem_s = ent(nc.semaphore("sem_s"))
        sem_g = ent(nc.semaphore("sem_g"))
        sem_gd = ent(nc.semaphore("sem_gd"))
        sem_sd = ent(nc.semaphore("sem_sd"))
        blk = ent(nc.Block())

        S = Sched(nc)
        for nm, h in [("t", sem_t), ("v", sem_v), ("s", sem_s),
                      ("g", sem_g), ("gd", sem_gd), ("sd", sem_sd)]:
            S.add_sem(nm, h)

        cx = dict(
            xt=xt, out=out, tmkT=tmkT, tmvT=tmvT, tmrT=tmrT, tmoT=tmoT,
            cmkT=cmkT, cmvT=cmvT, cmrT=cmrT, wqT=wqT, wkT=wkT, wvT=wvT,
            woT=woT, frT=frT, fkT=fkT, fvT=fvT, vecs=vecs,
            cmk_s=cmk_s, cmv_s=cmv_s, ksp=ksp, vtsp=vtsp, qsp=qsp,
            XRES=XRES, ARENA=ARENA, B0=B0, B1=B1, B2=B2, B3=B3,
            XHATp=XHATp, RWB=RWB, SQB=SQB, MXAp=MXAp, MXBp=MXBp,
            MXCp=MXCp, BCB=BCB, PAR=PAR, PARB=PARB, PS=PS,
            free={},
            bankfree=[("v", 0)] * 8,
        )
        emit_all(S, cx, debug_stage)

        @blk.tensor
        def _(e):
            for f in S.ops["tensor"]:
                f(e)

        @blk.vector
        def _(e):
            for f in S.ops["vector"]:
                f(e)

        @blk.scalar
        def _(e):
            for f in S.ops["scalar"]:
                f(e)

        @blk.gpsimd
        def _(e):
            for f in S.ops["gpsimd"]:
                f(e)

        @blk.sync
        def _(e):
            for f in S.ops["sync"]:
                f(e)

    return nc



def fget(cx, name):
    return cx["free"].get(name, ("v", 0))


def fset(cx, name, sem, val):
    cx["free"][name] = (sem, val)


def emit_all(S, cx, debug_stage):
    prologue(S, cx)
    passes = (0,)
    if debug_stage and debug_stage.startswith("1"):
        debug_stage = debug_stage[1:]
    else:
        passes = (0, 1)
    for p in passes:
        pass_begin(S, cx, p)
        src0 = list(cx["xres_ready"])
        for c in range(NCH):
            done_v = emit_ln_fp32_inplace(S, cx, c, src0)
        cx["xres_ready"] = [("v", done_v)]
        if debug_stage == "h0":
            pass_store(S, cx, p)
            continue
        stop = emit_rwkv_stack(S, cx, p, debug_stage)
        if stop:
            pass_store(S, cx, p)
            continue
        emit_lnf_add(S, cx, p)
        if debug_stage == "x1":
            pass_store(S, cx, p)
            continue
        emit_ffn(S, cx, p, site=8)
        if debug_stage == "x2":
            pass_store(S, cx, p)
            continue
        emit_attn(S, cx, p)
        if debug_stage == "x3":
            pass_store(S, cx, p)
            continue
        emit_ffn(S, cx, p, site=9)
        pass_store(S, cx, p)

    S.op("sync", lambda e: None, waits=[("sd", S.m("sd"))])


def prologue(S, cx):
    PAR, PARB, B0 = cx["PAR"], cx["PARB"], cx["B0"]

    S.op("sync", lambda e: e.dma_start(B0[:, 0:VEC_COLS], cx["vecs"]),
         inc="sd", inc_n=16)
    lded = S.m("sd")

    def rawc(name, l=None):
        i = VEC_ORDER.index(name if l is None else f"{name}{l}")
        return B0[:, i * 8:(i + 1) * 8]

    def c(name, l=None):
        col = pcol(name, l)
        return PAR[:, col:col + 8]

    for pp in (0, 32, 64):
        S.op("vector", lambda e, pp=pp: e.memset(
            PARB[pp:pp + 1, PB_ONESROW:PB_ONESROW + 128], 1.0), inc="v")
    S.op("vector", lambda e: e.memset(PARB[:, PB_ONESCOL:PB_ONESCOL + 1], 1.0),
         inc="v")
    of = PMAP["onesf"]
    S.op("vector", lambda e: e.memset(PAR[0:1, of:of + 128], 1.0), inc="v")
    S.op("vector", lambda e: e.memset(PAR[32:33, of:of + 128], 1.0), inc="v")
    ec = PMAP["epsc"]
    S.op("vector", lambda e: e.memset(PAR[:, ec:ec + 1], EPS), inc="v")
    S.op("vector", lambda e: e.memset(PAR[:, PMAP["G"]:PMAP["G"] + NL * 8], 0.0),
         inc="v")
    S.op("vector", lambda e: e.memset(PAR[:, PMAP["H"]:PMAP["H"] + NL * 8], 0.0),
         inc="v")
    S.op("vector", lambda e: e.memset(PAR[:, PMAP["halo"]:PMAP["halo"] + 80], 0.0),
         inc="v")

    for l in range(NL):
        for (a, b, mix, w) in [("Ak", "Bk", "raw_mk", "raw_ln1w"),
                               ("Av", "Bv", "raw_mv", "raw_ln1w"),
                               ("Ar", "Br", "raw_mr", "raw_ln1w"),
                               ("Ack", "Bck", "raw_cmk", "raw_ln2w"),
                               ("Acr", "Bcr", "raw_cmr", "raw_ln2w")]:
            S.op("vector", lambda e, a=a, mix=mix, w=w, l=l: e.tensor_tensor(
                c(a, l), rawc(mix, l), rawc(w, l), ALU.mult),
                waits=[("sd", lded)], inc="v")
            S.op("vector", lambda e, a=a, b=b, w=w, l=l: e.tensor_tensor(
                c(b, l), rawc(w, l), c(a, l), ALU.subtract), inc="v")
        S.op("scalar", lambda e, l=l: e.activation(
            c("adec", l), rawc("raw_td", l), AF.Exp), waits=[("sd", lded)],
            inc="s")
        S.op("scalar", lambda e, l=l: e.activation(
            c("adec", l), c("adec", l), AF.Exp, scale=-1.0), inc="s")
        S.op("scalar", lambda e, l=l: e.activation(
            c("Eu", l), rawc("raw_tf", l), AF.Exp, scale=-1.0), inc="s")
    for (a, b, raw) in [("fAr", "fBr", "raw_mur"), ("fAk", "fBk", "raw_muk")]:
        S.op("vector", lambda e, a=a, raw=raw: e.tensor_copy(c(a), rawc(raw)),
             waits=[("sd", lded)], inc="v")
        S.op("vector", lambda e, b=b, raw=raw: e.tensor_scalar(
            c(b), rawc(raw), -1.0, 1.0, ALU.mult, ALU.add), inc="v")
    for nm in ("lnfw", "bq", "bk", "bo", "bv"):
        S.op("vector", lambda e, nm=nm: e.tensor_copy(c(nm), rawc(nm)),
             waits=[("sd", lded)], inc="v")
    fset(cx, "B0", "v", S.m("v"))
    cx["params_v"] = S.m("v")
    cx["params_s"] = S.m("s")

    cx["cmprep"] = {}
    for l in range(NL):
        S.op("gpsimd", lambda e, l=l: e.dma_start(cx["cmk_s"][l], cx["cmkT"][l]),
             inc="gd", inc_n=16)
        S.op("gpsimd", lambda e, l=l: e.dma_start(cx["cmv_s"][l], cx["cmvT"][l]),
             inc="gd", inc_n=16)
        cx["cmprep"][l] = S.m("gd")


def pass_begin(S, cx, p):
    XRES = cx["XRES"]
    cx["pass"] = p
    src = cx["xt"].rearrange("(ct p) t -> p ct t", p=128)[:, :, p * HALF:(p + 1) * HALF]
    waits = [fget(cx, "XRES")]
    S.op("sync", lambda e: e.dma_start(r3(XRES[:], NCT), src),
         waits=waits, inc="sd", inc_n=16)
    cx["xres_ready"] = [("sd", S.m("sd"))]


def pass_store(S, cx, p):
    XRES = cx["XRES"]
    dst = cx["out"].rearrange("(ct p) t -> p ct t", p=128)[:, :, p * HALF:(p + 1) * HALF]
    S.op("sync", lambda e: e.dma_start(dst, r3(XRES[:], NCT)),
         waits=list(cx["xres_ready"]), inc="sd", inc_n=16)
    fset(cx, "XRES", "sd", S.m("sd"))


def emit_stats_pre(S, cx, src3, src_waits, par):
    SQB = cx["SQB"]
    XHAT = cx["XHATp"][par]
    xhv = r3(XHAT[:], NCT)[:, :, 1:1 + CH]
    S.op("scalar", lambda e: e.activation(xhv, src3, AF.Copy),
         waits=list(src_waits) + [fget(cx, f"XHAT{par}")], inc="s")
    cast_s = S.m("s")
    S.op("vector", lambda e: e.tensor_tensor(r3(SQB[:], NCT), xhv, xhv,
                                             ALU.mult),
         waits=[("s", cast_s), fget(cx, "SQB")], inc="v")
    return dict(cast_s=cast_s, sq_v=S.m("v"))


def emit_stats(S, cx, src3, src_waits, pbank, pre=None):
    PS, SQB, PARB = cx["PS"], cx["SQB"], cx["PARB"]
    par = pbank - 6
    XHAT = cx["XHATp"][par]
    SQF = SQB[:].bitcast(F32)
    bank = PS[:, pbank * 512:(pbank + 1) * 512]

    if pre is None:
        pre = emit_stats_pre(S, cx, src3, src_waits, par)
    cast_s, sq_v = pre["cast_s"], pre["sq_v"]

    onescol = PARB[:, PB_ONESCOL:PB_ONESCOL + 1]

    def fsum(t):
        inst = None
        for ct in range(NCT):
            inst = t.matmul(bank[0:1, 0:CH], onescol,
                            r3(XHAT[:], NCT)[:, ct, 1:1 + CH],
                            start=(ct == 0), stop=(ct == NCT - 1))
        for ct in range(NCT):
            inst = t.matmul(bank[0:1, CH:2 * CH], onescol,
                            r3(SQB[:], NCT)[:, ct],
                            start=(ct == 0), stop=(ct == NCT - 1))
        return inst
    S.op("tensor", fsum, waits=[("s", cast_s), ("v", sq_v),
                                cx["bankfree"][pbank],
                                ("v", cx["params_v"]), ("s", cx["params_s"])],
         inc="t")
    stats_t = S.m("t")
    fset(cx, f"XHAT{par}", "t", stats_t)
    fset(cx, "SQB", "t", stats_t)

    mu_f = SQF[0:1, 0:CH]
    var_f = SQF[0:1, CH:2 * CH]
    rstd_f = SQF[0:1, 2 * CH:3 * CH]
    scr_f = SQF[0:1, 3 * CH:4 * CH]
    S.op("vector", lambda e: e.tensor_scalar(mu_f, bank[0:1, 0:CH], INV_C,
                                             None, ALU.mult),
         waits=[("t", stats_t)], inc="v")
    S.op("vector", lambda e: e.tensor_scalar(var_f, bank[0:1, CH:2 * CH],
                                             INV_C, None, ALU.mult), inc="v")
    S.op("vector", lambda e: e.tensor_tensor(scr_f, mu_f, mu_f, ALU.mult),
         inc="v")
    S.op("vector", lambda e: e.tensor_tensor(var_f, var_f, scr_f,
                                             ALU.subtract), inc="v")
    fin_v = S.m("v")

    epsap = cx["PAR"][0:1, PMAP["epsc"]:PMAP["epsc"] + 1]
    S.op("scalar", lambda e: e.activation(scr_f, var_f, AF.Ln, bias=epsap),
         waits=[("v", fin_v)], inc="s")
    S.op("scalar", lambda e: e.activation(rstd_f, scr_f, AF.Exp, scale=-0.5),
         inc="s")
    S.op("scalar", lambda e: e.activation(PARB[0:1, PB_RSTD + par * CH:
                                               PB_RSTD + par * CH + CH],
                                          rstd_f, AF.Copy), inc="s")
    S.op("scalar", lambda e: e.activation(PARB[0:1, PB_MU + par * CH:
                                               PB_MU + par * CH + CH],
                                          mu_f, AF.Copy), inc="s")
    rows_s = S.m("s")
    fset(cx, "SQB", "s", rows_s)
    return dict(rows_s=rows_s, fin_v=fin_v, pbank=pbank, par=par,
                mu_f=mu_f, rstd_f=rstd_f)


def emit_bcast(S, cx, st, fp32=False):
    PS, PARB, PAR = cx["PS"], cx["PARB"], cx["PAR"]
    par = st["par"]
    bank = PS[:, st["pbank"] * 512:(st["pbank"] + 1) * 512]
    if fp32:
        lhs_r = PAR[0:1, PMAP["onesf"]:PMAP["onesf"] + 128]
        lhs_m = lhs_r
        rstd_src = st["rstd_f"]
        mu_src = st["mu_f"]
        w = [("s", st["rows_s"]), ("v", st["fin_v"])]
    else:
        lhs_r = PARB[0:1, PB_ONESROW:PB_ONESROW + 128]
        lhs_m = lhs_r
        rstd_src = PARB[0:1, PB_RSTD + par * CH:PB_RSTD + par * CH + CH]
        mu_src = PARB[0:1, PB_MU + par * CH:PB_MU + par * CH + CH]
        w = [("s", st["rows_s"])]

    def fbc(e):
        e.matmul(bank[:, 0:CH], lhs_r, rstd_src, start=True, stop=True)
        return e.matmul(bank[:, CH:2 * CH], lhs_m, mu_src, start=True,
                        stop=True)
    S.op("tensor", fbc, waits=w, inc="t")
    return S.m("t")


def emit_ln_fp32_inplace(S, cx, c, src_waits):
    XRES, PS = cx["XRES"], cx["PS"]
    par = c % 2
    pbank = 6 + par
    tmp = [cx["B0"], cx["B1"]][par]
    tname = ["B0", "B1"][par]
    xr = r3(XRES[:], NCT)[:, :, c * CH:(c + 1) * CH]
    st = emit_stats(S, cx, xr, src_waits, pbank)
    bc_t = emit_bcast(S, cx, st, fp32=True)
    fset(cx, "SQB", "t", bc_t)
    bank = PS[:, pbank * 512:(pbank + 1) * 512]
    mu_bc = bank[:, CH:2 * CH].unsqueeze(1).broadcast_to([128, NCT, CH])
    rstd_bc = bank[:, 0:CH].unsqueeze(1).broadcast_to([128, NCT, CH])
    b0r = r3(tmp[:, 0:NCT * CH], NCT)
    S.op("vector", lambda e: e.tensor_tensor(b0r, xr, mu_bc, ALU.subtract),
         waits=[("t", bc_t), fget(cx, tname)], inc="v")
    S.op("vector", lambda e: e.tensor_tensor(xr, b0r, rstd_bc, ALU.mult),
         inc="v")
    done_v = S.m("v")
    cx["bankfree"][pbank] = ("v", done_v)
    fset(cx, tname, "v", done_v)
    return done_v


def emit_ln_prep_pre(S, cx, c):
    xr = r3(cx["XRES"][:], NCT)[:, :, c * CH:(c + 1) * CH]
    return emit_stats_pre(S, cx, xr, cx["xres_ready"], c % 2)


def emit_ln_prep_stats(S, cx, c, site, pre=None):
    XRES, PS = cx["XRES"], cx["PS"]
    par = c % 2
    pbank = 6 + par
    XHAT = cx["XHATp"][par]
    BCB = cx["BCB"]
    xr = r3(XRES[:], NCT)[:, :, c * CH:(c + 1) * CH]
    st = emit_stats(S, cx, xr, cx["xres_ready"], pbank, pre=pre)
    bc_t = emit_bcast(S, cx, st, fp32=False)
    bcsrc = r3(PS[:, pbank * 512:(pbank + 1) * 512][:, 0:2 * CH], 2)
    S.op("scalar", lambda e: e.activation(r3(BCB[:], 2), bcsrc, AF.Copy),
         waits=[("t", bc_t), fget(cx, "BCB")], inc="s")
    bcb_s = S.m("s")
    cx["bankfree"][pbank] = ("s", bcb_s)
    return dict(par=par, site=site, bcb_s=bcb_s, xh=r3(XHAT[:], NCT), mix={})


def emit_ln_prep_finish(S, cx, rec, mixes):
    PAR = cx["PAR"]
    par, site, xh = rec["par"], rec["site"], rec["xh"]
    BCB = cx["BCB"]
    xh_cur = xh[:, :, 1:1 + CH]
    mu_bc = BCB[:, CH:2 * CH].unsqueeze(1).broadcast_to([128, NCT, CH])
    rstd_bc = BCB[:, 0:CH].unsqueeze(1).broadcast_to([128, NCT, CH])
    if site is not None:
        hc = PMAP["halo"] + site * 8
        S.op("vector", lambda e: e.tensor_copy(
            xh[:, :, 0:1], PAR[:, hc:hc + 8].bitcast(BF16)[:, 0:NCT]
            .rearrange("p (b w) -> p b w", b=NCT)), inc="v")
    S.op("vector", lambda e: e.tensor_tensor(xh_cur, xh_cur, mu_bc,
                                             ALU.subtract),
         waits=[("s", rec["bcb_s"])], inc="v")
    S.op("vector", lambda e: e.tensor_tensor(xh_cur, xh_cur, rstd_bc,
                                             ALU.mult), inc="v")
    xhat_v = S.m("v")
    fset(cx, "BCB", "v", xhat_v)
    if site is not None:
        hc = PMAP["halo"] + site * 8
        S.op("vector", lambda e: e.tensor_copy(
            PAR[:, hc:hc + 8].bitcast(BF16)[:, 0:NCT]
            .rearrange("p (b w) -> p b w", b=NCT), xh[:, :, CH:CH + 1]),
            inc="v")
    rec["xhat_v"] = xhat_v
    slot = {"A": "MXAp", "B": "MXBp", "C": "MXCp"}
    for (Aname, Bname, l, which) in mixes:
        dst = r3(cx[slot[which]][par][:], NCT)
        mv = emit_mix(S, cx, Aname, Bname, l, dst, f"MX{which}{par}", xhat_v,
                      xh)
        rec["mix"][which] = (dst, mv)
    return rec


def emit_ln_prep(S, cx, c, site, mixes):
    return emit_ln_prep_finish(S, cx, emit_ln_prep_stats(S, cx, c, site),
                               mixes)


def emit_mix(S, cx, Aname, Bname, l, dst, dst_name, xhat_v, xh):
    PAR = cx["PAR"]
    Ac = pcol(Aname, l) if l is not None else pcol(Aname)
    Bc = pcol(Bname, l) if l is not None else pcol(Bname)
    A_bc = r3(PAR[:, Ac:Ac + 8], NCT).broadcast_to([128, NCT, CH])
    S.op("vector", lambda e: e.tensor_tensor(dst, xh[:, :, 1:1 + CH], A_bc,
                                             ALU.mult),
         waits=[fget(cx, dst_name), ("v", xhat_v)], inc="v")

    def fb(e):
        inst = None
        for ct in range(NCT):
            inst = e.scalar_tensor_tensor(
                dst[:, ct], xh[:, ct, 0:CH], PAR[:, Bc + ct:Bc + ct + 1],
                dst[:, ct], ALU.mult, ALU.add)
        return inst
    S.op("vector", fb, inc="v")
    return S.m("v")


def emit_lnf_add(S, cx, p):
    XRES, PAR, PS = cx["XRES"], cx["PAR"], cx["PS"]
    lnfw = pcol("lnfw")
    src0 = list(cx["xres_ready"])
    for c in range(NCH):
        par = c % 2
        pbank = 6 + par
        tmp = [cx["B0"], cx["B1"]][par]
        tname = ["B0", "B1"][par]
        xbuf = [cx["B3"], cx["B2"]][par]
        xname = ["B3", "B2"][par]
        xr = r3(XRES[:], NCT)[:, :, c * CH:(c + 1) * CH]
        st = emit_stats(S, cx, xr, src0, pbank)
        bc_t = emit_bcast(S, cx, st, fp32=True)
        fset(cx, "SQB", "t", bc_t)
        bank = PS[:, pbank * 512:(pbank + 1) * 512]
        mu_bc = bank[:, CH:2 * CH].unsqueeze(1).broadcast_to([128, NCT, CH])
        rstd_bc = bank[:, 0:CH].unsqueeze(1).broadcast_to([128, NCT, CH])
        src = cx["xt"].rearrange("(ct p) t -> p ct t", p=128)[
            :, :, p * HALF + c * CH: p * HALF + (c + 1) * CH]
        b3r = r3(xbuf[:, 0:NCT * CH], NCT)
        S.op("sync", lambda e, src=src, b3r=b3r: e.dma_start(b3r, src),
             waits=[fget(cx, xname)], inc="sd", inc_n=16)
        xin_d = S.m("sd")
        b0r = r3(tmp[:, 0:NCT * CH], NCT)
        S.op("vector", lambda e, b0r=b0r, xr=xr, mu_bc=mu_bc: e.tensor_tensor(
            b0r, xr, mu_bc, ALU.subtract),
            waits=[("t", bc_t), fget(cx, tname)], inc="v")
        S.op("vector", lambda e, b0r=b0r, rstd_bc=rstd_bc: e.tensor_tensor(
            b0r, b0r, rstd_bc, ALU.mult), inc="v")
        bcd = S.m("v")
        cx["bankfree"][pbank] = ("v", bcd)

        def fn(e, b0r=b0r, xr=xr, b3r=b3r):
            inst = None
            for ct in range(NCT):
                inst = e.scalar_tensor_tensor(
                    xr[:, ct], b0r[:, ct], PAR[:, lnfw + ct:lnfw + ct + 1],
                    b3r[:, ct], ALU.mult, ALU.add)
            return inst
        S.op("vector", fn, waits=[("sd", xin_d)], inc="v")
        done_v = S.m("v")
        fset(cx, tname, "v", done_v)
        fset(cx, xname, "v", done_v)
    cx["xres_ready"] = [("v", done_v)]


PROJ_ROUNDS = [(0, 3, 0), (3, 6, 3), (6, 8, 0)]


def ps_banks(cx, blo, n):
    return r3(cx["PS"][:, blo * 512:(blo + n) * 512], n)[:, :, 0:CH]


def emit_proj(S, cx, warena_m, rhs3, rhs_wait, evac, wa3=None):
    ARENA, PS = cx["ARENA"], cx["PS"]
    if wa3 is None:
        wa3 = r3(ARENA[:], 4)[:, warena_m].rearrange("p (ct o) -> p ct o",
                                                     ct=NCT)
    for r, (glo, ghi, blo) in enumerate(PROJ_ROUNDS):
        def fn(t, glo=glo, ghi=ghi, blo=blo):
            inst = None
            for g in range(glo, ghi):
                bank = PS[:, (blo + g - glo) * 512:(blo + g - glo) * 512 + CH]
                for ct in range(NCT):
                    inst = t.matmul(bank, wa3[:, ct, g * 128:(g + 1) * 128],
                                    rhs3[:, ct], start=(ct == 0),
                                    stop=(ct == NCT - 1))
            return inst
        waits = [rhs_wait, cx["arena_ready"]] + [cx["bankfree"][blo + i]
                                                 for i in range(ghi - glo)]
        S.op("tensor", fn, waits=waits, inc="t")
        evac(r, glo, ghi, blo, S.m("t"))


def _dmp(S, cx, region, src, waits, cast=False, nct=NCT):
    dst = cx["out"].rearrange("(ct p) t -> p ct t", p=128)[
        :, 0:nct, region * CH:(region + 1) * CH]
    eng, inc = ("gpsimd", "gd") if cast else ("sync", "sd")
    S.op(eng, lambda e, dst=dst, src=src: e.dma_start(dst, src),
         waits=waits, inc=inc, inc_n=16)


def emit_rwkv_stack(S, cx, p, debug_stage):
    for l in range(NL):
        emit_rwkv_tm(S, cx, p, l)
        if debug_stage == f"L{l}tm":
            return True
        emit_rwkv_cm(S, cx, p, l)
        if debug_stage == f"L{l}":
            return True
    return False


def load_tm_arena(S, cx, l):
    ARENA = cx["ARENA"]
    waits = [("t", S.m("t"))]
    for m, w in enumerate([cx["tmkT"], cx["tmvT"], cx["tmrT"], cx["tmoT"]]):
        srcw = w[l].rearrange("(ct p) o -> p ct o", p=128)
        dst = r3(ARENA[:], 4)[:, m].rearrange("p (ct o) -> p ct o", ct=NCT)
        S.op("gpsimd", lambda e, dst=dst, srcw=srcw: e.dma_start(dst, srcw),
             waits=waits if m == 0 else [], inc="gd", inc_n=16)
    cx["arena_ready"] = ("gd", S.m("gd"))


def emit_rwkv_tm(S, cx, p, l):
    PS, PAR = cx["PS"], cx["PAR"]
    B0, B1, B2, B3 = cx["B0"], cx["B1"], cx["B2"], cx["B3"]
    XRES = cx["XRES"]
    load_tm_arena(S, cx, l)

    b0r = r3(B0[:], NCT)
    b1r = r3(B1[:], NCT)
    b2r = r3(B2[:, 0:NCT * (CH + 1)], NCT)
    b3r = r3(B3[:, 0:NCT * (CH + 1)], NCT)
    rw = r3(cx["RWB"][:], NCT)

    mixes = [("Ak", "Bk", l, "A"), ("Av", "Bv", l, "B"), ("Ar", "Br", l, "C")]
    preps = [emit_ln_prep(S, cx, 0, l, mixes)]
    for c in range(NCH):
        P = preps[c]
        par = P["par"]
        mxa, mixk_v = P["mix"]["A"]
        mxb, mixv_v = P["mix"]["B"]
        mxc, mixr_v = P["mix"]["C"]

        def ev_ek(r, glo, ghi, blo, mm_t):
            S.op("scalar", lambda e, glo=glo, ghi=ghi, blo=blo: e.activation(
                b0r[:, glo:ghi], ps_banks(cx, blo, ghi - glo), AF.Exp),
                waits=[("t", mm_t), fget(cx, "B0")], inc="s")
            for i in range(ghi - glo):
                cx["bankfree"][blo + i] = ("s", S.m("s"))
        emit_proj(S, cx, 0, mxa, ("v", mixk_v), ev_ek)
        ek_s = S.m("s")
        fset(cx, "B0", "s", ek_s)
        fset(cx, f"MXA{par}", "t", S.m("t"))
        if c + 1 < NCH:
            pre_next = emit_ln_prep_pre(S, cx, c + 1)

        def ev_v(r, glo, ghi, blo, mm_t):
            S.op("scalar", lambda e, glo=glo, ghi=ghi, blo=blo: e.activation(
                b1r[:, glo:ghi], ps_banks(cx, blo, ghi - glo), AF.Copy),
                waits=[("t", mm_t), fget(cx, "B1")], inc="s")
            for i in range(ghi - glo):
                cx["bankfree"][blo + i] = ("s", S.m("s"))
        emit_proj(S, cx, 1, mxb, ("v", mixv_v), ev_v)
        v_s = S.m("s")
        fset(cx, "B1", "s", v_s)
        fset(cx, f"MXB{par}", "t", S.m("t"))

        def ev_r(r, glo, ghi, blo, mm_t):
            S.op("scalar", lambda e, glo=glo, ghi=ghi, blo=blo: e.activation(
                b2r[:, glo:ghi, 0:CH], ps_banks(cx, blo, ghi - glo), AF.Exp,
                scale=-1.0),
                waits=[("t", mm_t), fget(cx, "B2")], inc="s")
            for i in range(ghi - glo):
                cx["bankfree"][blo + i] = ("s", S.m("s"))

        S.op("vector", lambda e: e.tensor_tensor(b1r, b0r, b1r, ALU.mult),
             waits=[("s", ek_s), ("s", v_s)], inc="v")
        c1_v = S.m("v")
        gcol = PMAP["G"] + l * 8
        hcol = PMAP["H"] + l * 8
        S.op("vector", lambda e: e.tensor_copy(
            b2r[:, :, 0:1], r3(PAR[:, gcol:gcol + 8], NCT)),
            waits=[fget(cx, "B2")], inc="v")
        S.op("vector", lambda e: e.tensor_copy(
            b3r[:, :, 0:1], r3(PAR[:, hcol:hcol + 8], NCT)),
            waits=[fget(cx, "B3")], inc="v")

        def scans(e):
            inst = None
            for ct in range(NCT):
                a_ap = PAR[:, pcol("adec", l) + ct: pcol("adec", l) + ct + 1]
                inst = e.tensor_tensor_scan(
                    b2r[:, ct, 1:1 + CH], a_ap.broadcast_to([128, CH]),
                    b1r[:, ct], PAR[:, gcol + ct:gcol + ct + 1],
                    ALU.mult, ALU.add)
            return inst
        S.op("vector", scans, waits=[("v", c1_v)], inc="v")

        def scand(e):
            inst = None
            for ct in range(NCT):
                a_ap = PAR[:, pcol("adec", l) + ct: pcol("adec", l) + ct + 1]
                inst = e.tensor_tensor_scan(
                    b3r[:, ct, 1:1 + CH], a_ap.broadcast_to([128, CH]),
                    b0r[:, ct], PAR[:, hcol + ct:hcol + ct + 1],
                    ALU.mult, ALU.add)
            return inst
        S.op("vector", scand, inc="v")
        S.op("vector", lambda e: e.tensor_copy(
            r3(PAR[:, gcol:gcol + 8], NCT), b2r[:, :, CH:CH + 1]), inc="v")
        S.op("vector", lambda e: e.tensor_copy(
            r3(PAR[:, hcol:hcol + 8], NCT), b3r[:, :, CH:CH + 1]), inc="v")

        def num(e):
            inst = None
            for ct in range(NCT):
                eu = PAR[:, pcol("Eu", l) + ct: pcol("Eu", l) + ct + 1]
                inst = e.scalar_tensor_tensor(b1r[:, ct], b2r[:, ct, 0:CH], eu,
                                              b1r[:, ct], ALU.mult, ALU.add)
            return inst
        S.op("vector", num, inc="v")
        num_v = S.m("v")

        def den(e):
            inst = None
            for ct in range(NCT):
                eu = PAR[:, pcol("Eu", l) + ct: pcol("Eu", l) + ct + 1]
                inst = e.scalar_tensor_tensor(b0r[:, ct], b3r[:, ct, 0:CH], eu,
                                              b0r[:, ct], ALU.mult, ALU.add)
            return inst
        S.op("vector", den, inc="v")
        den_v = S.m("v")
        fset(cx, "B2", "v", num_v)
        fset(cx, "B3", "v", den_v)

        emit_proj(S, cx, 2, mxc, ("v", mixr_v), ev_r)
        esm_s = S.m("s")
        fset(cx, "B2", "s", esm_s)
        fset(cx, f"MXC{par}", "t", S.m("t"))

        if c + 1 < NCH:
            pr_next = emit_ln_prep_stats(S, cx, c + 1, l, pre=pre_next)

        S.op("vector", lambda e: e.tensor_scalar(
            b2r[:, :, 0:CH], b2r[:, :, 0:CH], 1.0, None, ALU.add),
            waits=[("s", esm_s)], inc="v")
        S.op("vector", lambda e: e.tensor_tensor(
            b0r, b0r, b2r[:, :, 0:CH], ALU.mult),
            waits=[("v", den_v)], inc="v")
        d2_v = S.m("v")
        fset(cx, "B2", "v", d2_v)
        S.op("scalar", lambda e: e.activation(b3r[:, :, 0:CH], b0r, AF.Ln),
             waits=[("v", d2_v)], inc="s")
        S.op("scalar", lambda e: e.activation(b0r, b3r[:, :, 0:CH], AF.Exp,
                                              scale=-1.0), inc="s")
        rec_s = S.m("s")
        fset(cx, "B3", "s", rec_s)
        S.op("vector", lambda e: e.tensor_tensor(rw, b1r, b0r, ALU.mult),
             waits=[("s", rec_s), ("v", num_v), fget(cx, "RWB")], inc="v")
        rw_v = S.m("v")
        fset(cx, "RWB", "v", rw_v)
        fset(cx, "B0", "v", rw_v)
        fset(cx, "B1", "v", rw_v)

        xr = r3(XRES[:], NCT)[:, :, c * CH:(c + 1) * CH]

        def ev_o(r, glo, ghi, blo, mm_t, xr=xr):
            S.op("vector", lambda e, glo=glo, ghi=ghi, blo=blo, xr=xr:
                 e.tensor_tensor(xr[:, glo:ghi], xr[:, glo:ghi],
                                 ps_banks(cx, blo, ghi - glo), ALU.add),
                 waits=[("t", mm_t)] + cx["xres_ready"], inc="v")
            for i in range(ghi - glo):
                cx["bankfree"][blo + i] = ("v", S.m("v"))
        emit_proj(S, cx, 3, rw, ("v", rw_v), ev_o)
        oadd_v = S.m("v")
        fset(cx, "RWB", "t", S.m("t"))
        cx["xres_ready"] = [("v", oadd_v)]
        if c + 1 < NCH:
            preps.append(emit_ln_prep_finish(S, cx, pr_next, mixes))


def emit_rwkv_cm(S, cx, p, l):
    PS, PAR = cx["PS"], cx["PAR"]
    B0, B1, B2, B3 = cx["B0"], cx["B1"], cx["B2"], cx["B3"]
    ARENA, XRES = cx["ARENA"], cx["XRES"]

    cmr_dst = ARENA[:, 24576:32768].rearrange("p (ct o) -> p ct o", ct=NCT)
    cmr3 = ARENA[:, 24576:32768].rearrange("p (ct o) -> p ct o", ct=NCT)
    tm_done_t = S.m("t")
    S.op("gpsimd", lambda e: e.dma_start(
        cmr_dst, cx["cmrT"][l].rearrange("(ct p) o -> p ct o", p=128)),
        waits=[("t", tm_done_t)], inc="gd", inc_n=16)
    cmr_gd = S.m("gd")
    for nm in ("CKSLOT0", "CKSLOT1", "CV0", "CV1"):
        fset(cx, nm, "t", tm_done_t)

    b0r = r3(B0[:], NCT)
    b1r = r3(B1[:], NCT)
    b2r = r3(B2[:, 0:NCT * CH], NCT)
    sq_a = cx["B2"][:, 0:NCT * CH].bitcast(BF16)
    sq_b = cx["B3"][:, 0:NCT * CH].bitcast(BF16)

    def sq_slice(s):
        buf = sq_a if s < 4 else sq_b
        return r3(buf[:, (s % 4) * 1024:(s % 4) * 1024 + 1024], 4)

    prep = cx["cmprep"][l]
    fset(cx, "SQ", "s", S.m("s"))
    mixes = [("Ack", "Bck", l, "A"), ("Acr", "Bcr", l, "C")]
    preps = [emit_ln_prep(S, cx, 0, 4 + l, mixes)]
    for c in range(NCH):
        P = preps[c]
        par = P["par"]
        mxa, mixk_v = P["mix"]["A"]
        mxc, mixr_v = P["mix"]["C"]

        last_sq_v = 0
        for s in range(8):
            slot = s % 2
            slot3 = r3(ARENA[:, slot * 4096:(slot + 1) * 4096], NCT)
            srcw = cx["cmk_s"][l].rearrange("(ct p) f -> p ct f", p=128)[
                :, :, s * 512:(s + 1) * 512]
            S.op("sync", lambda e, slot3=slot3, srcw=srcw: e.dma_start(
                slot3, srcw),
                waits=[("gd", prep), fget(cx, f"CKSLOT{slot}")],
                inc="sd", inc_n=16)
            sl_sd = S.m("sd")
            blo = 3 * (s % 2)

            def fkk(t, slot3=slot3, blo=blo, mxa=mxa):
                inst = None
                for fi in range(4):
                    if fi < 3:
                        bank = PS[:, (blo + fi) * 512:(blo + fi) * 512 + CH]
                    else:
                        bank = PS[:, blo * 512 + CH:blo * 512 + 2 * CH]
                    for ct in range(NCT):
                        st = (ct == 0)
                        inst = t.matmul(bank,
                                        slot3[:, ct, fi * 128:(fi + 1) * 128],
                                        mxa[:, ct], start=st,
                                        stop=(ct == NCT - 1))
                return inst
            S.op("tensor", fkk, waits=[("sd", sl_sd), ("v", mixk_v)] +
                 [cx["bankfree"][blo + i] for i in range(3)], inc="t")
            kk_t = S.m("t")
            fset(cx, f"CKSLOT{slot}", "t", kk_t)
            if cx.get("dbgcm") == (l, c) and s == 0:
                stg = r3(cx["RWB"][:], NCT)
                S.op("scalar", lambda e, stg=stg: e.activation(
                    stg[:, 0:3], r3(PS[:, 0:3 * 512], 3)[:, :, 0:CH],
                    AF.Copy), waits=[("t", kk_t)], inc="s")
                S.op("scalar", lambda e, stg=stg: e.activation(
                    stg[:, 3], PS[:, CH:2 * CH], AF.Copy), inc="s")
                _dmp(S, cx, 14, stg[:, 0:4], [("s", S.m("s"))], cast=True,
                     nct=4)
                for i in range(3):
                    cx["bankfree"][i] = ("gd", S.m("gd"))
            sqv = sq_slice(s)

            def evk(e, sqv=sqv, blo=blo):
                src = r3(PS[:, blo * 512:(blo + 3) * 512], 3)[:, :, 0:CH]
                e.tensor_scalar(sqv[:, 0:3], src, 0.0, None, ALU.max)
                return e.tensor_scalar(
                    sqv[:, 3], PS[:, blo * 512 + CH:blo * 512 + 2 * CH], 0.0,
                    None, ALU.max)
            S.op("vector", evk, waits=[("t", kk_t), fget(cx, "SQ")], inc="v")
            S.op("vector", lambda e, sqv=sqv: e.tensor_tensor(
                sqv, sqv, sqv, ALU.mult), inc="v")
            last_sq_v = S.m("v")
            for i in range(3):
                cx["bankfree"][blo + i] = ("v", last_sq_v)

        if c + 1 < NCH:
            pre_next = emit_ln_prep_pre(S, cx, c + 1)

        def ev_rr(r, glo, ghi, blo, mm_t):
            S.op("scalar", lambda e, glo=glo, ghi=ghi, blo=blo: e.activation(
                b0r[:, glo:ghi], ps_banks(cx, blo, ghi - glo), AF.Exp,
                scale=-1.0),
                waits=[("t", mm_t), fget(cx, "B0")], inc="s")
            for i in range(ghi - glo):
                cx["bankfree"][blo + i] = ("s", S.m("s"))
        cx["arena_ready"] = ("gd", cmr_gd)
        emit_proj(S, cx, None, mxc, ("v", mixr_v), ev_rr, wa3=cmr3)
        e2m_s = S.m("s")
        fset(cx, "B0", "s", e2m_s)
        fset(cx, f"MXC{par}", "t", S.m("t"))
        dwait1, dwait2 = [], []
        if cx.get("dbgcm") == (l, c):
            _dmp(S, cx, 9, b0r, [("s", e2m_s)])
            dwait2 = [("sd", S.m("sd"))]
        S.op("vector", lambda e: e.tensor_scalar(b0r, b0r, 1.0, None, ALU.add),
             waits=[("s", e2m_s)] + dwait2, inc="v")
        d_v = S.m("v")

        def cv_slot(sl):
            return ARENA[:, 8192 + sl * 8192:8192 + (sl + 1) * 8192]\
                .rearrange("p (ft o) -> p ft o", ft=8)

        def cv_load(s):
            sl = s % 2
            srcw = cx["cmv_s"][l][s * 1024:(s + 1) * 1024].rearrange(
                "(ft p) o -> p ft o", p=128)
            S.op("sync", lambda e, sl=sl, srcw=srcw: e.dma_start(
                cv_slot(sl), srcw),
                waits=[("gd", prep), fget(cx, f"CV{sl}")], inc="sd", inc_n=16)
            return S.m("sd")
        cv_sd = {0: cv_load(0), 1: cv_load(1)}
        for s in range(4):
            slot3 = cv_slot(s % 2)
            for fi in range(8):
                ft = s * 8 + fi
                buf = sq_a if ft < 16 else sq_b
                ft_sq = buf[:, (ft % 16) * CH:(ft % 16) * CH + CH]

                def fvv(t, slot3=slot3, fi=fi, ft=ft, ft_sq=ft_sq):
                    inst = None
                    for g in range(8):
                        inst = t.matmul(
                            PS[:, g * 512:g * 512 + CH],
                            slot3[:, fi, g * 128:(g + 1) * 128],
                            ft_sq, start=(ft == 0), stop=(ft == NFT - 1))
                    return inst
                wv = [("sd", cv_sd[s]), ("v", last_sq_v)]
                if ft == 0:
                    wv += [cx["bankfree"][i] for i in range(8)]
                S.op("tensor", fvv, waits=wv, inc="t")
            fset(cx, f"CV{s % 2}", "t", S.m("t"))
            if s + 2 < 4:
                cv_sd[s + 2] = cv_load(s + 2)
        vv_t = S.m("t")
        S.op("scalar", lambda e: e.activation(
            b1r[:, 6:8], r3(PS[:, 6 * 512:8 * 512], 2)[:, :, 0:CH], AF.Copy),
            waits=[("t", vv_t), fget(cx, "B1")], inc="s")
        ev67_s = S.m("s")
        cx["bankfree"][6] = ("s", ev67_s)
        cx["bankfree"][7] = ("s", ev67_s)
        S.op("scalar", lambda e: e.activation(
            b1r[:, 0:6], r3(PS[:, 0:6 * 512], 6)[:, :, 0:CH], AF.Copy),
            inc="s")
        vv_s = S.m("s")
        for i in range(6):
            cx["bankfree"][i] = ("s", vv_s)
        fset(cx, "B1", "s", vv_s)
        fset(cx, "B2", "t", vv_t)
        fset(cx, "B3", "t", vv_t)

        if cx.get("dbgcm") == (l, c):
            _dmp(S, cx, 8, b1r, [("s", vv_s)])
            dwait1 = [("sd", S.m("sd"))]
            _dmp(S, cx, 12, r3(sq_a[:, 0:2048], NCT), [("v", last_sq_v)],
                 cast=True)
            _dmp(S, cx, 10, mxa, [("v", mixk_v)], cast=True)
            _dmp(S, cx, 11, mxc, [("v", mixr_v)], cast=True)
            _dmp(S, cx, 13, P["xh"][:, :, 1:1 + CH], [("v", P["xhat_v"])],
                 cast=True)

        if c + 1 < NCH:
            pr_next = emit_ln_prep_stats(S, cx, c + 1, 4 + l, pre=pre_next)
            preps.append(emit_ln_prep_finish(S, cx, pr_next, mixes))

        S.op("scalar", lambda e: e.activation(b2r, b0r, AF.Ln),
             waits=[("v", d_v), fget(cx, "B2")], inc="s")
        S.op("scalar", lambda e: e.activation(b0r, b2r, AF.Exp, scale=-1.0),
             inc="s")
        rec_s = S.m("s")
        S.op("vector", lambda e: e.tensor_tensor(b1r, b1r, b0r, ALU.mult),
             waits=[("s", rec_s), ("s", vv_s)] + dwait1, inc="v")
        xr = r3(XRES[:], NCT)[:, :, c * CH:(c + 1) * CH]
        S.op("vector", lambda e, xr=xr: e.tensor_tensor(xr, xr, b1r, ALU.add),
             waits=list(cx["xres_ready"]), inc="v")
        done_v = S.m("v")
        fset(cx, "B0", "v", done_v)
        fset(cx, "B1", "v", done_v)
        fset(cx, "B2", "s", rec_s)
        fset(cx, "B3", "s", rec_s)
        fset(cx, "SQ", "s", rec_s)
        fset(cx, f"MXA{par}", "t", S.m("t"))
        cx["xres_ready"] = [("v", done_v)]


def emit_ffn(S, cx, p, site):
    PS = cx["PS"]
    B0, B1, B2 = cx["B0"], cx["B1"], cx["B2"]
    ARENA, XRES = cx["ARENA"], cx["XRES"]
    waits = [("t", S.m("t"))]
    for m, w in enumerate([cx["frT"], cx["fkT"], cx["fvT"]]):
        srcw = w.rearrange("(ct p) o -> p ct o", p=128)
        dst = r3(ARENA[:], 4)[:, m].rearrange("p (ct o) -> p ct o", ct=NCT)
        S.op("gpsimd", lambda e, dst=dst, srcw=srcw: e.dma_start(dst, srcw),
             waits=waits if m == 0 else [], inc="gd", inc_n=16)
    cx["arena_ready"] = ("gd", S.m("gd"))

    b0r = r3(B0[:], NCT)
    b1r = r3(B1[:], NCT)
    b2f = r3(B2[:, 0:NCT * CH], NCT)
    sq3 = r3(B2[:, 0:NCT * CH].bitcast(BF16)[:, 0:NCT * CH], NCT)

    mixes = [("fAr", "fBr", None, "A"), ("fAk", "fBk", None, "B")]
    preps = [emit_ln_prep(S, cx, 0, site, mixes)]
    for c in range(NCH):
        P = preps[c]
        par = P["par"]
        mxa, mixr_v = P["mix"]["A"]
        mxb, mixk_v = P["mix"]["B"]

        def ev_r(r, glo, ghi, blo, mm_t):
            S.op("scalar", lambda e, glo=glo, ghi=ghi, blo=blo: e.activation(
                b0r[:, glo:ghi], ps_banks(cx, blo, ghi - glo), AF.Exp,
                scale=-1.0),
                waits=[("t", mm_t), fget(cx, "B0")], inc="s")
            for i in range(ghi - glo):
                cx["bankfree"][blo + i] = ("s", S.m("s"))
        emit_proj(S, cx, 0, mxa, ("v", mixr_v), ev_r)
        erm_s = S.m("s")
        fset(cx, "B0", "s", erm_s)
        fset(cx, f"MXA{par}", "t", S.m("t"))
        if c + 1 < NCH:
            pre_next = emit_ln_prep_pre(S, cx, c + 1)

        def ev_k(r, glo, ghi, blo, mm_t):
            S.op("vector", lambda e, glo=glo, ghi=ghi, blo=blo: e.tensor_scalar(
                sq3[:, glo:ghi], ps_banks(cx, blo, ghi - glo), 0.0, None,
                ALU.max),
                waits=[("t", mm_t), fget(cx, "B2")], inc="v")
            S.op("vector", lambda e, glo=glo, ghi=ghi: e.tensor_tensor(
                sq3[:, glo:ghi], sq3[:, glo:ghi], sq3[:, glo:ghi], ALU.mult),
                inc="v")
            for i in range(ghi - glo):
                cx["bankfree"][blo + i] = ("v", S.m("v"))
        emit_proj(S, cx, 1, mxb, ("v", mixk_v), ev_k)
        sq_v = S.m("v")
        fset(cx, "B2", "v", sq_v)
        fset(cx, f"MXB{par}", "t", S.m("t"))

        def ev_v(r, glo, ghi, blo, mm_t):
            S.op("scalar", lambda e, glo=glo, ghi=ghi, blo=blo: e.activation(
                b1r[:, glo:ghi], ps_banks(cx, blo, ghi - glo), AF.Copy),
                waits=[("t", mm_t), fget(cx, "B1")], inc="s")
            for i in range(ghi - glo):
                cx["bankfree"][blo + i] = ("s", S.m("s"))
        emit_proj(S, cx, 2, sq3, ("v", sq_v), ev_v)
        vv_s = S.m("s")
        fset(cx, "B1", "s", vv_s)
        fset(cx, "B2", "t", S.m("t"))

        if c + 1 < NCH:
            pr_next = emit_ln_prep_stats(S, cx, c + 1, site, pre=pre_next)

        S.op("vector", lambda e: e.tensor_scalar(b0r, b0r, 1.0, None, ALU.add),
             waits=[("s", erm_s)], inc="v")
        d_v = S.m("v")
        S.op("scalar", lambda e: e.activation(b2f, b0r, AF.Ln),
             waits=[("v", d_v), fget(cx, "B2")], inc="s")
        S.op("scalar", lambda e: e.activation(b0r, b2f, AF.Exp, scale=-1.0),
             inc="s")
        rec_s = S.m("s")
        S.op("vector", lambda e: e.tensor_tensor(b1r, b1r, b0r, ALU.mult),
             waits=[("s", rec_s), ("s", vv_s)], inc="v")
        xr = r3(XRES[:], NCT)[:, :, c * CH:(c + 1) * CH]
        S.op("vector", lambda e, xr=xr: e.tensor_tensor(xr, xr, b1r, ALU.add),
             waits=list(cx["xres_ready"]), inc="v")
        done_v = S.m("v")
        fset(cx, "B0", "v", done_v)
        fset(cx, "B1", "v", done_v)
        fset(cx, "B2", "s", rec_s)
        cx["xres_ready"] = [("v", done_v)]
        if c + 1 < NCH:
            preps.append(emit_ln_prep_finish(S, cx, pr_next, mixes))


def emit_attn(S, cx, p):
    PS, RWB, PAR, PARB = cx["PS"], cx["RWB"], cx["PAR"], cx["PARB"]
    B0, B1, B2, B3 = cx["B0"], cx["B1"], cx["B2"], cx["B3"]
    ARENA, XRES = cx["ARENA"], cx["XRES"]
    ksp, vtsp, qsp = cx["ksp"], cx["vtsp"], cx["qsp"]
    poff = p * NCH

    waits = [("t", S.m("t"))]
    for m, w in enumerate([cx["wqT"], cx["wkT"], cx["wvT"], cx["woT"]]):
        srcw = w.rearrange("(ct p) o -> p ct o", p=128)
        dst = r3(ARENA[:], 4)[:, m].rearrange("p (ct o) -> p ct o", ct=NCT)
        S.op("gpsimd", lambda e, dst=dst, srcw=srcw: e.dma_start(dst, srcw),
             waits=waits if m == 0 else [], inc="gd", inc_n=16)
    cx["arena_ready"] = ("gd", S.m("gd"))

    rwb3 = r3(RWB[:], NCT)
    stg = B2[:].bitcast(BF16)[:, 0:8 * 2 * 130]
    stg4 = stg.rearrange("p (pr tt w) -> p pr tt w", pr=8, tt=2)

    S.op("vector", lambda e: e.memset(
        stg.rearrange("p (b w) -> p b w", b=16, w=130)[:, :, 64:130:65], 1.0),
        waits=[fget(cx, "B2")], inc="v")
    ones_v = S.m("v")
    preps = [emit_ln_prep(S, cx, 0, None, [])]
    for c in range(NCH):
        if c + 1 < NCH:
            preps.append(emit_ln_prep(S, cx, c + 1, None, []))
        P = preps[c]
        par = P["par"]
        xh3 = P["xh"][:, :, 1:1 + CH]
        xhat_v = P["xhat_v"]
        gcol = p * HALF + c * CH

        def ev_q(r, glo, ghi, blo, mm_t):
            for g in range(glo, ghi):
                bq = PAR[:, pcol("bq") + g:pcol("bq") + g + 1]
                bank = PS[:, (blo + g - glo) * 512:(blo + g - glo) * 512 + CH]
                S.op("scalar", lambda e, g=g, bank=bank, bq=bq: e.activation(
                    rwb3[:, g], bank, AF.Identity, bias=bq),
                    waits=[("t", mm_t), fget(cx, "RWB")], inc="s")
                cx["bankfree"][blo + g - glo] = ("s", S.m("s"))
        emit_proj(S, cx, 0, xh3, ("v", xhat_v), ev_q)
        q_s = S.m("s")
        S.op("sync", lambda e, gcol=gcol: e.dma_start(
            qsp.rearrange("(ct p) t -> p ct t", p=128)[:, :, gcol:gcol + CH],
            rwb3), waits=[("s", q_s)], inc="sd", inc_n=16)
        fset(cx, "RWB", "sd", S.m("sd"))

        def ev_k(r, glo, ghi, blo, mm_t):
            for g in range(glo, ghi):
                bk = PAR[:, pcol("bk") + g:pcol("bk") + g + 1]
                bank = PS[:, (blo + g - glo) * 512:(blo + g - glo) * 512 + CH]
                S.op("scalar", lambda e, g=g, bank=bank, bk=bk: e.activation(
                    rwb3[:, g], bank, AF.Identity, bias=bk),
                    waits=[("t", mm_t), fget(cx, "RWB")], inc="s")
                cx["bankfree"][blo + g - glo] = ("s", S.m("s"))
        emit_proj(S, cx, 1, xh3, ("v", xhat_v), ev_k)
        k_s = S.m("s")
        S.op("sync", lambda e, gcol=gcol: e.dma_start(
            ksp.rearrange("(ct p) t -> p ct t", p=128)[:, :, gcol:gcol + CH],
            rwb3), waits=[("s", k_s)], inc="sd", inc_n=16)
        fset(cx, "RWB", "sd", S.m("sd"))

        wv3 = r3(ARENA[:], 4)[:, 2].rearrange("p (ct o) -> p ct o", ct=NCT)
        vmm_t = 0
        for tt in range(2):
            for nh in range(2):
                bank = PS[:, nh * 512:nh * 512 + 512]

                def fmm(t, tt=tt, nh=nh, bank=bank, P=P):
                    inst = None
                    for ct in range(NCT):
                        inst = t.matmul(
                            bank,
                            P["xh"][:, ct, 1 + tt * 128:1 + tt * 128 + 128],
                            wv3[:, ct, nh * 512:(nh + 1) * 512],
                            start=(ct == 0), stop=(ct == NCT - 1))
                    return inst
                S.op("tensor", fmm, waits=[("v", xhat_v),
                                           cx["bankfree"][nh],
                                           cx["arena_ready"]], inc="t")
            vmm_t = S.m("t")

            def ev(e, tt=tt):
                src_ap = PS[:, 0:1024].rearrange(
                    "p (pr h w) -> p pr h w", pr=8, h=2, w=64)
                dst_ap = stg4[:, :, tt, :].rearrange(
                    "p pr (h z) -> p pr h z", h=2, z=65)[:, :, :, 0:64]
                return e.activation(dst_ap, src_ap, AF.Copy)
            S.op("scalar", ev, waits=[("t", vmm_t), ("v", ones_v),
                                      fget(cx, "B2")], inc="s")
            cx["bankfree"][0] = ("s", S.m("s"))
            cx["bankfree"][1] = ("s", S.m("s"))
        fset(cx, f"XHAT{par}", "t", vmm_t)
        vstg_s = S.m("s")
        dst = vtsp.rearrange("p (pr tt w) -> p pr tt w", pr=8, tt=32,
                             w=130)[:, :, 2 * (poff + c):2 * (poff + c) + 2, :]
        S.op("sync", lambda e, dst=dst: e.dma_start(dst, stg4),
             waits=[("s", vstg_s)], inc="sd", inc_n=16)
        fset(cx, "B2", "sd", S.m("sd"))
    spills_sd = S.m("sd")

    kslot = [B0[:].bitcast(BF16), B1[:].bitcast(BF16)]
    vslot = [B2[:].bitcast(BF16), B3[:].bitcast(BF16)]
    ptbuf = cx["XHATp"][0][:]
    slot_free = {("k", 0): [fget(cx, "B0")], ("k", 1): [fget(cx, "B1")],
                 ("v", 0): [fget(cx, "B2")], ("v", 1): [fget(cx, "B3")]}
    cx["free"]["PT0"] = fget(cx, "XHAT0")
    cx["free"]["PT1"] = fget(cx, "XHAT0")
    cx["free"]["OB0"] = fget(cx, "XHAT0")
    cx["free"]["OB1"] = fget(cx, "XHAT0")
    SQF = cx["SQB"][:].bitcast(F32)
    QB = cx["MXAp"][0]

    for c in range(NCH):
        q0 = p * HALF + c * CH
        nkt = (q0 + CH) // 128
        S.op("sync", lambda e, q0=q0: e.dma_start(
            r3(QB[:], NCT),
            qsp.rearrange("(ct p) t -> p ct t", p=128)[:, :, q0:q0 + CH]),
            waits=[("sd", spills_sd), fget(cx, "MXA0")], inc="sd", inc_n=16)
        qd = S.m("sd")

        for pr in range(8):
            ks = kslot[pr % 2]
            vs = vslot[pr % 2]
            S.op("sync", lambda e, nkt=nkt, ks=ks, pr=pr: e.dma_start(
                ks[:, 0:nkt * 128], ksp[pr * 128:(pr + 1) * 128, 0:nkt * 128]),
                waits=[("sd", spills_sd)] + slot_free[("k", pr % 2)],
                inc="sd", inc_n=16)
            kd = S.m("sd")
            S.op("sync", lambda e, nkt=nkt, vs=vs, pr=pr: e.dma_start(
                vs[:, 0:nkt * 130],
                vtsp.rearrange("p (pr w) -> p pr w", pr=8, w=32 * 130)[
                    :, pr, 0:nkt * 130]),
                waits=[("sd", spills_sd)] + slot_free[("v", pr % 2)],
                inc="sd", inc_n=16)
            vd = S.m("sd")

            def emit_sc(kt):
                sb = [0, 2][kt % 2]

                def fsc(t, kt=kt, sb=sb, ks=ks, pr=pr):
                    t.matmul(PS[:, sb * 512:sb * 512 + CH],
                             ks[0:64, kt * 128:(kt + 1) * 128],
                             r3(QB[:], NCT)[0:64, pr, :],
                             start=True, stop=True, tile_position=(0, 0))
                    return t.matmul(PS[:, (sb + 1) * 512:(sb + 1) * 512 + CH],
                                    ks[64:128, kt * 128:(kt + 1) * 128],
                                    r3(QB[:], NCT)[64:128, pr, :],
                                    start=True, stop=True,
                                    tile_position=(64, 0))
                S.op("tensor", fsc, waits=[("sd", kd), ("sd", qd),
                                           cx["bankfree"][sb],
                                           cx["bankfree"][sb + 1]], inc="t")
                return S.m("t")

            sc_ts = {0: emit_sc(0)}
            for kt in range(nkt):
                sb = [0, 2][kt % 2]
                pt = kt % 2
                ptap = ptbuf[:, pt * 512:pt * 512 + 512]
                sc_t = sc_ts[kt]

                def fex(e, sb=sb, ptap=ptap):
                    src = PS[:, sb * 512:sb * 512 + 1024].rearrange(
                        "p (h w) -> p h w", h=2, w=512)[:, :, 0:CH]
                    return e.activation(
                        ptap.rearrange("p (h w) -> p h w", h=2, w=CH), src,
                        AF.Exp, scale=0.125)
                S.op("scalar", fex, waits=[("t", sc_t), fget(cx, f"PT{pt}")],
                     inc="s")
                ex_s = S.m("s")
                cx["bankfree"][sb] = ("s", ex_s)
                cx["bankfree"][sb + 1] = ("s", ex_s)
                avw = [("s", ex_s), ("sd", vd)]
                if kt * 128 > q0 - 128:
                    def fmask(e, kt=kt, ptap=ptap, q0=q0):
                        i = None
                        for h in range(2):
                            i = e.affine_select(
                                ptap[:, h * CH:(h + 1) * CH],
                                ptap[:, h * CH:(h + 1) * CH],
                                [[1, CH]], ALU.is_ge, 0.0,
                                base=q0 - kt * 128, channel_multiplier=-1)
                        return i
                    S.op("gpsimd", fmask, waits=[("s", ex_s)], inc="g")
                    avw = [("g", S.m("g")), ("sd", vd)]
                if kt + 1 < nkt:
                    sc_ts[kt + 1] = emit_sc(kt + 1)

                def fav(t, kt=kt, vs=vs, ptap=ptap, first=(kt == 0),
                        last=(kt == nkt - 1)):
                    t.matmul(PS[0:65, 4 * 512:4 * 512 + CH],
                             vs[:, kt * 130:kt * 130 + 65],
                             ptap[:, 0:CH], start=first, stop=last)
                    t.matmul(PS[64:128, 5 * 512:5 * 512 + CH],
                             vs[:, kt * 130 + 65:kt * 130 + 129],
                             ptap[:, CH:2 * CH], start=first, stop=last,
                             tile_position=(0, 64))
                    return t.matmul(PS[0:1, 5 * 512 + CH:5 * 512 + 2 * CH],
                                    vs[:, kt * 130 + 129:kt * 130 + 130],
                                    ptap[:, CH:2 * CH], start=first, stop=last)
                avw2 = avw + ([cx["bankfree"][4], cx["bankfree"][5]]
                              if kt == 0 else [])
                S.op("tensor", fav, waits=avw2, inc="t")
                fset(cx, f"PT{pt}", "t", S.m("t"))
            av_t = S.m("t")
            slot_free[("k", pr % 2)] = [("t", av_t)]
            slot_free[("v", pr % 2)] = [("t", av_t)]

            for h in range(2):
                pb = 64 if h == 0 else 0
                avbank = PS[0 + 64 * h:64 + 64 * h,
                            (4 + h) * 512:(4 + h) * 512 + CH]
                den_ap = (PS[64:65, 4 * 512:4 * 512 + CH] if h == 0 else
                          PS[0:1, 5 * 512 + CH:5 * 512 + 2 * CH])
                rrow = SQF[pb:pb + 1, 0:CH]
                rrow2 = SQF[pb:pb + 1, CH:2 * CH]
                brow = PARB[pb:pb + 1, PB_MU:PB_MU + CH]
                S.op("scalar", lambda e, den_ap=den_ap, rrow=rrow: e.activation(
                    rrow, den_ap, AF.Ln),
                    waits=[("t", av_t), fget(cx, f"DROW{h}")], inc="s")
                S.op("scalar", lambda e, rrow=rrow, rrow2=rrow2: e.activation(
                    rrow2, rrow, AF.Exp, scale=-1.0), inc="s")
                S.op("scalar", lambda e, rrow2=rrow2, brow=brow: e.activation(
                    brow, rrow2, AF.Copy), inc="s")
                row_s = S.m("s")
                bb = 6 + h

                def fbc(e, brow=brow, pb=pb, bb=bb, h=h):
                    ones = PARB[pb:pb + 1, PB_ONESROW:PB_ONESROW + 64]
                    if h == 0:
                        return e.matmul(PS[0:64, bb * 512:bb * 512 + CH],
                                        ones, brow, start=True, stop=True)
                    return e.matmul(PS[64:128, bb * 512:bb * 512 + CH],
                                    ones, brow, start=True, stop=True,
                                    tile_position=(0, 64))
                S.op("tensor", fbc, waits=[("s", row_s), cx["bankfree"][bb]],
                     inc="t")
                bc_t = S.m("t")
                fset(cx, f"DROW{h}", "t", bc_t)
                obuf = ptbuf[0 + 64 * h:64 + 64 * h, 1024:1024 + CH]
                S.op("scalar", lambda e, avbank=avbank, obuf=obuf: e.activation(
                    obuf, avbank, AF.Copy),
                    waits=[("t", av_t), fget(cx, f"OB{h}")], inc="s")
                ob_s = S.m("s")
                dst = rwb3[64 * h:64 * h + 64, pr, :]
                bvap = PAR[64 * h:64 * h + 64,
                           pcol("bv") + pr:pcol("bv") + pr + 1]
                bcread = (PS[0:64, bb * 512:bb * 512 + CH] if h == 0 else
                          PS[64:128, bb * 512:bb * 512 + CH])
                S.op("vector", lambda e, obuf=obuf, dst=dst, bcread=bcread:
                     e.tensor_tensor(dst, obuf, bcread, ALU.mult),
                     waits=[("s", ob_s), ("t", bc_t), fget(cx, "RWB")],
                     inc="v")
                S.op("vector", lambda e, dst=dst, bvap=bvap: e.tensor_scalar(
                    dst, dst, bvap, None, ALU.add), inc="v")
                fin_v = S.m("v")
                fset(cx, f"OB{h}", "v", fin_v)
                cx["bankfree"][bb] = ("v", fin_v)
                cx["bankfree"][4 + h] = ("v", fin_v)
        ofin_v = S.m("v")
        fset(cx, "RWB", "v", ofin_v)
        fset(cx, "MXA0", "t", S.m("t"))

        xr = r3(XRES[:], NCT)[:, :, c * CH:(c + 1) * CH]

        def ev_o(r, glo, ghi, blo, mm_t, xr=xr):
            def fn(e, glo=glo, ghi=ghi, blo=blo, xr=xr):
                inst = None
                for g in range(glo, ghi):
                    bank = PS[:, (blo + g - glo) * 512:
                              (blo + g - glo) * 512 + CH]
                    inst = e.scalar_tensor_tensor(
                        xr[:, g], bank, PAR[:, pcol("bo") + g:pcol("bo") + g + 1],
                        xr[:, g], ALU.add, ALU.add)
                return inst
            S.op("vector", fn, waits=[("t", mm_t)] + cx["xres_ready"], inc="v")
            for i in range(ghi - glo):
                cx["bankfree"][blo + i] = ("v", S.m("v"))
        emit_proj(S, cx, 3, rwb3, ("v", ofin_v), ev_o)
        oadd_v = S.m("v")
        fset(cx, "RWB", "t", S.m("t"))
        cx["xres_ready"] = [("v", oadd_v)]

    end_t = S.m("t")
    for nm in ("B0", "B1", "B2", "B3", "XHAT0"):
        fset(cx, nm, "t", end_t)
    fset(cx, "SQB", "s", S.m("s"))


def _prep_inputs(inputs):
    def tr(a):
        return np.ascontiguousarray(np.swapaxes(np.asarray(a), -1, -2))

    g = {k: np.asarray(v) for k, v in inputs.items()}
    com = {}
    com["tmkT"] = tr(g["L0_attn_a_tm_k_w"])
    com["tmvT"] = tr(g["L0_attn_a_tm_v_w"])
    com["tmrT"] = tr(g["L0_attn_a_tm_r_w"])
    com["tmoT"] = tr(g["L0_attn_a_tm_o_w"])
    com["cmkT"] = tr(g["L0_attn_a_cm_k_w"])
    com["cmvT"] = tr(g["L0_attn_a_cm_v_w"])
    com["cmrT"] = tr(g["L0_attn_a_cm_r_w"])
    wqkv = g["L0_attn_b_W_qkv"]
    com["wqT"] = tr(wqkv[0:1024])
    com["wkT"] = tr(wqkv[1024:2048])
    com["wvT"] = tr(wqkv[2048:3072])
    com["woT"] = tr(g["L0_attn_b_W_o"])
    com["frT"] = tr(g["L0_ffn_W_r"])
    com["fkT"] = tr(g["L0_ffn_W_k"])
    com["fvT"] = tr(g["L0_ffn_W_v"])

    def col8(v):
        return np.ascontiguousarray(np.asarray(v).reshape(8, 128).T)

    cols = {}
    for l in range(NL):
        cols[f"raw_ln1w{l}"] = col8(g["L0_attn_a_ln1_w"][l])
        cols[f"raw_ln2w{l}"] = col8(g["L0_attn_a_ln2_w"][l])
        cols[f"raw_mk{l}"] = col8(g["L0_attn_a_tm_mix_k"][l, 0, 0])
        cols[f"raw_mv{l}"] = col8(g["L0_attn_a_tm_mix_v"][l, 0, 0])
        cols[f"raw_mr{l}"] = col8(g["L0_attn_a_tm_mix_r"][l, 0, 0])
        cols[f"raw_cmk{l}"] = col8(g["L0_attn_a_cm_mix_k"][l, 0, 0])
        cols[f"raw_cmr{l}"] = col8(g["L0_attn_a_cm_mix_r"][l, 0, 0])
        cols[f"raw_td{l}"] = col8(g["L0_attn_a_time_decay"][l])
        cols[f"raw_tf{l}"] = col8(g["L0_attn_a_time_first"][l])
    cols["raw_mur"] = col8(g["L0_ffn_mu_r"])
    cols["raw_muk"] = col8(g["L0_ffn_mu_k"])
    cols["lnfw"] = col8(g["L0_attn_a_lnf_w"])
    bqkv = g["L0_attn_b_b_qkv"]
    cols["bq"] = col8(bqkv[0:1024])
    cols["bk"] = col8(bqkv[1024:2048])
    cols["bv"] = col8(bqkv[2048:3072])
    cols["bo"] = col8(g["L0_attn_b_b_o"])
    vec = np.concatenate([cols[n] for n in VEC_ORDER], axis=1).astype(np.float32)
    com["vecs"] = np.ascontiguousarray(vec)

    x = g["L0_attn_a_x"]
    per_core = []
    for b in range(8):
        m = dict(com)
        m["xt"] = np.ascontiguousarray(x[b].T.astype(np.float32))
        per_core.append(m)
    return per_core


def kernel(**inputs):
    in_maps = _prep_inputs(inputs)
    nc = build_program()
    res = run_bass_kernel_spmd(nc, in_maps, core_ids=list(range(8)))
    outs = [np.ascontiguousarray(np.asarray(r["out"]).T) for r in res.results]
    return np.stack(outs, axis=0).astype(np.float32)


# version 07
# speedup vs baseline: 1.1863x; 1.0037x over previous
import numpy as np
import concourse.bass as bass
import concourse.mybir as mybir
from concourse.bass_utils import run_bass_kernel_spmd

F32 = mybir.dt.float32
BF16 = mybir.dt.bfloat16
AF = mybir.ActivationFunctionType
ALU = mybir.AluOpType

C = 1024
NCT = 8
T = 4096
HALF = 2048
CH = 256
NCH = 8
FFND = 4096
NFT = 32
NL = 4
HEADS = 16
DH = 64
EPS = 1e-5
INV_C = 1.0 / C


SERIALIZE = False
SER_SAME_ENGINES = ("vector", "scalar", "gpsimd")
DETECT_RACES = False
FFN_TRUNC = None

ENG_SEM = {"tensor": "t", "vector": "v", "scalar": "s", "gpsimd": "g",
           "sync": "sd"}


class Sched:
    ENGINES = ("tensor", "vector", "scalar", "gpsimd", "sync")

    def __init__(self, nc):
        self.nc = nc
        self.ops = {e: [] for e in self.ENGINES}
        self.sem = {}
        self.count = {}
        self.seen = {e: {} for e in self.ENGINES}

    def add_sem(self, name, handle):
        self.sem[name] = handle
        self.count[name] = 0
        for e in self.ENGINES:
            self.seen[e][name] = 0

    def op(self, eng, fn, waits=(), inc=None, inc_n=1):
        for s, v in waits:
            assert v <= self.count[s], f"wait {s}>={v} but only {self.count[s]} emitted"
        if SERIALIZE:
            waits = list(waits) + [(s, self.count[s]) for s in self.count
                                   if s != ENG_SEM[eng]]
            if inc is None:
                inc = ENG_SEM[eng]
                inc_n = 16 if inc in ("sd", "gd") else 1
        elif eng in SER_SAME_ENGINES:
            own = ENG_SEM[eng]
            waits = list(waits) + [(own, self.count[own])]
            if inc is None:
                inc = own
                inc_n = 16 if inc in ("sd", "gd") else 1
        if inc is not None:
            self.count[inc] += inc_n
        sems = self.sem
        seen = self.seen[eng]
        eff = {}
        for s, v in waits:
            if v > seen[s] and v > eff.get(s, 0):
                eff[s] = v
        for s, v in eff.items():
            seen[s] = v
        waits = tuple(eff.items())

        def closure(e, fn=fn, waits=waits, inc=inc, inc_n=inc_n):
            for s, v in waits:
                e.wait_ge(sems[s], v)
            inst = fn(e)
            if inc is not None and inst is not None:
                inst.then_inc(sems[inc], inc_n)
        self.ops[eng].append(closure)
        return self.count[inc] if inc is not None else None

    def m(self, sem):
        return self.count[sem]


def r3(ap, nb):
    return ap.rearrange("p (b w) -> p b w", b=nb)


def _par_map():
    m = {}
    col = 0

    def add(name, n=8):
        nonlocal col
        m[name] = col
        col += n
    for l in range(NL):
        for nm in ("Ak", "Bk", "Av", "Bv", "Ar", "Br",
                   "Ack", "Bck", "Acr", "Bcr", "adec", "Eu"):
            add(f"{nm}{l}")
    for nm in ("fAr", "fBr", "fAk", "fBk", "lnfw", "bq", "bk", "bo", "bv"):
        add(nm)
    add("G", NL * 8)
    add("H", NL * 8)
    add("halo", 10 * 8)
    add("epsc", 8)
    add("onesf", 128)
    assert col <= 768, col
    return m, col


PMAP, PAR_USED = _par_map()

VEC_ORDER = ([f"raw_{nm}{l}" for l in range(NL)
              for nm in ("ln1w", "ln2w", "mk", "mv", "mr", "cmk", "cmr", "td", "tf")]
             + ["raw_mur", "raw_muk", "lnfw", "bq", "bk", "bo", "bv"])
VEC_COLS = len(VEC_ORDER) * 8

PB_RSTD = 0
PB_MU = 2 * CH
PB_ONESROW = 4 * CH
PB_ONESCOL = 4 * CH + 128
PARB_COLS = 4 * CH + 132


def pcol(name, l=None):
    return PMAP[name if l is None else f"{name}{l}"]


def build_program(debug_stage=None):
    nc = bass.Bass("TRN2", target_bir_lowering=False,
                   detect_race_conditions=DETECT_RACES)

    xt = nc.dram_tensor("xt", [C, T], F32, kind="ExternalInput").ap()
    out = nc.dram_tensor("out", [C, T], F32, kind="ExternalOutput").ap()
    tmkT = nc.dram_tensor("tmkT", [NL, C, C], F32, kind="ExternalInput").ap()
    tmvT = nc.dram_tensor("tmvT", [NL, C, C], F32, kind="ExternalInput").ap()
    tmrT = nc.dram_tensor("tmrT", [NL, C, C], F32, kind="ExternalInput").ap()
    tmoT = nc.dram_tensor("tmoT", [NL, C, C], F32, kind="ExternalInput").ap()
    cmkT = nc.dram_tensor("cmkT", [NL, C, FFND], F32, kind="ExternalInput").ap()
    cmvT = nc.dram_tensor("cmvT", [NL, FFND, C], F32, kind="ExternalInput").ap()
    cmrT = nc.dram_tensor("cmrT", [NL, C, C], F32, kind="ExternalInput").ap()
    wqT = nc.dram_tensor("wqT", [C, C], F32, kind="ExternalInput").ap()
    wkT = nc.dram_tensor("wkT", [C, C], F32, kind="ExternalInput").ap()
    wvT = nc.dram_tensor("wvT", [C, C], F32, kind="ExternalInput").ap()
    woT = nc.dram_tensor("woT", [C, C], F32, kind="ExternalInput").ap()
    frT = nc.dram_tensor("frT", [C, C], F32, kind="ExternalInput").ap()
    fkT = nc.dram_tensor("fkT", [C, C], F32, kind="ExternalInput").ap()
    fvT = nc.dram_tensor("fvT", [C, C], F32, kind="ExternalInput").ap()
    vecs = nc.dram_tensor("vecs", [128, VEC_COLS], F32, kind="ExternalInput").ap()
    cmk_s = nc.dram_tensor("cmk_s", [NL, C, FFND], BF16, kind="Internal").ap()
    cmv_s = nc.dram_tensor("cmv_s", [NL, FFND, C], BF16, kind="Internal").ap()
    ksp = nc.dram_tensor("ksp", [C, T], BF16, kind="Internal").ap()
    vtsp = nc.dram_tensor("vtsp", [128, 8 * 32 * 130], BF16, kind="Internal").ap()
    qsp = nc.dram_tensor("qsp", [C, T], BF16, kind="Internal").ap()

    import contextlib
    with contextlib.ExitStack() as stack:
        ent = stack.enter_context
        XRES = ent(nc.sbuf_tensor("XRES", [128, NCT * HALF], F32))
        ARENA = ent(nc.sbuf_tensor("ARENA", [128, 4 * NCT * 1024], BF16))
        B0 = ent(nc.sbuf_tensor("B0", [128, NCT * CH], F32))
        B1 = ent(nc.sbuf_tensor("B1", [128, NCT * CH], F32))
        B2 = ent(nc.sbuf_tensor("B2", [128, 2080], F32))
        B3 = ent(nc.sbuf_tensor("B3", [128, 2080], F32))
        XHATp = [ent(nc.sbuf_tensor(f"XHAT{i}", [128, NCT * (CH + 1)], BF16))
                 for i in range(2)]
        RWB = ent(nc.sbuf_tensor("RWB", [128, NCT * CH], BF16))
        SQB = ent(nc.sbuf_tensor("SQB", [128, NCT * CH], BF16))
        MXAp = [ent(nc.sbuf_tensor(f"MXA{i}", [128, NCT * CH], BF16))
                for i in range(2)]
        MXBp = [ent(nc.sbuf_tensor(f"MXB{i}", [128, NCT * CH], BF16))
                for i in range(2)]
        MXCp = [ent(nc.sbuf_tensor(f"MXC{i}", [128, NCT * CH], BF16))
                for i in range(2)]
        BCB = ent(nc.sbuf_tensor("BCB", [128, 2 * CH], BF16))
        PAR = ent(nc.sbuf_tensor("PAR", [128, PAR_USED], F32))
        PARB = ent(nc.sbuf_tensor("PARB", [128, PARB_COLS], BF16))
        PS = ent(nc.psum_tensor("PS", [128, 4096], F32))
        sem_t = ent(nc.semaphore("sem_t"))
        sem_v = ent(nc.semaphore("sem_v"))
        sem_s = ent(nc.semaphore("sem_s"))
        sem_g = ent(nc.semaphore("sem_g"))
        sem_gd = ent(nc.semaphore("sem_gd"))
        sem_sd = ent(nc.semaphore("sem_sd"))
        sem_ad = ent(nc.semaphore("sem_ad"))
        blk = ent(nc.Block())

        S = Sched(nc)
        for nm, h in [("t", sem_t), ("v", sem_v), ("s", sem_s),
                      ("g", sem_g), ("gd", sem_gd), ("sd", sem_sd),
                      ("ad", sem_ad)]:
            S.add_sem(nm, h)

        cx = dict(
            xt=xt, out=out, tmkT=tmkT, tmvT=tmvT, tmrT=tmrT, tmoT=tmoT,
            cmkT=cmkT, cmvT=cmvT, cmrT=cmrT, wqT=wqT, wkT=wkT, wvT=wvT,
            woT=woT, frT=frT, fkT=fkT, fvT=fvT, vecs=vecs,
            cmk_s=cmk_s, cmv_s=cmv_s, ksp=ksp, vtsp=vtsp, qsp=qsp,
            XRES=XRES, ARENA=ARENA, B0=B0, B1=B1, B2=B2, B3=B3,
            XHATp=XHATp, RWB=RWB, SQB=SQB, MXAp=MXAp, MXBp=MXBp,
            MXCp=MXCp, BCB=BCB, PAR=PAR, PARB=PARB, PS=PS,
            free={},
            bankfree=[("v", 0)] * 8,
        )
        emit_all(S, cx, debug_stage)

        @blk.tensor
        def _(e):
            for f in S.ops["tensor"]:
                f(e)

        @blk.vector
        def _(e):
            for f in S.ops["vector"]:
                f(e)

        @blk.scalar
        def _(e):
            for f in S.ops["scalar"]:
                f(e)

        @blk.gpsimd
        def _(e):
            for f in S.ops["gpsimd"]:
                f(e)

        @blk.sync
        def _(e):
            for f in S.ops["sync"]:
                f(e)

    return nc



def fget(cx, name):
    return cx["free"].get(name, ("v", 0))


def fset(cx, name, sem, val):
    cx["free"][name] = (sem, val)


def emit_all(S, cx, debug_stage):
    prologue(S, cx)
    passes = (0,)
    if debug_stage and debug_stage.startswith("1"):
        debug_stage = debug_stage[1:]
    else:
        passes = (0, 1)
    for p in passes:
        pass_begin(S, cx, p)
        src0 = list(cx["xres_ready"])
        for c in range(NCH):
            done_v = emit_ln_fp32_inplace(S, cx, c, src0)
        cx["xres_ready"] = [("v", done_v)]
        if debug_stage == "h0":
            pass_store(S, cx, p)
            continue
        stop = emit_rwkv_stack(S, cx, p, debug_stage)
        if stop:
            pass_store(S, cx, p)
            continue
        emit_lnf_add(S, cx, p)
        if debug_stage == "x1":
            pass_store(S, cx, p)
            continue
        emit_ffn(S, cx, p, site=8)
        if debug_stage == "x2":
            pass_store(S, cx, p)
            continue
        emit_attn(S, cx, p)
        if debug_stage == "x3":
            pass_store(S, cx, p)
            continue
        emit_ffn(S, cx, p, site=9)
        pass_store(S, cx, p)

    S.op("sync", lambda e: None, waits=[("sd", S.m("sd"))])


def prologue(S, cx):
    PAR, PARB, B0 = cx["PAR"], cx["PARB"], cx["B0"]

    S.op("sync", lambda e: e.dma_start(B0[:, 0:VEC_COLS], cx["vecs"]),
         inc="sd", inc_n=16)
    lded = S.m("sd")

    def rawc(name, l=None):
        i = VEC_ORDER.index(name if l is None else f"{name}{l}")
        return B0[:, i * 8:(i + 1) * 8]

    def c(name, l=None):
        col = pcol(name, l)
        return PAR[:, col:col + 8]

    for pp in (0, 32, 64):
        S.op("vector", lambda e, pp=pp: e.memset(
            PARB[pp:pp + 1, PB_ONESROW:PB_ONESROW + 128], 1.0), inc="v")
    S.op("vector", lambda e: e.memset(PARB[:, PB_ONESCOL:PB_ONESCOL + 1], 1.0),
         inc="v")
    of = PMAP["onesf"]
    S.op("vector", lambda e: e.memset(PAR[0:1, of:of + 128], 1.0), inc="v")
    S.op("vector", lambda e: e.memset(PAR[32:33, of:of + 128], 1.0), inc="v")
    ec = PMAP["epsc"]
    S.op("vector", lambda e: e.memset(PAR[:, ec:ec + 1], EPS), inc="v")
    S.op("vector", lambda e: e.memset(PAR[:, PMAP["G"]:PMAP["G"] + NL * 8], 0.0),
         inc="v")
    S.op("vector", lambda e: e.memset(PAR[:, PMAP["H"]:PMAP["H"] + NL * 8], 0.0),
         inc="v")
    S.op("vector", lambda e: e.memset(PAR[:, PMAP["halo"]:PMAP["halo"] + 80], 0.0),
         inc="v")

    for l in range(NL):
        for (a, b, mix, w) in [("Ak", "Bk", "raw_mk", "raw_ln1w"),
                               ("Av", "Bv", "raw_mv", "raw_ln1w"),
                               ("Ar", "Br", "raw_mr", "raw_ln1w"),
                               ("Ack", "Bck", "raw_cmk", "raw_ln2w"),
                               ("Acr", "Bcr", "raw_cmr", "raw_ln2w")]:
            S.op("vector", lambda e, a=a, mix=mix, w=w, l=l: e.tensor_tensor(
                c(a, l), rawc(mix, l), rawc(w, l), ALU.mult),
                waits=[("sd", lded)], inc="v")
            S.op("vector", lambda e, a=a, b=b, w=w, l=l: e.tensor_tensor(
                c(b, l), rawc(w, l), c(a, l), ALU.subtract), inc="v")
        S.op("scalar", lambda e, l=l: e.activation(
            c("adec", l), rawc("raw_td", l), AF.Exp), waits=[("sd", lded)],
            inc="s")
        S.op("scalar", lambda e, l=l: e.activation(
            c("adec", l), c("adec", l), AF.Exp, scale=-1.0), inc="s")
        S.op("scalar", lambda e, l=l: e.activation(
            c("Eu", l), rawc("raw_tf", l), AF.Exp, scale=-1.0), inc="s")
    for (a, b, raw) in [("fAr", "fBr", "raw_mur"), ("fAk", "fBk", "raw_muk")]:
        S.op("vector", lambda e, a=a, raw=raw: e.tensor_copy(c(a), rawc(raw)),
             waits=[("sd", lded)], inc="v")
        S.op("vector", lambda e, b=b, raw=raw: e.tensor_scalar(
            c(b), rawc(raw), -1.0, 1.0, ALU.mult, ALU.add), inc="v")
    for nm in ("lnfw", "bq", "bk", "bo", "bv"):
        S.op("vector", lambda e, nm=nm: e.tensor_copy(c(nm), rawc(nm)),
             waits=[("sd", lded)], inc="v")
    fset(cx, "B0", "v", S.m("v"))
    cx["params_v"] = S.m("v")
    cx["params_s"] = S.m("s")

    cx["cmprep"] = {}
    for l in range(NL):
        S.op("gpsimd", lambda e, l=l: e.dma_start(cx["cmk_s"][l], cx["cmkT"][l]),
             inc="gd", inc_n=16)
        S.op("gpsimd", lambda e, l=l: e.dma_start(cx["cmv_s"][l], cx["cmvT"][l]),
             inc="gd", inc_n=16)
        cx["cmprep"][l] = S.m("gd")


def pass_begin(S, cx, p):
    XRES = cx["XRES"]
    cx["pass"] = p
    src = cx["xt"].rearrange("(ct p) t -> p ct t", p=128)[:, :, p * HALF:(p + 1) * HALF]
    waits = [fget(cx, "XRES")]
    S.op("sync", lambda e: e.dma_start(r3(XRES[:], NCT), src),
         waits=waits, inc="sd", inc_n=16)
    cx["xres_ready"] = [("sd", S.m("sd"))]


def pass_store(S, cx, p):
    XRES = cx["XRES"]
    dst = cx["out"].rearrange("(ct p) t -> p ct t", p=128)[:, :, p * HALF:(p + 1) * HALF]
    S.op("sync", lambda e: e.dma_start(dst, r3(XRES[:], NCT)),
         waits=list(cx["xres_ready"]), inc="sd", inc_n=16)
    fset(cx, "XRES", "sd", S.m("sd"))


def emit_stats_pre(S, cx, src3, src_waits, par):
    SQB = cx["SQB"]
    XHAT = cx["XHATp"][par]
    xhv = r3(XHAT[:], NCT)[:, :, 1:1 + CH]
    S.op("scalar", lambda e: e.activation(xhv, src3, AF.Copy),
         waits=list(src_waits) + [fget(cx, f"XHAT{par}")], inc="s")
    cast_s = S.m("s")
    S.op("vector", lambda e: e.tensor_tensor(r3(SQB[:], NCT), xhv, xhv,
                                             ALU.mult),
         waits=[("s", cast_s), fget(cx, "SQB")], inc="v")
    return dict(cast_s=cast_s, sq_v=S.m("v"))


def emit_stats(S, cx, src3, src_waits, pbank, pre=None):
    PS, SQB, PARB = cx["PS"], cx["SQB"], cx["PARB"]
    par = pbank - 6
    XHAT = cx["XHATp"][par]
    SQF = SQB[:].bitcast(F32)
    bank = PS[:, pbank * 512:(pbank + 1) * 512]

    if pre is None:
        pre = emit_stats_pre(S, cx, src3, src_waits, par)
    cast_s, sq_v = pre["cast_s"], pre["sq_v"]

    onescol = PARB[:, PB_ONESCOL:PB_ONESCOL + 1]

    def fsum(t):
        inst = None
        for ct in range(NCT):
            inst = t.matmul(bank[0:1, 0:CH], onescol,
                            r3(XHAT[:], NCT)[:, ct, 1:1 + CH],
                            start=(ct == 0), stop=(ct == NCT - 1))
        for ct in range(NCT):
            inst = t.matmul(bank[0:1, CH:2 * CH], onescol,
                            r3(SQB[:], NCT)[:, ct],
                            start=(ct == 0), stop=(ct == NCT - 1))
        return inst
    S.op("tensor", fsum, waits=[("s", cast_s), ("v", sq_v),
                                cx["bankfree"][pbank],
                                ("v", cx["params_v"]), ("s", cx["params_s"])],
         inc="t")
    stats_t = S.m("t")
    fset(cx, f"XHAT{par}", "t", stats_t)
    fset(cx, "SQB", "t", stats_t)

    mu_f = SQF[0:1, 0:CH]
    var_f = SQF[0:1, CH:2 * CH]
    rstd_f = SQF[0:1, 2 * CH:3 * CH]
    scr_f = SQF[0:1, 3 * CH:4 * CH]
    S.op("vector", lambda e: e.tensor_scalar(mu_f, bank[0:1, 0:CH], INV_C,
                                             None, ALU.mult),
         waits=[("t", stats_t)], inc="v")
    S.op("vector", lambda e: e.tensor_scalar(var_f, bank[0:1, CH:2 * CH],
                                             INV_C, None, ALU.mult), inc="v")
    S.op("vector", lambda e: e.tensor_tensor(scr_f, mu_f, mu_f, ALU.mult),
         inc="v")
    S.op("vector", lambda e: e.tensor_tensor(var_f, var_f, scr_f,
                                             ALU.subtract), inc="v")
    fin_v = S.m("v")

    epsap = cx["PAR"][0:1, PMAP["epsc"]:PMAP["epsc"] + 1]
    S.op("scalar", lambda e: e.activation(scr_f, var_f, AF.Ln, bias=epsap),
         waits=[("v", fin_v)], inc="s")
    S.op("scalar", lambda e: e.activation(rstd_f, scr_f, AF.Exp, scale=-0.5),
         inc="s")
    S.op("scalar", lambda e: e.activation(PARB[0:1, PB_RSTD + par * CH:
                                               PB_RSTD + par * CH + CH],
                                          rstd_f, AF.Copy), inc="s")
    S.op("scalar", lambda e: e.activation(PARB[0:1, PB_MU + par * CH:
                                               PB_MU + par * CH + CH],
                                          mu_f, AF.Copy), inc="s")
    rows_s = S.m("s")
    fset(cx, "SQB", "s", rows_s)
    return dict(rows_s=rows_s, fin_v=fin_v, pbank=pbank, par=par,
                mu_f=mu_f, rstd_f=rstd_f)


def emit_bcast(S, cx, st, fp32=False):
    PS, PARB, PAR = cx["PS"], cx["PARB"], cx["PAR"]
    par = st["par"]
    bank = PS[:, st["pbank"] * 512:(st["pbank"] + 1) * 512]
    if fp32:
        lhs_r = PAR[0:1, PMAP["onesf"]:PMAP["onesf"] + 128]
        lhs_m = lhs_r
        rstd_src = st["rstd_f"]
        mu_src = st["mu_f"]
        w = [("s", st["rows_s"]), ("v", st["fin_v"])]
    else:
        lhs_r = PARB[0:1, PB_ONESROW:PB_ONESROW + 128]
        lhs_m = lhs_r
        rstd_src = PARB[0:1, PB_RSTD + par * CH:PB_RSTD + par * CH + CH]
        mu_src = PARB[0:1, PB_MU + par * CH:PB_MU + par * CH + CH]
        w = [("s", st["rows_s"])]

    def fbc(e):
        e.matmul(bank[:, 0:CH], lhs_r, rstd_src, start=True, stop=True)
        return e.matmul(bank[:, CH:2 * CH], lhs_m, mu_src, start=True,
                        stop=True)
    S.op("tensor", fbc, waits=w, inc="t")
    return S.m("t")


def emit_ln_fp32_inplace(S, cx, c, src_waits):
    XRES, PS = cx["XRES"], cx["PS"]
    par = c % 2
    pbank = 6 + par
    tmp = [cx["B0"], cx["B1"]][par]
    tname = ["B0", "B1"][par]
    xr = r3(XRES[:], NCT)[:, :, c * CH:(c + 1) * CH]
    st = emit_stats(S, cx, xr, src_waits, pbank)
    bc_t = emit_bcast(S, cx, st, fp32=True)
    fset(cx, "SQB", "t", bc_t)
    bank = PS[:, pbank * 512:(pbank + 1) * 512]
    mu_bc = bank[:, CH:2 * CH].unsqueeze(1).broadcast_to([128, NCT, CH])
    rstd_bc = bank[:, 0:CH].unsqueeze(1).broadcast_to([128, NCT, CH])
    b0r = r3(tmp[:, 0:NCT * CH], NCT)
    S.op("vector", lambda e: e.tensor_tensor(b0r, xr, mu_bc, ALU.subtract),
         waits=[("t", bc_t), fget(cx, tname)], inc="v")
    S.op("vector", lambda e: e.tensor_tensor(xr, b0r, rstd_bc, ALU.mult),
         inc="v")
    done_v = S.m("v")
    cx["bankfree"][pbank] = ("v", done_v)
    fset(cx, tname, "v", done_v)
    return done_v


def emit_ln_prep_pre(S, cx, c):
    xr = r3(cx["XRES"][:], NCT)[:, :, c * CH:(c + 1) * CH]
    return emit_stats_pre(S, cx, xr, cx["xres_ready"], c % 2)


def emit_ln_prep_stats(S, cx, c, site, pre=None):
    XRES, PS = cx["XRES"], cx["PS"]
    par = c % 2
    pbank = 6 + par
    XHAT = cx["XHATp"][par]
    BCB = cx["BCB"]
    xr = r3(XRES[:], NCT)[:, :, c * CH:(c + 1) * CH]
    st = emit_stats(S, cx, xr, cx["xres_ready"], pbank, pre=pre)
    bc_t = emit_bcast(S, cx, st, fp32=False)
    bcsrc = r3(PS[:, pbank * 512:(pbank + 1) * 512][:, 0:2 * CH], 2)
    S.op("scalar", lambda e: e.activation(r3(BCB[:], 2), bcsrc, AF.Copy),
         waits=[("t", bc_t), fget(cx, "BCB")], inc="s")
    bcb_s = S.m("s")
    cx["bankfree"][pbank] = ("s", bcb_s)
    return dict(par=par, site=site, bcb_s=bcb_s, xh=r3(XHAT[:], NCT), mix={})


def emit_ln_prep_finish(S, cx, rec, mixes):
    PAR = cx["PAR"]
    par, site, xh = rec["par"], rec["site"], rec["xh"]
    BCB = cx["BCB"]
    xh_cur = xh[:, :, 1:1 + CH]
    mu_bc = BCB[:, CH:2 * CH].unsqueeze(1).broadcast_to([128, NCT, CH])
    rstd_bc = BCB[:, 0:CH].unsqueeze(1).broadcast_to([128, NCT, CH])
    if site is not None:
        hc = PMAP["halo"] + site * 8
        S.op("vector", lambda e: e.tensor_copy(
            xh[:, :, 0:1], PAR[:, hc:hc + 8].bitcast(BF16)[:, 0:NCT]
            .rearrange("p (b w) -> p b w", b=NCT)), inc="v")
    S.op("vector", lambda e: e.tensor_tensor(xh_cur, xh_cur, mu_bc,
                                             ALU.subtract),
         waits=[("s", rec["bcb_s"])], inc="v")
    S.op("vector", lambda e: e.tensor_tensor(xh_cur, xh_cur, rstd_bc,
                                             ALU.mult), inc="v")
    xhat_v = S.m("v")
    fset(cx, "BCB", "v", xhat_v)
    if site is not None:
        hc = PMAP["halo"] + site * 8
        S.op("vector", lambda e: e.tensor_copy(
            PAR[:, hc:hc + 8].bitcast(BF16)[:, 0:NCT]
            .rearrange("p (b w) -> p b w", b=NCT), xh[:, :, CH:CH + 1]),
            inc="v")
    rec["xhat_v"] = xhat_v
    slot = {"A": "MXAp", "B": "MXBp", "C": "MXCp"}
    for (Aname, Bname, l, which) in mixes:
        dst = r3(cx[slot[which]][par][:], NCT)
        mv = emit_mix(S, cx, Aname, Bname, l, dst, f"MX{which}{par}", xhat_v,
                      xh)
        rec["mix"][which] = (dst, mv)
    return rec


def emit_ln_prep(S, cx, c, site, mixes):
    return emit_ln_prep_finish(S, cx, emit_ln_prep_stats(S, cx, c, site),
                               mixes)


def emit_mix(S, cx, Aname, Bname, l, dst, dst_name, xhat_v, xh):
    PAR = cx["PAR"]
    Ac = pcol(Aname, l) if l is not None else pcol(Aname)
    Bc = pcol(Bname, l) if l is not None else pcol(Bname)
    A_bc = r3(PAR[:, Ac:Ac + 8], NCT).broadcast_to([128, NCT, CH])
    S.op("vector", lambda e: e.tensor_tensor(dst, xh[:, :, 1:1 + CH], A_bc,
                                             ALU.mult),
         waits=[fget(cx, dst_name), ("v", xhat_v)], inc="v")

    def fb(e):
        inst = None
        for ct in range(NCT):
            inst = e.scalar_tensor_tensor(
                dst[:, ct], xh[:, ct, 0:CH], PAR[:, Bc + ct:Bc + ct + 1],
                dst[:, ct], ALU.mult, ALU.add)
        return inst
    S.op("vector", fb, inc="v")
    return S.m("v")


def emit_lnf_add(S, cx, p):
    XRES, PAR, PS = cx["XRES"], cx["PAR"], cx["PS"]
    lnfw = pcol("lnfw")
    src0 = list(cx["xres_ready"])
    for c in range(NCH):
        par = c % 2
        pbank = 6 + par
        tmp = [cx["B0"], cx["B1"]][par]
        tname = ["B0", "B1"][par]
        xbuf = [cx["B3"], cx["B2"]][par]
        xname = ["B3", "B2"][par]
        xr = r3(XRES[:], NCT)[:, :, c * CH:(c + 1) * CH]
        st = emit_stats(S, cx, xr, src0, pbank)
        bc_t = emit_bcast(S, cx, st, fp32=True)
        fset(cx, "SQB", "t", bc_t)
        bank = PS[:, pbank * 512:(pbank + 1) * 512]
        mu_bc = bank[:, CH:2 * CH].unsqueeze(1).broadcast_to([128, NCT, CH])
        rstd_bc = bank[:, 0:CH].unsqueeze(1).broadcast_to([128, NCT, CH])
        src = cx["xt"].rearrange("(ct p) t -> p ct t", p=128)[
            :, :, p * HALF + c * CH: p * HALF + (c + 1) * CH]
        b3r = r3(xbuf[:, 0:NCT * CH], NCT)
        S.op("sync", lambda e, src=src, b3r=b3r: e.dma_start(b3r, src),
             waits=[fget(cx, xname)], inc="sd", inc_n=16)
        xin_d = S.m("sd")
        b0r = r3(tmp[:, 0:NCT * CH], NCT)
        S.op("vector", lambda e, b0r=b0r, xr=xr, mu_bc=mu_bc: e.tensor_tensor(
            b0r, xr, mu_bc, ALU.subtract),
            waits=[("t", bc_t), fget(cx, tname)], inc="v")
        S.op("vector", lambda e, b0r=b0r, rstd_bc=rstd_bc: e.tensor_tensor(
            b0r, b0r, rstd_bc, ALU.mult), inc="v")
        bcd = S.m("v")
        cx["bankfree"][pbank] = ("v", bcd)

        def fn(e, b0r=b0r, xr=xr, b3r=b3r):
            inst = None
            for ct in range(NCT):
                inst = e.scalar_tensor_tensor(
                    xr[:, ct], b0r[:, ct], PAR[:, lnfw + ct:lnfw + ct + 1],
                    b3r[:, ct], ALU.mult, ALU.add)
            return inst
        S.op("vector", fn, waits=[("sd", xin_d)], inc="v")
        done_v = S.m("v")
        fset(cx, tname, "v", done_v)
        fset(cx, xname, "v", done_v)
    cx["xres_ready"] = [("v", done_v)]


PROJ_ROUNDS = [(0, 3, 0), (3, 6, 3), (6, 8, 0)]


def ps_banks(cx, blo, n):
    return r3(cx["PS"][:, blo * 512:(blo + n) * 512], n)[:, :, 0:CH]


def emit_proj(S, cx, warena_m, rhs3, rhs_wait, evac, wa3=None):
    ARENA, PS = cx["ARENA"], cx["PS"]
    if wa3 is None:
        wa3 = r3(ARENA[:], 4)[:, warena_m].rearrange("p (ct o) -> p ct o",
                                                     ct=NCT)
    for r, (glo, ghi, blo) in enumerate(PROJ_ROUNDS):
        def fn(t, glo=glo, ghi=ghi, blo=blo):
            inst = None
            for g in range(glo, ghi):
                bank = PS[:, (blo + g - glo) * 512:(blo + g - glo) * 512 + CH]
                for ct in range(NCT):
                    inst = t.matmul(bank, wa3[:, ct, g * 128:(g + 1) * 128],
                                    rhs3[:, ct], start=(ct == 0),
                                    stop=(ct == NCT - 1))
            return inst
        waits = [rhs_wait, cx["arena_ready"]] + [cx["bankfree"][blo + i]
                                                 for i in range(ghi - glo)]
        S.op("tensor", fn, waits=waits, inc="t")
        evac(r, glo, ghi, blo, S.m("t"))


def _dmp(S, cx, region, src, waits, cast=False, nct=NCT):
    dst = cx["out"].rearrange("(ct p) t -> p ct t", p=128)[
        :, 0:nct, region * CH:(region + 1) * CH]
    eng, inc = ("gpsimd", "gd") if cast else ("sync", "sd")
    S.op(eng, lambda e, dst=dst, src=src: e.dma_start(dst, src),
         waits=waits, inc=inc, inc_n=16)


def emit_rwkv_stack(S, cx, p, debug_stage):
    for l in range(NL):
        emit_rwkv_tm(S, cx, p, l)
        if debug_stage == f"L{l}tm":
            return True
        emit_rwkv_cm(S, cx, p, l)
        if debug_stage == f"L{l}":
            return True
    return False


def load_tm_arena(S, cx, l):
    ARENA = cx["ARENA"]
    waits = [("t", S.m("t"))]
    for m, w in enumerate([cx["tmkT"], cx["tmvT"], cx["tmrT"], cx["tmoT"]]):
        srcw = w[l].rearrange("(ct p) o -> p ct o", p=128)
        dst = r3(ARENA[:], 4)[:, m].rearrange("p (ct o) -> p ct o", ct=NCT)
        S.op("gpsimd", lambda e, dst=dst, srcw=srcw: e.dma_start(dst, srcw),
             waits=waits if m == 0 else [], inc="gd", inc_n=16)
    cx["arena_ready"] = ("gd", S.m("gd"))


def emit_rwkv_tm(S, cx, p, l):
    PS, PAR = cx["PS"], cx["PAR"]
    B0, B1, B2, B3 = cx["B0"], cx["B1"], cx["B2"], cx["B3"]
    XRES = cx["XRES"]
    load_tm_arena(S, cx, l)

    b0r = r3(B0[:], NCT)
    b1r = r3(B1[:], NCT)
    b2r = r3(B2[:, 0:NCT * (CH + 1)], NCT)
    b3r = r3(B3[:, 0:NCT * (CH + 1)], NCT)
    rw = r3(cx["RWB"][:], NCT)

    mixes = [("Ak", "Bk", l, "A"), ("Av", "Bv", l, "B"), ("Ar", "Br", l, "C")]
    preps = [emit_ln_prep(S, cx, 0, l, mixes)]
    for c in range(NCH):
        P = preps[c]
        par = P["par"]
        mxa, mixk_v = P["mix"]["A"]
        mxb, mixv_v = P["mix"]["B"]
        mxc, mixr_v = P["mix"]["C"]

        def ev_ek(r, glo, ghi, blo, mm_t):
            S.op("scalar", lambda e, glo=glo, ghi=ghi, blo=blo: e.activation(
                b0r[:, glo:ghi], ps_banks(cx, blo, ghi - glo), AF.Exp),
                waits=[("t", mm_t), fget(cx, "B0")], inc="s")
            for i in range(ghi - glo):
                cx["bankfree"][blo + i] = ("s", S.m("s"))
        emit_proj(S, cx, 0, mxa, ("v", mixk_v), ev_ek)
        ek_s = S.m("s")
        fset(cx, "B0", "s", ek_s)
        fset(cx, f"MXA{par}", "t", S.m("t"))
        if c + 1 < NCH:
            pre_next = emit_ln_prep_pre(S, cx, c + 1)

        def ev_v(r, glo, ghi, blo, mm_t):
            S.op("scalar", lambda e, glo=glo, ghi=ghi, blo=blo: e.activation(
                b1r[:, glo:ghi], ps_banks(cx, blo, ghi - glo), AF.Copy),
                waits=[("t", mm_t), fget(cx, "B1")], inc="s")
            for i in range(ghi - glo):
                cx["bankfree"][blo + i] = ("s", S.m("s"))
        emit_proj(S, cx, 1, mxb, ("v", mixv_v), ev_v)
        v_s = S.m("s")
        fset(cx, "B1", "s", v_s)
        fset(cx, f"MXB{par}", "t", S.m("t"))

        def ev_r(r, glo, ghi, blo, mm_t):
            S.op("scalar", lambda e, glo=glo, ghi=ghi, blo=blo: e.activation(
                b2r[:, glo:ghi, 0:CH], ps_banks(cx, blo, ghi - glo), AF.Exp,
                scale=-1.0),
                waits=[("t", mm_t), fget(cx, "B2")], inc="s")
            for i in range(ghi - glo):
                cx["bankfree"][blo + i] = ("s", S.m("s"))

        S.op("vector", lambda e: e.tensor_tensor(b1r, b0r, b1r, ALU.mult),
             waits=[("s", ek_s), ("s", v_s)], inc="v")
        c1_v = S.m("v")
        gcol = PMAP["G"] + l * 8
        hcol = PMAP["H"] + l * 8
        S.op("vector", lambda e: e.tensor_copy(
            b2r[:, :, 0:1], r3(PAR[:, gcol:gcol + 8], NCT)),
            waits=[fget(cx, "B2")], inc="v")
        S.op("vector", lambda e: e.tensor_copy(
            b3r[:, :, 0:1], r3(PAR[:, hcol:hcol + 8], NCT)),
            waits=[fget(cx, "B3")], inc="v")

        def scans(e):
            inst = None
            for ct in range(NCT):
                a_ap = PAR[:, pcol("adec", l) + ct: pcol("adec", l) + ct + 1]
                inst = e.tensor_tensor_scan(
                    b2r[:, ct, 1:1 + CH], a_ap.broadcast_to([128, CH]),
                    b1r[:, ct], PAR[:, gcol + ct:gcol + ct + 1],
                    ALU.mult, ALU.add)
            return inst
        S.op("vector", scans, waits=[("v", c1_v)], inc="v")

        def scand(e):
            inst = None
            for ct in range(NCT):
                a_ap = PAR[:, pcol("adec", l) + ct: pcol("adec", l) + ct + 1]
                inst = e.tensor_tensor_scan(
                    b3r[:, ct, 1:1 + CH], a_ap.broadcast_to([128, CH]),
                    b0r[:, ct], PAR[:, hcol + ct:hcol + ct + 1],
                    ALU.mult, ALU.add)
            return inst
        S.op("vector", scand, inc="v")
        S.op("vector", lambda e: e.tensor_copy(
            r3(PAR[:, gcol:gcol + 8], NCT), b2r[:, :, CH:CH + 1]), inc="v")
        S.op("vector", lambda e: e.tensor_copy(
            r3(PAR[:, hcol:hcol + 8], NCT), b3r[:, :, CH:CH + 1]), inc="v")

        def num(e):
            inst = None
            for ct in range(NCT):
                eu = PAR[:, pcol("Eu", l) + ct: pcol("Eu", l) + ct + 1]
                inst = e.scalar_tensor_tensor(b1r[:, ct], b2r[:, ct, 0:CH], eu,
                                              b1r[:, ct], ALU.mult, ALU.add)
            return inst
        S.op("vector", num, inc="v")
        num_v = S.m("v")

        def den(e):
            inst = None
            for ct in range(NCT):
                eu = PAR[:, pcol("Eu", l) + ct: pcol("Eu", l) + ct + 1]
                inst = e.scalar_tensor_tensor(b0r[:, ct], b3r[:, ct, 0:CH], eu,
                                              b0r[:, ct], ALU.mult, ALU.add)
            return inst
        S.op("vector", den, inc="v")
        den_v = S.m("v")
        fset(cx, "B2", "v", num_v)
        fset(cx, "B3", "v", den_v)

        emit_proj(S, cx, 2, mxc, ("v", mixr_v), ev_r)
        esm_s = S.m("s")
        fset(cx, "B2", "s", esm_s)
        fset(cx, f"MXC{par}", "t", S.m("t"))

        if c + 1 < NCH:
            pr_next = emit_ln_prep_stats(S, cx, c + 1, l, pre=pre_next)

        S.op("vector", lambda e: e.tensor_scalar(
            b2r[:, :, 0:CH], b2r[:, :, 0:CH], 1.0, None, ALU.add),
            waits=[("s", esm_s)], inc="v")
        S.op("vector", lambda e: e.tensor_tensor(
            b0r, b0r, b2r[:, :, 0:CH], ALU.mult),
            waits=[("v", den_v)], inc="v")
        d2_v = S.m("v")
        fset(cx, "B2", "v", d2_v)
        S.op("scalar", lambda e: e.activation(b3r[:, :, 0:CH], b0r, AF.Ln),
             waits=[("v", d2_v)], inc="s")
        S.op("scalar", lambda e: e.activation(b0r, b3r[:, :, 0:CH], AF.Exp,
                                              scale=-1.0), inc="s")
        rec_s = S.m("s")
        fset(cx, "B3", "s", rec_s)
        S.op("vector", lambda e: e.tensor_tensor(rw, b1r, b0r, ALU.mult),
             waits=[("s", rec_s), ("v", num_v), fget(cx, "RWB")], inc="v")
        rw_v = S.m("v")
        fset(cx, "RWB", "v", rw_v)
        fset(cx, "B0", "v", rw_v)
        fset(cx, "B1", "v", rw_v)

        xr = r3(XRES[:], NCT)[:, :, c * CH:(c + 1) * CH]

        def ev_o(r, glo, ghi, blo, mm_t, xr=xr):
            S.op("vector", lambda e, glo=glo, ghi=ghi, blo=blo, xr=xr:
                 e.tensor_tensor(xr[:, glo:ghi], xr[:, glo:ghi],
                                 ps_banks(cx, blo, ghi - glo), ALU.add),
                 waits=[("t", mm_t)] + cx["xres_ready"], inc="v")
            for i in range(ghi - glo):
                cx["bankfree"][blo + i] = ("v", S.m("v"))
        emit_proj(S, cx, 3, rw, ("v", rw_v), ev_o)
        oadd_v = S.m("v")
        fset(cx, "RWB", "t", S.m("t"))
        cx["xres_ready"] = [("v", oadd_v)]
        if c + 1 < NCH:
            preps.append(emit_ln_prep_finish(S, cx, pr_next, mixes))


def emit_rwkv_cm(S, cx, p, l):
    PS, PAR = cx["PS"], cx["PAR"]
    B0, B1, B2, B3 = cx["B0"], cx["B1"], cx["B2"], cx["B3"]
    ARENA, XRES = cx["ARENA"], cx["XRES"]

    cmr_dst = ARENA[:, 24576:32768].rearrange("p (ct o) -> p ct o", ct=NCT)
    cmr3 = ARENA[:, 24576:32768].rearrange("p (ct o) -> p ct o", ct=NCT)
    tm_done_t = S.m("t")
    S.op("gpsimd", lambda e: e.dma_start(
        cmr_dst, cx["cmrT"][l].rearrange("(ct p) o -> p ct o", p=128)),
        waits=[("t", tm_done_t)], inc="gd", inc_n=16)
    cmr_gd = S.m("gd")
    for nm in ("CKSLOT0", "CKSLOT1", "CV0", "CV1"):
        fset(cx, nm, "t", tm_done_t)

    b0r = r3(B0[:], NCT)
    b1r = r3(B1[:], NCT)
    b2r = r3(B2[:, 0:NCT * CH], NCT)
    sq_a = cx["B2"][:, 0:NCT * CH].bitcast(BF16)
    sq_b = cx["B3"][:, 0:NCT * CH].bitcast(BF16)

    def sq_slice(s):
        buf = sq_a if s < 4 else sq_b
        return r3(buf[:, (s % 4) * 1024:(s % 4) * 1024 + 1024], 4)

    prep = cx["cmprep"][l]
    fset(cx, "SQ", "s", S.m("s"))
    mixes = [("Ack", "Bck", l, "A"), ("Acr", "Bcr", l, "C")]
    preps = [emit_ln_prep(S, cx, 0, 4 + l, mixes)]
    for c in range(NCH):
        P = preps[c]
        par = P["par"]
        mxa, mixk_v = P["mix"]["A"]
        mxc, mixr_v = P["mix"]["C"]

        last_sq_v = 0
        for s in range(8):
            slot = s % 2
            slot3 = r3(ARENA[:, slot * 4096:(slot + 1) * 4096], NCT)
            srcw = cx["cmk_s"][l].rearrange("(ct p) f -> p ct f", p=128)[
                :, :, s * 512:(s + 1) * 512]
            dq, dsem = [("sync", "sd"), ("scalar", "ad")][s % 2]
            S.op(dq, lambda e, slot3=slot3, srcw=srcw: e.dma_start(
                slot3, srcw),
                waits=[("gd", prep), fget(cx, f"CKSLOT{slot}")],
                inc=dsem, inc_n=16)
            sl_sd = (dsem, S.m(dsem))
            blo = 3 * (s % 2)

            def fkk(t, slot3=slot3, blo=blo, mxa=mxa):
                inst = None
                for fi in range(4):
                    if fi < 3:
                        bank = PS[:, (blo + fi) * 512:(blo + fi) * 512 + CH]
                    else:
                        bank = PS[:, blo * 512 + CH:blo * 512 + 2 * CH]
                    for ct in range(NCT):
                        st = (ct == 0)
                        inst = t.matmul(bank,
                                        slot3[:, ct, fi * 128:(fi + 1) * 128],
                                        mxa[:, ct], start=st,
                                        stop=(ct == NCT - 1))
                return inst
            S.op("tensor", fkk, waits=[sl_sd, ("v", mixk_v)] +
                 [cx["bankfree"][blo + i] for i in range(3)], inc="t")
            kk_t = S.m("t")
            fset(cx, f"CKSLOT{slot}", "t", kk_t)
            if cx.get("dbgcm") == (l, c) and s == 0:
                stg = r3(cx["RWB"][:], NCT)
                S.op("scalar", lambda e, stg=stg: e.activation(
                    stg[:, 0:3], r3(PS[:, 0:3 * 512], 3)[:, :, 0:CH],
                    AF.Copy), waits=[("t", kk_t)], inc="s")
                S.op("scalar", lambda e, stg=stg: e.activation(
                    stg[:, 3], PS[:, CH:2 * CH], AF.Copy), inc="s")
                _dmp(S, cx, 14, stg[:, 0:4], [("s", S.m("s"))], cast=True,
                     nct=4)
                for i in range(3):
                    cx["bankfree"][i] = ("gd", S.m("gd"))
            sqv = sq_slice(s)

            def evk(e, sqv=sqv, blo=blo):
                src = r3(PS[:, blo * 512:(blo + 3) * 512], 3)[:, :, 0:CH]
                e.tensor_scalar(sqv[:, 0:3], src, 0.0, None, ALU.max)
                return e.tensor_scalar(
                    sqv[:, 3], PS[:, blo * 512 + CH:blo * 512 + 2 * CH], 0.0,
                    None, ALU.max)
            S.op("vector", evk, waits=[("t", kk_t), fget(cx, "SQ")], inc="v")
            S.op("vector", lambda e, sqv=sqv: e.tensor_tensor(
                sqv, sqv, sqv, ALU.mult), inc="v")
            last_sq_v = S.m("v")
            for i in range(3):
                cx["bankfree"][blo + i] = ("v", last_sq_v)

        if c + 1 < NCH:
            pre_next = emit_ln_prep_pre(S, cx, c + 1)

        def ev_rr(r, glo, ghi, blo, mm_t):
            S.op("scalar", lambda e, glo=glo, ghi=ghi, blo=blo: e.activation(
                b0r[:, glo:ghi], ps_banks(cx, blo, ghi - glo), AF.Exp,
                scale=-1.0),
                waits=[("t", mm_t), fget(cx, "B0")], inc="s")
            for i in range(ghi - glo):
                cx["bankfree"][blo + i] = ("s", S.m("s"))
        cx["arena_ready"] = ("gd", cmr_gd)
        emit_proj(S, cx, None, mxc, ("v", mixr_v), ev_rr, wa3=cmr3)
        e2m_s = S.m("s")
        fset(cx, "B0", "s", e2m_s)
        fset(cx, f"MXC{par}", "t", S.m("t"))
        dwait1, dwait2 = [], []
        if cx.get("dbgcm") == (l, c):
            _dmp(S, cx, 9, b0r, [("s", e2m_s)])
            dwait2 = [("sd", S.m("sd"))]
        S.op("vector", lambda e: e.tensor_scalar(b0r, b0r, 1.0, None, ALU.add),
             waits=[("s", e2m_s)] + dwait2, inc="v")
        d_v = S.m("v")

        def cv_slot(sl):
            return ARENA[:, 8192 + sl * 8192:8192 + (sl + 1) * 8192]\
                .rearrange("p (ft o) -> p ft o", ft=8)

        def cv_load(s):
            sl = s % 2
            srcw = cx["cmv_s"][l][s * 1024:(s + 1) * 1024].rearrange(
                "(ft p) o -> p ft o", p=128)
            S.op("gpsimd", lambda e, sl=sl, srcw=srcw: e.dma_start(
                cv_slot(sl), srcw),
                waits=[("gd", prep), fget(cx, f"CV{sl}")], inc="gd", inc_n=16)
            return S.m("gd")
        cv_sd = {0: cv_load(0), 1: cv_load(1)}
        for s in range(4):
            slot3 = cv_slot(s % 2)
            for fi in range(8):
                ft = s * 8 + fi
                buf = sq_a if ft < 16 else sq_b
                ft_sq = buf[:, (ft % 16) * CH:(ft % 16) * CH + CH]

                def fvv(t, slot3=slot3, fi=fi, ft=ft, ft_sq=ft_sq):
                    inst = None
                    for g in range(8):
                        inst = t.matmul(
                            PS[:, g * 512:g * 512 + CH],
                            slot3[:, fi, g * 128:(g + 1) * 128],
                            ft_sq, start=(ft == 0), stop=(ft == NFT - 1))
                    return inst
                wv = [("gd", cv_sd[s]), ("v", last_sq_v)]
                if ft == 0:
                    wv += [cx["bankfree"][i] for i in range(8)]
                S.op("tensor", fvv, waits=wv, inc="t")
            fset(cx, f"CV{s % 2}", "t", S.m("t"))
            if s + 2 < 4:
                cv_sd[s + 2] = cv_load(s + 2)
        vv_t = S.m("t")
        S.op("scalar", lambda e: e.activation(
            b1r[:, 6:8], r3(PS[:, 6 * 512:8 * 512], 2)[:, :, 0:CH], AF.Copy),
            waits=[("t", vv_t), fget(cx, "B1")], inc="s")
        ev67_s = S.m("s")
        cx["bankfree"][6] = ("s", ev67_s)
        cx["bankfree"][7] = ("s", ev67_s)
        S.op("scalar", lambda e: e.activation(
            b1r[:, 0:6], r3(PS[:, 0:6 * 512], 6)[:, :, 0:CH], AF.Copy),
            inc="s")
        vv_s = S.m("s")
        for i in range(6):
            cx["bankfree"][i] = ("s", vv_s)
        fset(cx, "B1", "s", vv_s)
        fset(cx, "B2", "t", vv_t)
        fset(cx, "B3", "t", vv_t)

        if cx.get("dbgcm") == (l, c):
            _dmp(S, cx, 8, b1r, [("s", vv_s)])
            dwait1 = [("sd", S.m("sd"))]
            _dmp(S, cx, 12, r3(sq_a[:, 0:2048], NCT), [("v", last_sq_v)],
                 cast=True)
            _dmp(S, cx, 10, mxa, [("v", mixk_v)], cast=True)
            _dmp(S, cx, 11, mxc, [("v", mixr_v)], cast=True)
            _dmp(S, cx, 13, P["xh"][:, :, 1:1 + CH], [("v", P["xhat_v"])],
                 cast=True)

        if c + 1 < NCH:
            pr_next = emit_ln_prep_stats(S, cx, c + 1, 4 + l, pre=pre_next)
            preps.append(emit_ln_prep_finish(S, cx, pr_next, mixes))

        S.op("scalar", lambda e: e.activation(b2r, b0r, AF.Ln),
             waits=[("v", d_v), fget(cx, "B2")], inc="s")
        S.op("scalar", lambda e: e.activation(b0r, b2r, AF.Exp, scale=-1.0),
             inc="s")
        rec_s = S.m("s")
        S.op("vector", lambda e: e.tensor_tensor(b1r, b1r, b0r, ALU.mult),
             waits=[("s", rec_s), ("s", vv_s)] + dwait1, inc="v")
        xr = r3(XRES[:], NCT)[:, :, c * CH:(c + 1) * CH]
        S.op("vector", lambda e, xr=xr: e.tensor_tensor(xr, xr, b1r, ALU.add),
             waits=list(cx["xres_ready"]), inc="v")
        done_v = S.m("v")
        fset(cx, "B0", "v", done_v)
        fset(cx, "B1", "v", done_v)
        fset(cx, "B2", "s", rec_s)
        fset(cx, "B3", "s", rec_s)
        fset(cx, "SQ", "s", rec_s)
        fset(cx, f"MXA{par}", "t", S.m("t"))
        cx["xres_ready"] = [("v", done_v)]


def emit_ffn(S, cx, p, site):
    PS = cx["PS"]
    B0, B1, B2 = cx["B0"], cx["B1"], cx["B2"]
    ARENA, XRES = cx["ARENA"], cx["XRES"]
    waits = [("t", S.m("t"))]
    for m, w in enumerate([cx["frT"], cx["fkT"], cx["fvT"]]):
        srcw = w.rearrange("(ct p) o -> p ct o", p=128)
        dst = r3(ARENA[:], 4)[:, m].rearrange("p (ct o) -> p ct o", ct=NCT)
        S.op("gpsimd", lambda e, dst=dst, srcw=srcw: e.dma_start(dst, srcw),
             waits=waits if m == 0 else [], inc="gd", inc_n=16)
    cx["arena_ready"] = ("gd", S.m("gd"))

    b0r = r3(B0[:], NCT)
    b1r = r3(B1[:], NCT)
    b2f = r3(B2[:, 0:NCT * CH], NCT)
    sq3 = r3(B2[:, 0:NCT * CH].bitcast(BF16)[:, 0:NCT * CH], NCT)

    mixes = [("fAr", "fBr", None, "A"), ("fAk", "fBk", None, "B")]
    preps = [emit_ln_prep(S, cx, 0, site, mixes)]
    for c in range(NCH):
        P = preps[c]
        par = P["par"]
        mxa, mixr_v = P["mix"]["A"]
        mxb, mixk_v = P["mix"]["B"]

        def ev_r(r, glo, ghi, blo, mm_t):
            S.op("scalar", lambda e, glo=glo, ghi=ghi, blo=blo: e.activation(
                b0r[:, glo:ghi], ps_banks(cx, blo, ghi - glo), AF.Exp,
                scale=-1.0),
                waits=[("t", mm_t), fget(cx, "B0")], inc="s")
            for i in range(ghi - glo):
                cx["bankfree"][blo + i] = ("s", S.m("s"))
        emit_proj(S, cx, 0, mxa, ("v", mixr_v), ev_r)
        erm_s = S.m("s")
        fset(cx, "B0", "s", erm_s)
        fset(cx, f"MXA{par}", "t", S.m("t"))
        if c + 1 < NCH:
            pre_next = emit_ln_prep_pre(S, cx, c + 1)

        def ev_k(r, glo, ghi, blo, mm_t):
            S.op("vector", lambda e, glo=glo, ghi=ghi, blo=blo: e.tensor_scalar(
                sq3[:, glo:ghi], ps_banks(cx, blo, ghi - glo), 0.0, None,
                ALU.max),
                waits=[("t", mm_t), fget(cx, "B2")], inc="v")
            S.op("vector", lambda e, glo=glo, ghi=ghi: e.tensor_tensor(
                sq3[:, glo:ghi], sq3[:, glo:ghi], sq3[:, glo:ghi], ALU.mult),
                inc="v")
            for i in range(ghi - glo):
                cx["bankfree"][blo + i] = ("v", S.m("v"))
        emit_proj(S, cx, 1, mxb, ("v", mixk_v), ev_k)
        sq_v = S.m("v")
        fset(cx, "B2", "v", sq_v)
        fset(cx, f"MXB{par}", "t", S.m("t"))

        def ev_v(r, glo, ghi, blo, mm_t):
            S.op("scalar", lambda e, glo=glo, ghi=ghi, blo=blo: e.activation(
                b1r[:, glo:ghi], ps_banks(cx, blo, ghi - glo), AF.Copy),
                waits=[("t", mm_t), fget(cx, "B1")], inc="s")
            for i in range(ghi - glo):
                cx["bankfree"][blo + i] = ("s", S.m("s"))
        emit_proj(S, cx, 2, sq3, ("v", sq_v), ev_v)
        vv_s = S.m("s")
        fset(cx, "B1", "s", vv_s)
        fset(cx, "B2", "t", S.m("t"))

        if c + 1 < NCH:
            pr_next = emit_ln_prep_stats(S, cx, c + 1, site, pre=pre_next)

        S.op("vector", lambda e: e.tensor_scalar(b0r, b0r, 1.0, None, ALU.add),
             waits=[("s", erm_s)], inc="v")
        d_v = S.m("v")
        S.op("scalar", lambda e: e.activation(b2f, b0r, AF.Ln),
             waits=[("v", d_v), fget(cx, "B2")], inc="s")
        S.op("scalar", lambda e: e.activation(b0r, b2f, AF.Exp, scale=-1.0),
             inc="s")
        rec_s = S.m("s")
        S.op("vector", lambda e: e.tensor_tensor(b1r, b1r, b0r, ALU.mult),
             waits=[("s", rec_s), ("s", vv_s)], inc="v")
        xr = r3(XRES[:], NCT)[:, :, c * CH:(c + 1) * CH]
        S.op("vector", lambda e, xr=xr: e.tensor_tensor(xr, xr, b1r, ALU.add),
             waits=list(cx["xres_ready"]), inc="v")
        done_v = S.m("v")
        fset(cx, "B0", "v", done_v)
        fset(cx, "B1", "v", done_v)
        fset(cx, "B2", "s", rec_s)
        cx["xres_ready"] = [("v", done_v)]
        if c + 1 < NCH:
            preps.append(emit_ln_prep_finish(S, cx, pr_next, mixes))


def emit_attn(S, cx, p):
    PS, RWB, PAR, PARB = cx["PS"], cx["RWB"], cx["PAR"], cx["PARB"]
    B0, B1, B2, B3 = cx["B0"], cx["B1"], cx["B2"], cx["B3"]
    ARENA, XRES = cx["ARENA"], cx["XRES"]
    ksp, vtsp, qsp = cx["ksp"], cx["vtsp"], cx["qsp"]
    poff = p * NCH

    waits = [("t", S.m("t"))]
    for m, w in enumerate([cx["wqT"], cx["wkT"], cx["wvT"], cx["woT"]]):
        srcw = w.rearrange("(ct p) o -> p ct o", p=128)
        dst = r3(ARENA[:], 4)[:, m].rearrange("p (ct o) -> p ct o", ct=NCT)
        S.op("gpsimd", lambda e, dst=dst, srcw=srcw: e.dma_start(dst, srcw),
             waits=waits if m == 0 else [], inc="gd", inc_n=16)
    cx["arena_ready"] = ("gd", S.m("gd"))

    rwb3 = r3(RWB[:], NCT)
    stg = B2[:].bitcast(BF16)[:, 0:8 * 2 * 130]
    stg4 = stg.rearrange("p (pr tt w) -> p pr tt w", pr=8, tt=2)

    S.op("vector", lambda e: e.memset(
        stg.rearrange("p (b w) -> p b w", b=16, w=130)[:, :, 64:130:65], 1.0),
        waits=[fget(cx, "B2")], inc="v")
    ones_v = S.m("v")
    preps = [emit_ln_prep(S, cx, 0, None, [])]
    for c in range(NCH):
        if c + 1 < NCH:
            preps.append(emit_ln_prep(S, cx, c + 1, None, []))
        P = preps[c]
        par = P["par"]
        xh3 = P["xh"][:, :, 1:1 + CH]
        xhat_v = P["xhat_v"]
        gcol = p * HALF + c * CH

        def ev_q(r, glo, ghi, blo, mm_t):
            for g in range(glo, ghi):
                bq = PAR[:, pcol("bq") + g:pcol("bq") + g + 1]
                bank = PS[:, (blo + g - glo) * 512:(blo + g - glo) * 512 + CH]
                S.op("scalar", lambda e, g=g, bank=bank, bq=bq: e.activation(
                    rwb3[:, g], bank, AF.Identity, bias=bq),
                    waits=[("t", mm_t), fget(cx, "RWB")], inc="s")
                cx["bankfree"][blo + g - glo] = ("s", S.m("s"))
        emit_proj(S, cx, 0, xh3, ("v", xhat_v), ev_q)
        q_s = S.m("s")
        S.op("sync", lambda e, gcol=gcol: e.dma_start(
            qsp.rearrange("(ct p) t -> p ct t", p=128)[:, :, gcol:gcol + CH],
            rwb3), waits=[("s", q_s)], inc="sd", inc_n=16)
        fset(cx, "RWB", "sd", S.m("sd"))

        def ev_k(r, glo, ghi, blo, mm_t):
            for g in range(glo, ghi):
                bk = PAR[:, pcol("bk") + g:pcol("bk") + g + 1]
                bank = PS[:, (blo + g - glo) * 512:(blo + g - glo) * 512 + CH]
                S.op("scalar", lambda e, g=g, bank=bank, bk=bk: e.activation(
                    rwb3[:, g], bank, AF.Identity, bias=bk),
                    waits=[("t", mm_t), fget(cx, "RWB")], inc="s")
                cx["bankfree"][blo + g - glo] = ("s", S.m("s"))
        emit_proj(S, cx, 1, xh3, ("v", xhat_v), ev_k)
        k_s = S.m("s")
        S.op("sync", lambda e, gcol=gcol: e.dma_start(
            ksp.rearrange("(ct p) t -> p ct t", p=128)[:, :, gcol:gcol + CH],
            rwb3), waits=[("s", k_s)], inc="sd", inc_n=16)
        fset(cx, "RWB", "sd", S.m("sd"))

        wv3 = r3(ARENA[:], 4)[:, 2].rearrange("p (ct o) -> p ct o", ct=NCT)
        vmm_t = 0
        for tt in range(2):
            for nh in range(2):
                bank = PS[:, nh * 512:nh * 512 + 512]

                def fmm(t, tt=tt, nh=nh, bank=bank, P=P):
                    inst = None
                    for ct in range(NCT):
                        inst = t.matmul(
                            bank,
                            P["xh"][:, ct, 1 + tt * 128:1 + tt * 128 + 128],
                            wv3[:, ct, nh * 512:(nh + 1) * 512],
                            start=(ct == 0), stop=(ct == NCT - 1))
                    return inst
                S.op("tensor", fmm, waits=[("v", xhat_v),
                                           cx["bankfree"][nh],
                                           cx["arena_ready"]], inc="t")
            vmm_t = S.m("t")

            def ev(e, tt=tt):
                src_ap = PS[:, 0:1024].rearrange(
                    "p (pr h w) -> p pr h w", pr=8, h=2, w=64)
                dst_ap = stg4[:, :, tt, :].rearrange(
                    "p pr (h z) -> p pr h z", h=2, z=65)[:, :, :, 0:64]
                return e.activation(dst_ap, src_ap, AF.Copy)
            S.op("scalar", ev, waits=[("t", vmm_t), ("v", ones_v),
                                      fget(cx, "B2")], inc="s")
            cx["bankfree"][0] = ("s", S.m("s"))
            cx["bankfree"][1] = ("s", S.m("s"))
        fset(cx, f"XHAT{par}", "t", vmm_t)
        vstg_s = S.m("s")
        dst = vtsp.rearrange("p (pr tt w) -> p pr tt w", pr=8, tt=32,
                             w=130)[:, :, 2 * (poff + c):2 * (poff + c) + 2, :]
        S.op("sync", lambda e, dst=dst: e.dma_start(dst, stg4),
             waits=[("s", vstg_s)], inc="sd", inc_n=16)
        fset(cx, "B2", "sd", S.m("sd"))
    spills_sd = S.m("sd")

    kslot = [B0[:].bitcast(BF16), B1[:].bitcast(BF16)]
    vslot = [B2[:].bitcast(BF16), B3[:].bitcast(BF16)]
    ptbuf = cx["XHATp"][0][:]
    slot_free = {("k", 0): [fget(cx, "B0")], ("k", 1): [fget(cx, "B1")],
                 ("v", 0): [fget(cx, "B2")], ("v", 1): [fget(cx, "B3")]}
    cx["free"]["PT0"] = fget(cx, "XHAT0")
    cx["free"]["PT1"] = fget(cx, "XHAT0")
    cx["free"]["OB0"] = fget(cx, "XHAT0")
    cx["free"]["OB1"] = fget(cx, "XHAT0")
    SQF = cx["SQB"][:].bitcast(F32)
    QB = cx["MXAp"][0]

    for c in range(NCH):
        q0 = p * HALF + c * CH
        nkt = (q0 + CH) // 128
        S.op("sync", lambda e, q0=q0: e.dma_start(
            r3(QB[:], NCT),
            qsp.rearrange("(ct p) t -> p ct t", p=128)[:, :, q0:q0 + CH]),
            waits=[("sd", spills_sd), fget(cx, "MXA0")], inc="sd", inc_n=16)
        qd = S.m("sd")

        for pr in range(8):
            ks = kslot[pr % 2]
            vs = vslot[pr % 2]
            S.op("sync", lambda e, nkt=nkt, ks=ks, pr=pr: e.dma_start(
                ks[:, 0:nkt * 128], ksp[pr * 128:(pr + 1) * 128, 0:nkt * 128]),
                waits=[("sd", spills_sd)] + slot_free[("k", pr % 2)],
                inc="sd", inc_n=16)
            kd = S.m("sd")
            S.op("gpsimd", lambda e, nkt=nkt, vs=vs, pr=pr: e.dma_start(
                vs[:, 0:nkt * 130],
                vtsp.rearrange("p (pr w) -> p pr w", pr=8, w=32 * 130)[
                    :, pr, 0:nkt * 130]),
                waits=[("sd", spills_sd)] + slot_free[("v", pr % 2)],
                inc="gd", inc_n=16)
            vd = ("gd", S.m("gd"))

            def emit_sc(kt):
                sb = [0, 2][kt % 2]

                def fsc(t, kt=kt, sb=sb, ks=ks, pr=pr):
                    t.matmul(PS[:, sb * 512:sb * 512 + CH],
                             ks[0:64, kt * 128:(kt + 1) * 128],
                             r3(QB[:], NCT)[0:64, pr, :],
                             start=True, stop=True, tile_position=(0, 0))
                    return t.matmul(PS[:, (sb + 1) * 512:(sb + 1) * 512 + CH],
                                    ks[64:128, kt * 128:(kt + 1) * 128],
                                    r3(QB[:], NCT)[64:128, pr, :],
                                    start=True, stop=True,
                                    tile_position=(64, 0))
                S.op("tensor", fsc, waits=[("sd", kd), ("sd", qd),
                                           cx["bankfree"][sb],
                                           cx["bankfree"][sb + 1]], inc="t")
                return S.m("t")

            sc_ts = {0: emit_sc(0)}
            for kt in range(nkt):
                sb = [0, 2][kt % 2]
                pt = kt % 2
                ptap = ptbuf[:, pt * 512:pt * 512 + 512]
                sc_t = sc_ts[kt]

                def fex(e, sb=sb, ptap=ptap):
                    src = PS[:, sb * 512:sb * 512 + 1024].rearrange(
                        "p (h w) -> p h w", h=2, w=512)[:, :, 0:CH]
                    return e.activation(
                        ptap.rearrange("p (h w) -> p h w", h=2, w=CH), src,
                        AF.Exp, scale=0.125)
                S.op("scalar", fex, waits=[("t", sc_t), fget(cx, f"PT{pt}")],
                     inc="s")
                ex_s = S.m("s")
                cx["bankfree"][sb] = ("s", ex_s)
                cx["bankfree"][sb + 1] = ("s", ex_s)
                avw = [("s", ex_s), vd]
                if kt * 128 > q0 - 128:
                    def fmask(e, kt=kt, ptap=ptap, q0=q0):
                        i = None
                        for h in range(2):
                            i = e.affine_select(
                                ptap[:, h * CH:(h + 1) * CH],
                                ptap[:, h * CH:(h + 1) * CH],
                                [[1, CH]], ALU.is_ge, 0.0,
                                base=q0 - kt * 128, channel_multiplier=-1)
                        return i
                    S.op("gpsimd", fmask, waits=[("s", ex_s)], inc="g")
                    avw = [("g", S.m("g")), vd]
                if kt + 1 < nkt:
                    sc_ts[kt + 1] = emit_sc(kt + 1)

                def fav(t, kt=kt, vs=vs, ptap=ptap, first=(kt == 0),
                        last=(kt == nkt - 1)):
                    t.matmul(PS[0:65, 4 * 512:4 * 512 + CH],
                             vs[:, kt * 130:kt * 130 + 65],
                             ptap[:, 0:CH], start=first, stop=last)
                    t.matmul(PS[64:128, 5 * 512:5 * 512 + CH],
                             vs[:, kt * 130 + 65:kt * 130 + 129],
                             ptap[:, CH:2 * CH], start=first, stop=last,
                             tile_position=(0, 64))
                    return t.matmul(PS[0:1, 5 * 512 + CH:5 * 512 + 2 * CH],
                                    vs[:, kt * 130 + 129:kt * 130 + 130],
                                    ptap[:, CH:2 * CH], start=first, stop=last)
                avw2 = avw + ([cx["bankfree"][4], cx["bankfree"][5]]
                              if kt == 0 else [])
                S.op("tensor", fav, waits=avw2, inc="t")
                fset(cx, f"PT{pt}", "t", S.m("t"))
            av_t = S.m("t")
            slot_free[("k", pr % 2)] = [("t", av_t)]
            slot_free[("v", pr % 2)] = [("t", av_t)]

            for h in range(2):
                pb = 64 if h == 0 else 0
                avbank = PS[0 + 64 * h:64 + 64 * h,
                            (4 + h) * 512:(4 + h) * 512 + CH]
                den_ap = (PS[64:65, 4 * 512:4 * 512 + CH] if h == 0 else
                          PS[0:1, 5 * 512 + CH:5 * 512 + 2 * CH])
                rrow = SQF[pb:pb + 1, 0:CH]
                rrow2 = SQF[pb:pb + 1, CH:2 * CH]
                brow = PARB[pb:pb + 1, PB_MU:PB_MU + CH]
                S.op("scalar", lambda e, den_ap=den_ap, rrow=rrow: e.activation(
                    rrow, den_ap, AF.Ln),
                    waits=[("t", av_t), fget(cx, f"DROW{h}")], inc="s")
                S.op("scalar", lambda e, rrow=rrow, rrow2=rrow2: e.activation(
                    rrow2, rrow, AF.Exp, scale=-1.0), inc="s")
                S.op("scalar", lambda e, rrow2=rrow2, brow=brow: e.activation(
                    brow, rrow2, AF.Copy), inc="s")
                row_s = S.m("s")
                bb = 6 + h

                def fbc(e, brow=brow, pb=pb, bb=bb, h=h):
                    ones = PARB[pb:pb + 1, PB_ONESROW:PB_ONESROW + 64]
                    if h == 0:
                        return e.matmul(PS[0:64, bb * 512:bb * 512 + CH],
                                        ones, brow, start=True, stop=True)
                    return e.matmul(PS[64:128, bb * 512:bb * 512 + CH],
                                    ones, brow, start=True, stop=True,
                                    tile_position=(0, 64))
                S.op("tensor", fbc, waits=[("s", row_s), cx["bankfree"][bb]],
                     inc="t")
                bc_t = S.m("t")
                fset(cx, f"DROW{h}", "t", bc_t)
                obuf = ptbuf[0 + 64 * h:64 + 64 * h, 1024:1024 + CH]
                S.op("scalar", lambda e, avbank=avbank, obuf=obuf: e.activation(
                    obuf, avbank, AF.Copy),
                    waits=[("t", av_t), fget(cx, f"OB{h}")], inc="s")
                ob_s = S.m("s")
                dst = rwb3[64 * h:64 * h + 64, pr, :]
                bvap = PAR[64 * h:64 * h + 64,
                           pcol("bv") + pr:pcol("bv") + pr + 1]
                bcread = (PS[0:64, bb * 512:bb * 512 + CH] if h == 0 else
                          PS[64:128, bb * 512:bb * 512 + CH])
                S.op("vector", lambda e, obuf=obuf, dst=dst, bcread=bcread:
                     e.tensor_tensor(dst, obuf, bcread, ALU.mult),
                     waits=[("s", ob_s), ("t", bc_t), fget(cx, "RWB")],
                     inc="v")
                S.op("vector", lambda e, dst=dst, bvap=bvap: e.tensor_scalar(
                    dst, dst, bvap, None, ALU.add), inc="v")
                fin_v = S.m("v")
                fset(cx, f"OB{h}", "v", fin_v)
                cx["bankfree"][bb] = ("v", fin_v)
                cx["bankfree"][4 + h] = ("v", fin_v)
        ofin_v = S.m("v")
        fset(cx, "RWB", "v", ofin_v)
        fset(cx, "MXA0", "t", S.m("t"))

        xr = r3(XRES[:], NCT)[:, :, c * CH:(c + 1) * CH]

        def ev_o(r, glo, ghi, blo, mm_t, xr=xr):
            def fn(e, glo=glo, ghi=ghi, blo=blo, xr=xr):
                inst = None
                for g in range(glo, ghi):
                    bank = PS[:, (blo + g - glo) * 512:
                              (blo + g - glo) * 512 + CH]
                    inst = e.scalar_tensor_tensor(
                        xr[:, g], bank, PAR[:, pcol("bo") + g:pcol("bo") + g + 1],
                        xr[:, g], ALU.add, ALU.add)
                return inst
            S.op("vector", fn, waits=[("t", mm_t)] + cx["xres_ready"], inc="v")
            for i in range(ghi - glo):
                cx["bankfree"][blo + i] = ("v", S.m("v"))
        emit_proj(S, cx, 3, rwb3, ("v", ofin_v), ev_o)
        oadd_v = S.m("v")
        fset(cx, "RWB", "t", S.m("t"))
        cx["xres_ready"] = [("v", oadd_v)]

    end_t = S.m("t")
    for nm in ("B0", "B1", "B2", "B3", "XHAT0"):
        fset(cx, nm, "t", end_t)
    fset(cx, "SQB", "s", S.m("s"))


def _prep_inputs(inputs):
    def tr(a):
        return np.ascontiguousarray(np.swapaxes(np.asarray(a), -1, -2))

    g = {k: np.asarray(v) for k, v in inputs.items()}
    com = {}
    com["tmkT"] = tr(g["L0_attn_a_tm_k_w"])
    com["tmvT"] = tr(g["L0_attn_a_tm_v_w"])
    com["tmrT"] = tr(g["L0_attn_a_tm_r_w"])
    com["tmoT"] = tr(g["L0_attn_a_tm_o_w"])
    com["cmkT"] = tr(g["L0_attn_a_cm_k_w"])
    com["cmvT"] = tr(g["L0_attn_a_cm_v_w"])
    com["cmrT"] = tr(g["L0_attn_a_cm_r_w"])
    wqkv = g["L0_attn_b_W_qkv"]
    com["wqT"] = tr(wqkv[0:1024])
    com["wkT"] = tr(wqkv[1024:2048])
    com["wvT"] = tr(wqkv[2048:3072])
    com["woT"] = tr(g["L0_attn_b_W_o"])
    com["frT"] = tr(g["L0_ffn_W_r"])
    com["fkT"] = tr(g["L0_ffn_W_k"])
    com["fvT"] = tr(g["L0_ffn_W_v"])

    def col8(v):
        return np.ascontiguousarray(np.asarray(v).reshape(8, 128).T)

    cols = {}
    for l in range(NL):
        cols[f"raw_ln1w{l}"] = col8(g["L0_attn_a_ln1_w"][l])
        cols[f"raw_ln2w{l}"] = col8(g["L0_attn_a_ln2_w"][l])
        cols[f"raw_mk{l}"] = col8(g["L0_attn_a_tm_mix_k"][l, 0, 0])
        cols[f"raw_mv{l}"] = col8(g["L0_attn_a_tm_mix_v"][l, 0, 0])
        cols[f"raw_mr{l}"] = col8(g["L0_attn_a_tm_mix_r"][l, 0, 0])
        cols[f"raw_cmk{l}"] = col8(g["L0_attn_a_cm_mix_k"][l, 0, 0])
        cols[f"raw_cmr{l}"] = col8(g["L0_attn_a_cm_mix_r"][l, 0, 0])
        cols[f"raw_td{l}"] = col8(g["L0_attn_a_time_decay"][l])
        cols[f"raw_tf{l}"] = col8(g["L0_attn_a_time_first"][l])
    cols["raw_mur"] = col8(g["L0_ffn_mu_r"])
    cols["raw_muk"] = col8(g["L0_ffn_mu_k"])
    cols["lnfw"] = col8(g["L0_attn_a_lnf_w"])
    bqkv = g["L0_attn_b_b_qkv"]
    cols["bq"] = col8(bqkv[0:1024])
    cols["bk"] = col8(bqkv[1024:2048])
    cols["bv"] = col8(bqkv[2048:3072])
    cols["bo"] = col8(g["L0_attn_b_b_o"])
    vec = np.concatenate([cols[n] for n in VEC_ORDER], axis=1).astype(np.float32)
    com["vecs"] = np.ascontiguousarray(vec)

    x = g["L0_attn_a_x"]
    per_core = []
    for b in range(8):
        m = dict(com)
        m["xt"] = np.ascontiguousarray(x[b].T.astype(np.float32))
        per_core.append(m)
    return per_core


def kernel(**inputs):
    in_maps = _prep_inputs(inputs)
    nc = build_program()
    res = run_bass_kernel_spmd(nc, in_maps, core_ids=list(range(8)))
    outs = [np.ascontiguousarray(np.asarray(r["out"]).T) for r in res.results]
    return np.stack(outs, axis=0).astype(np.float32)


# version 10
# speedup vs baseline: 1.2152x; 1.0244x over previous
import numpy as np
import concourse.bass as bass
import concourse.mybir as mybir
from concourse.bass_utils import run_bass_kernel_spmd

F32 = mybir.dt.float32
BF16 = mybir.dt.bfloat16
AF = mybir.ActivationFunctionType
ALU = mybir.AluOpType

C = 1024
NCT = 8
T = 4096
HALF = 2048
CH = 256
NCH = 8
FFND = 4096
NFT = 32
NL = 4
HEADS = 16
DH = 64
EPS = 1e-5
INV_C = 1.0 / C


SERIALIZE = False
SER_SAME_ENGINES = ("vector", "scalar", "gpsimd")
DETECT_RACES = False
FFN_TRUNC = None

ENG_SEM = {"tensor": "t", "vector": "v", "scalar": "s", "gpsimd": "g",
           "sync": "sd"}


class Sched:
    ENGINES = ("tensor", "vector", "scalar", "gpsimd", "sync")

    def __init__(self, nc):
        self.nc = nc
        self.ops = {e: [] for e in self.ENGINES}
        self.sem = {}
        self.count = {}
        self.seen = {e: {} for e in self.ENGINES}

    def add_sem(self, name, handle):
        self.sem[name] = handle
        self.count[name] = 0
        for e in self.ENGINES:
            self.seen[e][name] = 0

    def op(self, eng, fn, waits=(), inc=None, inc_n=1):
        for s, v in waits:
            assert v <= self.count[s], f"wait {s}>={v} but only {self.count[s]} emitted"
        if SERIALIZE:
            waits = list(waits) + [(s, self.count[s]) for s in self.count
                                   if s != ENG_SEM[eng]]
            if inc is None:
                inc = ENG_SEM[eng]
                inc_n = 16 if inc in ("sd", "gd") else 1
        elif eng in SER_SAME_ENGINES:
            own = ENG_SEM[eng]
            waits = list(waits) + [(own, self.count[own])]
            if inc is None:
                inc = own
                inc_n = 16 if inc in ("sd", "gd") else 1
        if inc is not None:
            self.count[inc] += inc_n
        sems = self.sem
        seen = self.seen[eng]
        eff = {}
        for s, v in waits:
            if v > seen[s] and v > eff.get(s, 0):
                eff[s] = v
        for s, v in eff.items():
            seen[s] = v
        waits = tuple(eff.items())

        def closure(e, fn=fn, waits=waits, inc=inc, inc_n=inc_n):
            for s, v in waits:
                e.wait_ge(sems[s], v)
            inst = fn(e)
            if inc is not None and inst is not None:
                inst.then_inc(sems[inc], inc_n)
        self.ops[eng].append(closure)
        return self.count[inc] if inc is not None else None

    def m(self, sem):
        return self.count[sem]


def r3(ap, nb):
    return ap.rearrange("p (b w) -> p b w", b=nb)


def _par_map():
    m = {}
    col = 0

    def add(name, n=8):
        nonlocal col
        m[name] = col
        col += n
    for l in range(NL):
        for nm in ("Ak", "Bk", "Av", "Bv", "Ar", "Br",
                   "Ack", "Bck", "Acr", "Bcr", "adec", "Eu"):
            add(f"{nm}{l}")
    for nm in ("fAr", "fBr", "fAk", "fBk", "lnfw", "bq", "bk", "bo", "bv"):
        add(nm)
    add("G", NL * 8)
    add("H", NL * 8)
    add("halo", 10 * 8)
    add("epsc", 8)
    add("onesf", 128)
    assert col <= 768, col
    return m, col


PMAP, PAR_USED = _par_map()

VEC_ORDER = ([f"raw_{nm}{l}" for l in range(NL)
              for nm in ("ln1w", "ln2w", "mk", "mv", "mr", "cmk", "cmr", "td", "tf")]
             + ["raw_mur", "raw_muk", "lnfw", "bq", "bk", "bo", "bv"])
VEC_COLS = len(VEC_ORDER) * 8

PB_RSTD = 0
PB_MU = 2 * CH
PB_ONESROW = 4 * CH
PB_ONESCOL = 4 * CH + 128
PARB_COLS = 4 * CH + 132


def pcol(name, l=None):
    return PMAP[name if l is None else f"{name}{l}"]


def build_program(debug_stage=None):
    nc = bass.Bass("TRN2", target_bir_lowering=False,
                   detect_race_conditions=DETECT_RACES)

    xt = nc.dram_tensor("xt", [C, T], F32, kind="ExternalInput").ap()
    out = nc.dram_tensor("out", [C, T], F32, kind="ExternalOutput").ap()
    tmkT = nc.dram_tensor("tmkT", [NL, C, C], F32, kind="ExternalInput").ap()
    tmvT = nc.dram_tensor("tmvT", [NL, C, C], F32, kind="ExternalInput").ap()
    tmrT = nc.dram_tensor("tmrT", [NL, C, C], F32, kind="ExternalInput").ap()
    tmoT = nc.dram_tensor("tmoT", [NL, C, C], F32, kind="ExternalInput").ap()
    cmkT = nc.dram_tensor("cmkT", [NL, C, FFND], F32, kind="ExternalInput").ap()
    cmvT = nc.dram_tensor("cmvT", [NL, FFND, C], F32, kind="ExternalInput").ap()
    cmrT = nc.dram_tensor("cmrT", [NL, C, C], F32, kind="ExternalInput").ap()
    wqT = nc.dram_tensor("wqT", [C, C], F32, kind="ExternalInput").ap()
    wkT = nc.dram_tensor("wkT", [C, C], F32, kind="ExternalInput").ap()
    wvT = nc.dram_tensor("wvT", [C, C], F32, kind="ExternalInput").ap()
    woT = nc.dram_tensor("woT", [C, C], F32, kind="ExternalInput").ap()
    frT = nc.dram_tensor("frT", [C, C], F32, kind="ExternalInput").ap()
    fkT = nc.dram_tensor("fkT", [C, C], F32, kind="ExternalInput").ap()
    fvT = nc.dram_tensor("fvT", [C, C], F32, kind="ExternalInput").ap()
    vecs = nc.dram_tensor("vecs", [128, VEC_COLS], F32, kind="ExternalInput").ap()
    cmk_s = nc.dram_tensor("cmk_s", [NL, C, FFND], BF16, kind="Internal").ap()
    cmv_s = nc.dram_tensor("cmv_s", [NL, FFND, C], BF16, kind="Internal").ap()
    ksp = nc.dram_tensor("ksp", [C, T], BF16, kind="Internal").ap()
    vtsp = nc.dram_tensor("vtsp", [128, 8 * 32 * 130], BF16, kind="Internal").ap()
    qsp = nc.dram_tensor("qsp", [C, T], BF16, kind="Internal").ap()

    import contextlib
    with contextlib.ExitStack() as stack:
        ent = stack.enter_context
        XRES = ent(nc.sbuf_tensor("XRES", [128, NCT * HALF], F32))
        ARENA = ent(nc.sbuf_tensor("ARENA", [128, 4 * NCT * 1024], BF16))
        B0 = ent(nc.sbuf_tensor("B0", [128, NCT * CH], F32))
        B1 = ent(nc.sbuf_tensor("B1", [128, NCT * CH], F32))
        B2 = ent(nc.sbuf_tensor("B2", [128, 2080], F32))
        B3 = ent(nc.sbuf_tensor("B3", [128, 2080], F32))
        XHATp = [ent(nc.sbuf_tensor(f"XHAT{i}", [128, NCT * (CH + 1)], BF16))
                 for i in range(2)]
        RWB = ent(nc.sbuf_tensor("RWB", [128, NCT * CH], BF16))
        SQB = ent(nc.sbuf_tensor("SQB", [128, NCT * CH], BF16))
        MXAp = [ent(nc.sbuf_tensor(f"MXA{i}", [128, NCT * CH], BF16))
                for i in range(2)]
        MXBp = [ent(nc.sbuf_tensor(f"MXB{i}", [128, NCT * CH], BF16))
                for i in range(2)]
        MXCp = [ent(nc.sbuf_tensor(f"MXC{i}", [128, NCT * CH], BF16))
                for i in range(2)]
        BCB = ent(nc.sbuf_tensor("BCB", [128, 2 * CH], BF16))
        PAR = ent(nc.sbuf_tensor("PAR", [128, PAR_USED], F32))
        PARB = ent(nc.sbuf_tensor("PARB", [128, PARB_COLS], BF16))
        PS = ent(nc.psum_tensor("PS", [128, 4096], F32))
        sem_t = ent(nc.semaphore("sem_t"))
        sem_v = ent(nc.semaphore("sem_v"))
        sem_s = ent(nc.semaphore("sem_s"))
        sem_g = ent(nc.semaphore("sem_g"))
        sem_gd = ent(nc.semaphore("sem_gd"))
        sem_sd = ent(nc.semaphore("sem_sd"))
        sem_ad = ent(nc.semaphore("sem_ad"))
        blk = ent(nc.Block())

        S = Sched(nc)
        for nm, h in [("t", sem_t), ("v", sem_v), ("s", sem_s),
                      ("g", sem_g), ("gd", sem_gd), ("sd", sem_sd),
                      ("ad", sem_ad)]:
            S.add_sem(nm, h)

        cx = dict(
            xt=xt, out=out, tmkT=tmkT, tmvT=tmvT, tmrT=tmrT, tmoT=tmoT,
            cmkT=cmkT, cmvT=cmvT, cmrT=cmrT, wqT=wqT, wkT=wkT, wvT=wvT,
            woT=woT, frT=frT, fkT=fkT, fvT=fvT, vecs=vecs,
            cmk_s=cmk_s, cmv_s=cmv_s, ksp=ksp, vtsp=vtsp, qsp=qsp,
            XRES=XRES, ARENA=ARENA, B0=B0, B1=B1, B2=B2, B3=B3,
            XHATp=XHATp, RWB=RWB, SQB=SQB, MXAp=MXAp, MXBp=MXBp,
            MXCp=MXCp, BCB=BCB, PAR=PAR, PARB=PARB, PS=PS,
            free={},
            bankfree=[("v", 0)] * 8,
        )
        emit_all(S, cx, debug_stage)

        @blk.tensor
        def _(e):
            for f in S.ops["tensor"]:
                f(e)

        @blk.vector
        def _(e):
            for f in S.ops["vector"]:
                f(e)

        @blk.scalar
        def _(e):
            for f in S.ops["scalar"]:
                f(e)

        @blk.gpsimd
        def _(e):
            for f in S.ops["gpsimd"]:
                f(e)

        @blk.sync
        def _(e):
            for f in S.ops["sync"]:
                f(e)

    return nc



def fget(cx, name):
    return cx["free"].get(name, ("v", 0))


def fset(cx, name, sem, val):
    cx["free"][name] = (sem, val)


def emit_all(S, cx, debug_stage):
    prologue(S, cx)
    passes = (0,)
    if debug_stage and debug_stage.startswith("1"):
        debug_stage = debug_stage[1:]
    else:
        passes = (0, 1)
    for p in passes:
        pass_begin(S, cx, p)
        src0 = list(cx["xres_ready"])
        for c in range(NCH):
            done_v = emit_ln_fp32_inplace(S, cx, c, src0)
        cx["xres_ready"] = [("v", done_v)]
        if debug_stage == "h0":
            pass_store(S, cx, p)
            continue
        stop = emit_rwkv_stack(S, cx, p, debug_stage)
        if stop:
            pass_store(S, cx, p)
            continue
        emit_lnf_add(S, cx, p)
        if debug_stage == "x1":
            pass_store(S, cx, p)
            continue
        emit_ffn(S, cx, p, site=8)
        if debug_stage == "x2":
            pass_store(S, cx, p)
            continue
        emit_attn(S, cx, p)
        if debug_stage == "x3":
            pass_store(S, cx, p)
            continue
        emit_ffn(S, cx, p, site=9)
        pass_store(S, cx, p)

    S.op("sync", lambda e: None, waits=[("sd", S.m("sd"))])


def prologue(S, cx):
    PAR, PARB, B0 = cx["PAR"], cx["PARB"], cx["B0"]

    S.op("sync", lambda e: e.dma_start(B0[:, 0:VEC_COLS], cx["vecs"]),
         inc="sd", inc_n=16)
    lded = S.m("sd")

    def rawc(name, l=None):
        i = VEC_ORDER.index(name if l is None else f"{name}{l}")
        return B0[:, i * 8:(i + 1) * 8]

    def c(name, l=None):
        col = pcol(name, l)
        return PAR[:, col:col + 8]

    for pp in (0, 32, 64):
        S.op("vector", lambda e, pp=pp: e.memset(
            PARB[pp:pp + 1, PB_ONESROW:PB_ONESROW + 128], 1.0), inc="v")
    S.op("vector", lambda e: e.memset(PARB[:, PB_ONESCOL:PB_ONESCOL + 1], 1.0),
         inc="v")
    of = PMAP["onesf"]
    S.op("vector", lambda e: e.memset(PAR[0:1, of:of + 128], 1.0), inc="v")
    S.op("vector", lambda e: e.memset(PAR[32:33, of:of + 128], 1.0), inc="v")
    ec = PMAP["epsc"]
    S.op("vector", lambda e: e.memset(PAR[:, ec:ec + 1], EPS), inc="v")
    S.op("vector", lambda e: e.memset(PAR[:, PMAP["G"]:PMAP["G"] + NL * 8], 0.0),
         inc="v")
    S.op("vector", lambda e: e.memset(PAR[:, PMAP["H"]:PMAP["H"] + NL * 8], 0.0),
         inc="v")
    S.op("vector", lambda e: e.memset(PAR[:, PMAP["halo"]:PMAP["halo"] + 80], 0.0),
         inc="v")

    for l in range(NL):
        for (a, b, mix, w) in [("Ak", "Bk", "raw_mk", "raw_ln1w"),
                               ("Av", "Bv", "raw_mv", "raw_ln1w"),
                               ("Ar", "Br", "raw_mr", "raw_ln1w"),
                               ("Ack", "Bck", "raw_cmk", "raw_ln2w"),
                               ("Acr", "Bcr", "raw_cmr", "raw_ln2w")]:
            S.op("vector", lambda e, a=a, mix=mix, w=w, l=l: e.tensor_tensor(
                c(a, l), rawc(mix, l), rawc(w, l), ALU.mult),
                waits=[("sd", lded)], inc="v")
            S.op("vector", lambda e, a=a, b=b, w=w, l=l: e.tensor_tensor(
                c(b, l), rawc(w, l), c(a, l), ALU.subtract), inc="v")
        S.op("scalar", lambda e, l=l: e.activation(
            c("adec", l), rawc("raw_td", l), AF.Exp), waits=[("sd", lded)],
            inc="s")
        S.op("scalar", lambda e, l=l: e.activation(
            c("adec", l), c("adec", l), AF.Exp, scale=-1.0), inc="s")
        S.op("scalar", lambda e, l=l: e.activation(
            c("Eu", l), rawc("raw_tf", l), AF.Exp, scale=-1.0), inc="s")
    for (a, b, raw) in [("fAr", "fBr", "raw_mur"), ("fAk", "fBk", "raw_muk")]:
        S.op("vector", lambda e, a=a, raw=raw: e.tensor_copy(c(a), rawc(raw)),
             waits=[("sd", lded)], inc="v")
        S.op("vector", lambda e, b=b, raw=raw: e.tensor_scalar(
            c(b), rawc(raw), -1.0, 1.0, ALU.mult, ALU.add), inc="v")
    for nm in ("lnfw", "bq", "bk", "bo", "bv"):
        S.op("vector", lambda e, nm=nm: e.tensor_copy(c(nm), rawc(nm)),
             waits=[("sd", lded)], inc="v")
    fset(cx, "B0", "v", S.m("v"))
    cx["params_v"] = S.m("v")
    cx["params_s"] = S.m("s")

    cx["cmprep"] = {}
    for l in range(NL):
        S.op("gpsimd", lambda e, l=l: e.dma_start(cx["cmk_s"][l], cx["cmkT"][l]),
             inc="gd", inc_n=16)
        S.op("gpsimd", lambda e, l=l: e.dma_start(cx["cmv_s"][l], cx["cmvT"][l]),
             inc="gd", inc_n=16)
        cx["cmprep"][l] = S.m("gd")


def pass_begin(S, cx, p):
    XRES = cx["XRES"]
    cx["pass"] = p
    src = cx["xt"].rearrange("(ct p) t -> p ct t", p=128)[:, :, p * HALF:(p + 1) * HALF]
    waits = [fget(cx, "XRES")]
    S.op("sync", lambda e: e.dma_start(r3(XRES[:], NCT), src),
         waits=waits, inc="sd", inc_n=16)
    cx["xres_ready"] = [("sd", S.m("sd"))]


def pass_store(S, cx, p):
    XRES = cx["XRES"]
    dst = cx["out"].rearrange("(ct p) t -> p ct t", p=128)[:, :, p * HALF:(p + 1) * HALF]
    S.op("sync", lambda e: e.dma_start(dst, r3(XRES[:], NCT)),
         waits=list(cx["xres_ready"]), inc="sd", inc_n=16)
    fset(cx, "XRES", "sd", S.m("sd"))


def emit_stats_pre(S, cx, src3, src_waits, par):
    SQB = cx["SQB"]
    XHAT = cx["XHATp"][par]
    xhv = r3(XHAT[:, 0:NCT * CH], NCT)
    S.op("scalar", lambda e: e.activation(xhv, src3, AF.Copy),
         waits=list(src_waits) + [fget(cx, f"XHAT{par}")], inc="s")
    cast_s = S.m("s")
    S.op("vector", lambda e: e.tensor_tensor(r3(SQB[:], NCT), xhv, xhv,
                                             ALU.mult),
         waits=[("s", cast_s), fget(cx, "SQB")], inc="v")
    return dict(cast_s=cast_s, sq_v=S.m("v"))


def emit_stats(S, cx, src3, src_waits, pbank, pre=None):
    PS, SQB, PARB = cx["PS"], cx["SQB"], cx["PARB"]
    par = pbank - 6
    XHAT = cx["XHATp"][par]
    SQF = SQB[:].bitcast(F32)
    bank = PS[:, pbank * 512:(pbank + 1) * 512]

    if pre is None:
        pre = emit_stats_pre(S, cx, src3, src_waits, par)
    cast_s, sq_v = pre["cast_s"], pre["sq_v"]

    onescol = PARB[:, PB_ONESCOL:PB_ONESCOL + 1]

    def fsum(t):
        inst = None
        for ct in range(NCT):
            inst = t.matmul(bank[0:1, 0:CH], onescol,
                            r3(XHAT[:, 0:NCT * CH], NCT)[:, ct],
                            start=(ct == 0), stop=(ct == NCT - 1))
        for ct in range(NCT):
            inst = t.matmul(bank[0:1, CH:2 * CH], onescol,
                            r3(SQB[:], NCT)[:, ct],
                            start=(ct == 0), stop=(ct == NCT - 1))
        return inst
    S.op("tensor", fsum, waits=[("s", cast_s), ("v", sq_v),
                                cx["bankfree"][pbank],
                                ("v", cx["params_v"]), ("s", cx["params_s"])],
         inc="t")
    stats_t = S.m("t")
    fset(cx, f"XHAT{par}", "t", stats_t)
    fset(cx, "SQB", "t", stats_t)

    mu_f = SQF[0:1, 0:CH]
    var_f = SQF[0:1, CH:2 * CH]
    rstd_f = SQF[0:1, 2 * CH:3 * CH]
    scr_f = SQF[0:1, 3 * CH:4 * CH]
    S.op("vector", lambda e: e.tensor_scalar(mu_f, bank[0:1, 0:CH], INV_C,
                                             None, ALU.mult),
         waits=[("t", stats_t)], inc="v")
    S.op("vector", lambda e: e.tensor_scalar(var_f, bank[0:1, CH:2 * CH],
                                             INV_C, None, ALU.mult), inc="v")
    S.op("vector", lambda e: e.tensor_tensor(scr_f, mu_f, mu_f, ALU.mult),
         inc="v")
    S.op("vector", lambda e: e.tensor_tensor(var_f, var_f, scr_f,
                                             ALU.subtract), inc="v")
    fin_v = S.m("v")

    epsap = cx["PAR"][0:1, PMAP["epsc"]:PMAP["epsc"] + 1]
    S.op("scalar", lambda e: e.activation(scr_f, var_f, AF.Ln, bias=epsap),
         waits=[("v", fin_v)], inc="s")
    S.op("scalar", lambda e: e.activation(rstd_f, scr_f, AF.Exp, scale=-0.5),
         inc="s")
    S.op("scalar", lambda e: e.activation(PARB[0:1, PB_RSTD + par * CH:
                                               PB_RSTD + par * CH + CH],
                                          rstd_f, AF.Copy), inc="s")
    S.op("scalar", lambda e: e.activation(PARB[0:1, PB_MU + par * CH:
                                               PB_MU + par * CH + CH],
                                          mu_f, AF.Copy), inc="s")
    rows_s = S.m("s")
    fset(cx, "SQB", "s", rows_s)
    return dict(rows_s=rows_s, fin_v=fin_v, pbank=pbank, par=par,
                mu_f=mu_f, rstd_f=rstd_f)


def emit_bcast(S, cx, st, fp32=False):
    PS, PARB, PAR = cx["PS"], cx["PARB"], cx["PAR"]
    par = st["par"]
    bank = PS[:, st["pbank"] * 512:(st["pbank"] + 1) * 512]
    if fp32:
        lhs_r = PAR[0:1, PMAP["onesf"]:PMAP["onesf"] + 128]
        lhs_m = lhs_r
        rstd_src = st["rstd_f"]
        mu_src = st["mu_f"]
        w = [("s", st["rows_s"]), ("v", st["fin_v"])]
    else:
        lhs_r = PARB[0:1, PB_ONESROW:PB_ONESROW + 128]
        lhs_m = lhs_r
        rstd_src = PARB[0:1, PB_RSTD + par * CH:PB_RSTD + par * CH + CH]
        mu_src = PARB[0:1, PB_MU + par * CH:PB_MU + par * CH + CH]
        w = [("s", st["rows_s"])]

    def fbc(e):
        e.matmul(bank[:, 0:CH], lhs_r, rstd_src, start=True, stop=True)
        return e.matmul(bank[:, CH:2 * CH], lhs_m, mu_src, start=True,
                        stop=True)
    S.op("tensor", fbc, waits=w, inc="t")
    return S.m("t")


def emit_ln_fp32_inplace(S, cx, c, src_waits):
    XRES, PS = cx["XRES"], cx["PS"]
    par = c % 2
    pbank = 6 + par
    tmp = [cx["B0"], cx["B1"]][par]
    tname = ["B0", "B1"][par]
    xr = r3(XRES[:], NCT)[:, :, c * CH:(c + 1) * CH]
    st = emit_stats(S, cx, xr, src_waits, pbank)
    bc_t = emit_bcast(S, cx, st, fp32=True)
    fset(cx, "SQB", "t", bc_t)
    bank = PS[:, pbank * 512:(pbank + 1) * 512]
    mu_bc = bank[:, CH:2 * CH].unsqueeze(1).broadcast_to([128, NCT, CH])
    rstd_bc = bank[:, 0:CH].unsqueeze(1).broadcast_to([128, NCT, CH])
    b0r = r3(tmp[:, 0:NCT * CH], NCT)
    S.op("vector", lambda e: e.tensor_tensor(b0r, xr, mu_bc, ALU.subtract),
         waits=[("t", bc_t), fget(cx, tname)], inc="v")
    S.op("vector", lambda e: e.tensor_tensor(xr, b0r, rstd_bc, ALU.mult),
         inc="v")
    done_v = S.m("v")
    cx["bankfree"][pbank] = ("v", done_v)
    fset(cx, tname, "v", done_v)
    return done_v


def emit_ln_prep_pre(S, cx, c):
    xr = r3(cx["XRES"][:], NCT)[:, :, c * CH:(c + 1) * CH]
    return emit_stats_pre(S, cx, xr, cx["xres_ready"], c % 2)


def emit_ln_prep_stats(S, cx, c, site, pre=None):
    XRES, PS = cx["XRES"], cx["PS"]
    par = c % 2
    pbank = 6 + par
    XHAT = cx["XHATp"][par]
    BCB = cx["BCB"]
    xr = r3(XRES[:], NCT)[:, :, c * CH:(c + 1) * CH]
    st = emit_stats(S, cx, xr, cx["xres_ready"], pbank, pre=pre)
    bc_t = emit_bcast(S, cx, st, fp32=False)
    bcsrc = r3(PS[:, pbank * 512:(pbank + 1) * 512][:, 0:2 * CH], 2)
    S.op("scalar", lambda e: e.activation(r3(BCB[:], 2), bcsrc, AF.Copy),
         waits=[("t", bc_t), fget(cx, "BCB")], inc="s")
    bcb_s = S.m("s")
    cx["bankfree"][pbank] = ("s", bcb_s)
    return dict(par=par, site=site, bcb_s=bcb_s,
                xh=r3(XHAT[:, 0:NCT * CH], NCT), xflat=XHAT[:, 0:NCT * CH],
                mix={})


def emit_ln_prep_finish(S, cx, rec, mixes):
    PAR, RWB = cx["PAR"], cx["RWB"]
    par, site, xh = rec["par"], rec["site"], rec["xh"]
    BCB = cx["BCB"]
    mu_bc = BCB[:, CH:2 * CH].unsqueeze(1).broadcast_to([128, NCT, CH])
    rstd_bc = BCB[:, 0:CH].unsqueeze(1).broadcast_to([128, NCT, CH])
    S.op("vector", lambda e: e.tensor_tensor(xh, xh, mu_bc, ALU.subtract),
         waits=[("s", rec["bcb_s"])], inc="v")
    S.op("vector", lambda e: e.tensor_tensor(xh, xh, rstd_bc, ALU.mult),
         inc="v")
    xhat_v = S.m("v")
    fset(cx, "BCB", "v", xhat_v)
    rec["xhat_v"] = xhat_v
    if site is not None:
        hc = PMAP["halo"] + site * 8
        S.op("vector", lambda e: e.tensor_copy(
            RWB[:, 1:NCT * CH], rec["xflat"][:, 0:NCT * CH - 1]),
            waits=[fget(cx, "RWB")], inc="v")
        S.op("vector", lambda e: e.tensor_copy(
            r3(RWB[:], NCT)[:, :, 0:1], PAR[:, hc:hc + 8].bitcast(BF16)
            [:, 0:NCT].rearrange("p (b w) -> p b w", b=NCT)), inc="v")
        S.op("vector", lambda e: e.tensor_copy(
            PAR[:, hc:hc + 8].bitcast(BF16)[:, 0:NCT]
            .rearrange("p (b w) -> p b w", b=NCT),
            xh[:, :, CH - 1:CH]), inc="v")
    sh3 = r3(RWB[:], NCT)
    slot = {"A": "MXAp", "B": "MXBp", "C": "MXCp"}
    for (Aname, Bname, l, which) in mixes:
        dst = r3(cx[slot[which]][par][:], NCT)
        mv = emit_mix(S, cx, Aname, Bname, l, dst, f"MX{which}{par}", xh, sh3)
        rec["mix"][which] = (dst, mv)
    if mixes:
        fset(cx, "RWB", "v", S.m("v"))
    return rec


def emit_ln_prep(S, cx, c, site, mixes):
    return emit_ln_prep_finish(S, cx, emit_ln_prep_stats(S, cx, c, site),
                               mixes)


def emit_mix(S, cx, Aname, Bname, l, dst, dst_name, xh, sh3):
    PAR = cx["PAR"]
    SC = r3(cx["SQB"][:], NCT)
    Ac = pcol(Aname, l) if l is not None else pcol(Aname)
    Bc = pcol(Bname, l) if l is not None else pcol(Bname)
    A_bc = r3(PAR[:, Ac:Ac + 8], NCT).broadcast_to([128, NCT, CH])
    B_bc = r3(PAR[:, Bc:Bc + 8], NCT).broadcast_to([128, NCT, CH])
    S.op("vector", lambda e: e.tensor_tensor(dst, sh3, B_bc, ALU.mult),
         waits=[fget(cx, dst_name)], inc="v")
    S.op("vector", lambda e: e.tensor_tensor(SC, xh, A_bc, ALU.mult),
         waits=[fget(cx, "SQB")], inc="v")
    S.op("vector", lambda e: e.tensor_tensor(dst, dst, SC, ALU.add), inc="v")
    mix_v = S.m("v")
    fset(cx, "SQB", "v", mix_v)
    return mix_v


def emit_lnf_add(S, cx, p):
    XRES, PAR, PS = cx["XRES"], cx["PAR"], cx["PS"]
    lnfw = pcol("lnfw")
    src0 = list(cx["xres_ready"])
    for c in range(NCH):
        par = c % 2
        pbank = 6 + par
        tmp = [cx["B0"], cx["B1"]][par]
        tname = ["B0", "B1"][par]
        xbuf = [cx["B3"], cx["B2"]][par]
        xname = ["B3", "B2"][par]
        xr = r3(XRES[:], NCT)[:, :, c * CH:(c + 1) * CH]
        st = emit_stats(S, cx, xr, src0, pbank)
        bc_t = emit_bcast(S, cx, st, fp32=True)
        fset(cx, "SQB", "t", bc_t)
        bank = PS[:, pbank * 512:(pbank + 1) * 512]
        mu_bc = bank[:, CH:2 * CH].unsqueeze(1).broadcast_to([128, NCT, CH])
        rstd_bc = bank[:, 0:CH].unsqueeze(1).broadcast_to([128, NCT, CH])
        src = cx["xt"].rearrange("(ct p) t -> p ct t", p=128)[
            :, :, p * HALF + c * CH: p * HALF + (c + 1) * CH]
        b3r = r3(xbuf[:, 0:NCT * CH], NCT)
        S.op("sync", lambda e, src=src, b3r=b3r: e.dma_start(b3r, src),
             waits=[fget(cx, xname)], inc="sd", inc_n=16)
        xin_d = S.m("sd")
        b0r = r3(tmp[:, 0:NCT * CH], NCT)
        S.op("vector", lambda e, b0r=b0r, xr=xr, mu_bc=mu_bc: e.tensor_tensor(
            b0r, xr, mu_bc, ALU.subtract),
            waits=[("t", bc_t), fget(cx, tname)], inc="v")
        S.op("vector", lambda e, b0r=b0r, rstd_bc=rstd_bc: e.tensor_tensor(
            b0r, b0r, rstd_bc, ALU.mult), inc="v")
        bcd = S.m("v")
        cx["bankfree"][pbank] = ("v", bcd)

        def fn(e, b0r=b0r, xr=xr, b3r=b3r):
            inst = None
            for ct in range(NCT):
                inst = e.scalar_tensor_tensor(
                    xr[:, ct], b0r[:, ct], PAR[:, lnfw + ct:lnfw + ct + 1],
                    b3r[:, ct], ALU.mult, ALU.add)
            return inst
        S.op("vector", fn, waits=[("sd", xin_d)], inc="v")
        done_v = S.m("v")
        fset(cx, tname, "v", done_v)
        fset(cx, xname, "v", done_v)
    cx["xres_ready"] = [("v", done_v)]


PROJ_ROUNDS = [(0, 3, 0), (3, 6, 3), (6, 8, 0)]


def ps_banks(cx, blo, n):
    return r3(cx["PS"][:, blo * 512:(blo + n) * 512], n)[:, :, 0:CH]


def emit_proj(S, cx, warena_m, rhs3, rhs_wait, evac, wa3=None):
    ARENA, PS = cx["ARENA"], cx["PS"]
    if wa3 is None:
        wa3 = r3(ARENA[:], 4)[:, warena_m].rearrange("p (ct o) -> p ct o",
                                                     ct=NCT)
    for r, (glo, ghi, blo) in enumerate(PROJ_ROUNDS):
        def fn(t, glo=glo, ghi=ghi, blo=blo):
            inst = None
            for g in range(glo, ghi):
                bank = PS[:, (blo + g - glo) * 512:(blo + g - glo) * 512 + CH]
                for ct in range(NCT):
                    inst = t.matmul(bank, wa3[:, ct, g * 128:(g + 1) * 128],
                                    rhs3[:, ct], start=(ct == 0),
                                    stop=(ct == NCT - 1))
            return inst
        waits = [rhs_wait, cx["arena_ready"]] + [cx["bankfree"][blo + i]
                                                 for i in range(ghi - glo)]
        S.op("tensor", fn, waits=waits, inc="t")
        evac(r, glo, ghi, blo, S.m("t"))


def _dmp(S, cx, region, src, waits, cast=False, nct=NCT):
    dst = cx["out"].rearrange("(ct p) t -> p ct t", p=128)[
        :, 0:nct, region * CH:(region + 1) * CH]
    eng, inc = ("gpsimd", "gd") if cast else ("sync", "sd")
    S.op(eng, lambda e, dst=dst, src=src: e.dma_start(dst, src),
         waits=waits, inc=inc, inc_n=16)


def emit_rwkv_stack(S, cx, p, debug_stage):
    for l in range(NL):
        emit_rwkv_tm(S, cx, p, l)
        if debug_stage == f"L{l}tm":
            return True
        emit_rwkv_cm(S, cx, p, l)
        if debug_stage == f"L{l}":
            return True
    return False


def load_tm_arena(S, cx, l):
    ARENA = cx["ARENA"]
    waits = [("t", S.m("t"))]
    for m, w in enumerate([cx["tmkT"], cx["tmvT"], cx["tmrT"], cx["tmoT"]]):
        srcw = w[l].rearrange("(ct p) o -> p ct o", p=128)
        dst = r3(ARENA[:], 4)[:, m].rearrange("p (ct o) -> p ct o", ct=NCT)
        S.op("gpsimd", lambda e, dst=dst, srcw=srcw: e.dma_start(dst, srcw),
             waits=waits if m == 0 else [], inc="gd", inc_n=16)
    cx["arena_ready"] = ("gd", S.m("gd"))


def emit_rwkv_tm(S, cx, p, l):
    PS, PAR = cx["PS"], cx["PAR"]
    B0, B1, B2, B3 = cx["B0"], cx["B1"], cx["B2"], cx["B3"]
    XRES = cx["XRES"]
    load_tm_arena(S, cx, l)

    b0r = r3(B0[:], NCT)
    b1r = r3(B1[:], NCT)
    b2r = r3(B2[:, 0:NCT * (CH + 1)], NCT)
    b3r = r3(B3[:, 0:NCT * (CH + 1)], NCT)
    rw = r3(cx["RWB"][:], NCT)

    mixes = [("Ak", "Bk", l, "A"), ("Av", "Bv", l, "B"), ("Ar", "Br", l, "C")]
    preps = [emit_ln_prep(S, cx, 0, l, mixes)]
    for c in range(NCH):
        P = preps[c]
        par = P["par"]
        mxa, mixk_v = P["mix"]["A"]
        mxb, mixv_v = P["mix"]["B"]
        mxc, mixr_v = P["mix"]["C"]

        def ev_ek(r, glo, ghi, blo, mm_t):
            S.op("scalar", lambda e, glo=glo, ghi=ghi, blo=blo: e.activation(
                b0r[:, glo:ghi], ps_banks(cx, blo, ghi - glo), AF.Exp),
                waits=[("t", mm_t), fget(cx, "B0")], inc="s")
            for i in range(ghi - glo):
                cx["bankfree"][blo + i] = ("s", S.m("s"))
        emit_proj(S, cx, 0, mxa, ("v", mixk_v), ev_ek)
        ek_s = S.m("s")
        fset(cx, "B0", "s", ek_s)
        fset(cx, f"MXA{par}", "t", S.m("t"))
        if c + 1 < NCH:
            pre_next = emit_ln_prep_pre(S, cx, c + 1)

        def ev_v(r, glo, ghi, blo, mm_t):
            S.op("scalar", lambda e, glo=glo, ghi=ghi, blo=blo: e.activation(
                b1r[:, glo:ghi], ps_banks(cx, blo, ghi - glo), AF.Copy),
                waits=[("t", mm_t), fget(cx, "B1")], inc="s")
            for i in range(ghi - glo):
                cx["bankfree"][blo + i] = ("s", S.m("s"))
        emit_proj(S, cx, 1, mxb, ("v", mixv_v), ev_v)
        v_s = S.m("s")
        fset(cx, "B1", "s", v_s)
        fset(cx, f"MXB{par}", "t", S.m("t"))

        def ev_r(r, glo, ghi, blo, mm_t):
            S.op("scalar", lambda e, glo=glo, ghi=ghi, blo=blo: e.activation(
                b2r[:, glo:ghi, 0:CH], ps_banks(cx, blo, ghi - glo), AF.Exp,
                scale=-1.0),
                waits=[("t", mm_t), fget(cx, "B2")], inc="s")
            for i in range(ghi - glo):
                cx["bankfree"][blo + i] = ("s", S.m("s"))

        S.op("vector", lambda e: e.tensor_tensor(b1r, b0r, b1r, ALU.mult),
             waits=[("s", ek_s), ("s", v_s)], inc="v")
        c1_v = S.m("v")
        gcol = PMAP["G"] + l * 8
        hcol = PMAP["H"] + l * 8
        S.op("vector", lambda e: e.tensor_copy(
            b2r[:, :, 0:1], r3(PAR[:, gcol:gcol + 8], NCT)),
            waits=[fget(cx, "B2")], inc="v")
        S.op("vector", lambda e: e.tensor_copy(
            b3r[:, :, 0:1], r3(PAR[:, hcol:hcol + 8], NCT)),
            waits=[fget(cx, "B3")], inc="v")

        def scans(e):
            inst = None
            for ct in range(NCT):
                a_ap = PAR[:, pcol("adec", l) + ct: pcol("adec", l) + ct + 1]
                inst = e.tensor_tensor_scan(
                    b2r[:, ct, 1:1 + CH], a_ap.broadcast_to([128, CH]),
                    b1r[:, ct], PAR[:, gcol + ct:gcol + ct + 1],
                    ALU.mult, ALU.add)
            return inst
        S.op("vector", scans, waits=[("v", c1_v)], inc="v")

        def scand(e):
            inst = None
            for ct in range(NCT):
                a_ap = PAR[:, pcol("adec", l) + ct: pcol("adec", l) + ct + 1]
                inst = e.tensor_tensor_scan(
                    b3r[:, ct, 1:1 + CH], a_ap.broadcast_to([128, CH]),
                    b0r[:, ct], PAR[:, hcol + ct:hcol + ct + 1],
                    ALU.mult, ALU.add)
            return inst
        S.op("vector", scand, inc="v")
        S.op("vector", lambda e: e.tensor_copy(
            r3(PAR[:, gcol:gcol + 8], NCT), b2r[:, :, CH:CH + 1]), inc="v")
        S.op("vector", lambda e: e.tensor_copy(
            r3(PAR[:, hcol:hcol + 8], NCT), b3r[:, :, CH:CH + 1]), inc="v")

        def num(e):
            inst = None
            for ct in range(NCT):
                eu = PAR[:, pcol("Eu", l) + ct: pcol("Eu", l) + ct + 1]
                inst = e.scalar_tensor_tensor(b1r[:, ct], b2r[:, ct, 0:CH], eu,
                                              b1r[:, ct], ALU.mult, ALU.add)
            return inst
        S.op("vector", num, inc="v")
        num_v = S.m("v")

        def den(e):
            inst = None
            for ct in range(NCT):
                eu = PAR[:, pcol("Eu", l) + ct: pcol("Eu", l) + ct + 1]
                inst = e.scalar_tensor_tensor(b0r[:, ct], b3r[:, ct, 0:CH], eu,
                                              b0r[:, ct], ALU.mult, ALU.add)
            return inst
        S.op("vector", den, inc="v")
        den_v = S.m("v")
        fset(cx, "B2", "v", num_v)
        fset(cx, "B3", "v", den_v)

        emit_proj(S, cx, 2, mxc, ("v", mixr_v), ev_r)
        esm_s = S.m("s")
        fset(cx, "B2", "s", esm_s)
        fset(cx, f"MXC{par}", "t", S.m("t"))

        if c + 1 < NCH:
            pr_next = emit_ln_prep_stats(S, cx, c + 1, l, pre=pre_next)

        S.op("vector", lambda e: e.tensor_scalar(
            b2r[:, :, 0:CH], b2r[:, :, 0:CH], 1.0, None, ALU.add),
            waits=[("s", esm_s)], inc="v")
        S.op("vector", lambda e: e.tensor_tensor(
            b0r, b0r, b2r[:, :, 0:CH], ALU.mult),
            waits=[("v", den_v)], inc="v")
        d2_v = S.m("v")
        fset(cx, "B2", "v", d2_v)
        S.op("scalar", lambda e: e.activation(b3r[:, :, 0:CH], b0r, AF.Ln),
             waits=[("v", d2_v)], inc="s")
        S.op("scalar", lambda e: e.activation(b0r, b3r[:, :, 0:CH], AF.Exp,
                                              scale=-1.0), inc="s")
        rec_s = S.m("s")
        fset(cx, "B3", "s", rec_s)
        S.op("vector", lambda e: e.tensor_tensor(rw, b1r, b0r, ALU.mult),
             waits=[("s", rec_s), ("v", num_v), fget(cx, "RWB")], inc="v")
        rw_v = S.m("v")
        fset(cx, "RWB", "v", rw_v)
        fset(cx, "B0", "v", rw_v)
        fset(cx, "B1", "v", rw_v)

        xr = r3(XRES[:], NCT)[:, :, c * CH:(c + 1) * CH]

        def ev_o(r, glo, ghi, blo, mm_t, xr=xr):
            S.op("vector", lambda e, glo=glo, ghi=ghi, blo=blo, xr=xr:
                 e.tensor_tensor(xr[:, glo:ghi], xr[:, glo:ghi],
                                 ps_banks(cx, blo, ghi - glo), ALU.add),
                 waits=[("t", mm_t)] + cx["xres_ready"], inc="v")
            for i in range(ghi - glo):
                cx["bankfree"][blo + i] = ("v", S.m("v"))
        emit_proj(S, cx, 3, rw, ("v", rw_v), ev_o)
        oadd_v = S.m("v")
        fset(cx, "RWB", "t", S.m("t"))
        cx["xres_ready"] = [("v", oadd_v)]
        if c + 1 < NCH:
            preps.append(emit_ln_prep_finish(S, cx, pr_next, mixes))


def emit_rwkv_cm(S, cx, p, l):
    PS, PAR = cx["PS"], cx["PAR"]
    B0, B1, B2, B3 = cx["B0"], cx["B1"], cx["B2"], cx["B3"]
    ARENA, XRES = cx["ARENA"], cx["XRES"]

    cmr_dst = ARENA[:, 24576:32768].rearrange("p (ct o) -> p ct o", ct=NCT)
    cmr3 = ARENA[:, 24576:32768].rearrange("p (ct o) -> p ct o", ct=NCT)
    tm_done_t = S.m("t")
    S.op("gpsimd", lambda e: e.dma_start(
        cmr_dst, cx["cmrT"][l].rearrange("(ct p) o -> p ct o", p=128)),
        waits=[("t", tm_done_t)], inc="gd", inc_n=16)
    cmr_gd = S.m("gd")
    for nm in ("CKSLOT0", "CKSLOT1", "CV0", "CV1"):
        fset(cx, nm, "t", tm_done_t)

    b0r = r3(B0[:], NCT)
    b1r = r3(B1[:], NCT)
    b2r = r3(B2[:, 0:NCT * CH], NCT)
    sq_a = cx["B2"][:, 0:NCT * CH].bitcast(BF16)
    sq_b = cx["B3"][:, 0:NCT * CH].bitcast(BF16)

    def sq_slice(s):
        buf = sq_a if s < 4 else sq_b
        return r3(buf[:, (s % 4) * 1024:(s % 4) * 1024 + 1024], 4)

    prep = cx["cmprep"][l]
    fset(cx, "SQ", "s", S.m("s"))
    mixes = [("Ack", "Bck", l, "A"), ("Acr", "Bcr", l, "C")]
    preps = [emit_ln_prep(S, cx, 0, 4 + l, mixes)]
    for c in range(NCH):
        P = preps[c]
        par = P["par"]
        mxa, mixk_v = P["mix"]["A"]
        mxc, mixr_v = P["mix"]["C"]

        last_sq_v = 0
        for s in range(8):
            slot = s % 2
            slot3 = r3(ARENA[:, slot * 4096:(slot + 1) * 4096], NCT)
            srcw = cx["cmk_s"][l].rearrange("(ct p) f -> p ct f", p=128)[
                :, :, s * 512:(s + 1) * 512]
            dq, dsem = [("sync", "sd"), ("scalar", "ad")][s % 2]
            S.op(dq, lambda e, slot3=slot3, srcw=srcw: e.dma_start(
                slot3, srcw),
                waits=[("gd", prep), fget(cx, f"CKSLOT{slot}")],
                inc=dsem, inc_n=16)
            sl_sd = (dsem, S.m(dsem))
            blo = 3 * (s % 2)

            def fkk(t, slot3=slot3, blo=blo, mxa=mxa):
                inst = None
                for fi in range(4):
                    if fi < 3:
                        bank = PS[:, (blo + fi) * 512:(blo + fi) * 512 + CH]
                    else:
                        bank = PS[:, blo * 512 + CH:blo * 512 + 2 * CH]
                    for ct in range(NCT):
                        st = (ct == 0)
                        inst = t.matmul(bank,
                                        slot3[:, ct, fi * 128:(fi + 1) * 128],
                                        mxa[:, ct], start=st,
                                        stop=(ct == NCT - 1))
                return inst
            S.op("tensor", fkk, waits=[sl_sd, ("v", mixk_v)] +
                 [cx["bankfree"][blo + i] for i in range(3)], inc="t")
            kk_t = S.m("t")
            fset(cx, f"CKSLOT{slot}", "t", kk_t)
            if cx.get("dbgcm") == (l, c) and s == 0:
                stg = r3(cx["RWB"][:], NCT)
                S.op("scalar", lambda e, stg=stg: e.activation(
                    stg[:, 0:3], r3(PS[:, 0:3 * 512], 3)[:, :, 0:CH],
                    AF.Copy), waits=[("t", kk_t)], inc="s")
                S.op("scalar", lambda e, stg=stg: e.activation(
                    stg[:, 3], PS[:, CH:2 * CH], AF.Copy), inc="s")
                _dmp(S, cx, 14, stg[:, 0:4], [("s", S.m("s"))], cast=True,
                     nct=4)
                for i in range(3):
                    cx["bankfree"][i] = ("gd", S.m("gd"))
            sqv = sq_slice(s)

            def evk(e, sqv=sqv, blo=blo):
                src = r3(PS[:, blo * 512:(blo + 3) * 512], 3)[:, :, 0:CH]
                e.tensor_scalar(sqv[:, 0:3], src, 0.0, None, ALU.max)
                return e.tensor_scalar(
                    sqv[:, 3], PS[:, blo * 512 + CH:blo * 512 + 2 * CH], 0.0,
                    None, ALU.max)
            S.op("vector", evk, waits=[("t", kk_t), fget(cx, "SQ")], inc="v")
            S.op("vector", lambda e, sqv=sqv: e.tensor_tensor(
                sqv, sqv, sqv, ALU.mult), inc="v")
            last_sq_v = S.m("v")
            for i in range(3):
                cx["bankfree"][blo + i] = ("v", last_sq_v)

        if c + 1 < NCH:
            pre_next = emit_ln_prep_pre(S, cx, c + 1)

        def ev_rr(r, glo, ghi, blo, mm_t):
            S.op("scalar", lambda e, glo=glo, ghi=ghi, blo=blo: e.activation(
                b0r[:, glo:ghi], ps_banks(cx, blo, ghi - glo), AF.Exp,
                scale=-1.0),
                waits=[("t", mm_t), fget(cx, "B0")], inc="s")
            for i in range(ghi - glo):
                cx["bankfree"][blo + i] = ("s", S.m("s"))
        cx["arena_ready"] = ("gd", cmr_gd)
        emit_proj(S, cx, None, mxc, ("v", mixr_v), ev_rr, wa3=cmr3)
        e2m_s = S.m("s")
        fset(cx, "B0", "s", e2m_s)
        fset(cx, f"MXC{par}", "t", S.m("t"))
        dwait1, dwait2 = [], []
        if cx.get("dbgcm") == (l, c):
            _dmp(S, cx, 9, b0r, [("s", e2m_s)])
            dwait2 = [("sd", S.m("sd"))]
        S.op("vector", lambda e: e.tensor_scalar(b0r, b0r, 1.0, None, ALU.add),
             waits=[("s", e2m_s)] + dwait2, inc="v")
        d_v = S.m("v")

        def cv_slot(sl):
            return ARENA[:, 8192 + sl * 8192:8192 + (sl + 1) * 8192]\
                .rearrange("p (ft o) -> p ft o", ft=8)

        def cv_load(s):
            sl = s % 2
            srcw = cx["cmv_s"][l][s * 1024:(s + 1) * 1024].rearrange(
                "(ft p) o -> p ft o", p=128)
            dq, dsem = [("gpsimd", "gd"), ("gpsimd", "gd"), ("sync", "sd"),
                        ("scalar", "ad")][s]
            S.op(dq, lambda e, sl=sl, srcw=srcw: e.dma_start(
                cv_slot(sl), srcw),
                waits=[("gd", prep), fget(cx, f"CV{sl}")], inc=dsem, inc_n=16)
            return (dsem, S.m(dsem))
        cv_sd = {0: cv_load(0), 1: cv_load(1)}
        for s in range(4):
            slot3 = cv_slot(s % 2)
            for fi in range(8):
                ft = s * 8 + fi
                buf = sq_a if ft < 16 else sq_b
                ft_sq = buf[:, (ft % 16) * CH:(ft % 16) * CH + CH]

                def fvv(t, slot3=slot3, fi=fi, ft=ft, ft_sq=ft_sq):
                    inst = None
                    for g in range(8):
                        inst = t.matmul(
                            PS[:, g * 512:g * 512 + CH],
                            slot3[:, fi, g * 128:(g + 1) * 128],
                            ft_sq, start=(ft == 0), stop=(ft == NFT - 1))
                    return inst
                wv = [cv_sd[s], ("v", last_sq_v)]
                if ft == 0:
                    wv += [cx["bankfree"][i] for i in range(8)]
                S.op("tensor", fvv, waits=wv, inc="t")
            fset(cx, f"CV{s % 2}", "t", S.m("t"))
            if s + 2 < 4:
                cv_sd[s + 2] = cv_load(s + 2)
        vv_t = S.m("t")
        S.op("scalar", lambda e: e.activation(
            b1r[:, 6:8], r3(PS[:, 6 * 512:8 * 512], 2)[:, :, 0:CH], AF.Copy),
            waits=[("t", vv_t), fget(cx, "B1")], inc="s")
        ev67_s = S.m("s")
        cx["bankfree"][6] = ("s", ev67_s)
        cx["bankfree"][7] = ("s", ev67_s)
        S.op("scalar", lambda e: e.activation(
            b1r[:, 0:6], r3(PS[:, 0:6 * 512], 6)[:, :, 0:CH], AF.Copy),
            inc="s")
        vv_s = S.m("s")
        for i in range(6):
            cx["bankfree"][i] = ("s", vv_s)
        fset(cx, "B1", "s", vv_s)
        fset(cx, "B2", "t", vv_t)
        fset(cx, "B3", "t", vv_t)

        if cx.get("dbgcm") == (l, c):
            _dmp(S, cx, 8, b1r, [("s", vv_s)])
            dwait1 = [("sd", S.m("sd"))]
            _dmp(S, cx, 12, r3(sq_a[:, 0:2048], NCT), [("v", last_sq_v)],
                 cast=True)
            _dmp(S, cx, 10, mxa, [("v", mixk_v)], cast=True)
            _dmp(S, cx, 11, mxc, [("v", mixr_v)], cast=True)
            _dmp(S, cx, 13, P["xh"], [("v", P["xhat_v"])], cast=True)

        if c + 1 < NCH:
            pr_next = emit_ln_prep_stats(S, cx, c + 1, 4 + l, pre=pre_next)
            preps.append(emit_ln_prep_finish(S, cx, pr_next, mixes))

        S.op("scalar", lambda e: e.activation(b2r, b0r, AF.Ln),
             waits=[("v", d_v), fget(cx, "B2")], inc="s")
        S.op("scalar", lambda e: e.activation(b0r, b2r, AF.Exp, scale=-1.0),
             inc="s")
        rec_s = S.m("s")
        S.op("vector", lambda e: e.tensor_tensor(b1r, b1r, b0r, ALU.mult),
             waits=[("s", rec_s), ("s", vv_s)] + dwait1, inc="v")
        xr = r3(XRES[:], NCT)[:, :, c * CH:(c + 1) * CH]
        S.op("vector", lambda e, xr=xr: e.tensor_tensor(xr, xr, b1r, ALU.add),
             waits=list(cx["xres_ready"]), inc="v")
        done_v = S.m("v")
        fset(cx, "B0", "v", done_v)
        fset(cx, "B1", "v", done_v)
        fset(cx, "B2", "s", rec_s)
        fset(cx, "B3", "s", rec_s)
        fset(cx, "SQ", "s", rec_s)
        fset(cx, f"MXA{par}", "t", S.m("t"))
        cx["xres_ready"] = [("v", done_v)]


def emit_ffn(S, cx, p, site):
    PS = cx["PS"]
    B0, B1, B2 = cx["B0"], cx["B1"], cx["B2"]
    ARENA, XRES = cx["ARENA"], cx["XRES"]
    waits = [("t", S.m("t"))]
    for m, w in enumerate([cx["frT"], cx["fkT"], cx["fvT"]]):
        srcw = w.rearrange("(ct p) o -> p ct o", p=128)
        dst = r3(ARENA[:], 4)[:, m].rearrange("p (ct o) -> p ct o", ct=NCT)
        S.op("gpsimd", lambda e, dst=dst, srcw=srcw: e.dma_start(dst, srcw),
             waits=waits if m == 0 else [], inc="gd", inc_n=16)
    cx["arena_ready"] = ("gd", S.m("gd"))

    b0r = r3(B0[:], NCT)
    b1r = r3(B1[:], NCT)
    b2f = r3(B2[:, 0:NCT * CH], NCT)
    sq3 = r3(B2[:, 0:NCT * CH].bitcast(BF16)[:, 0:NCT * CH], NCT)

    mixes = [("fAr", "fBr", None, "A"), ("fAk", "fBk", None, "B")]
    preps = [emit_ln_prep(S, cx, 0, site, mixes)]
    for c in range(NCH):
        P = preps[c]
        par = P["par"]
        mxa, mixr_v = P["mix"]["A"]
        mxb, mixk_v = P["mix"]["B"]

        def ev_r(r, glo, ghi, blo, mm_t):
            S.op("scalar", lambda e, glo=glo, ghi=ghi, blo=blo: e.activation(
                b0r[:, glo:ghi], ps_banks(cx, blo, ghi - glo), AF.Exp,
                scale=-1.0),
                waits=[("t", mm_t), fget(cx, "B0")], inc="s")
            for i in range(ghi - glo):
                cx["bankfree"][blo + i] = ("s", S.m("s"))
        emit_proj(S, cx, 0, mxa, ("v", mixr_v), ev_r)
        erm_s = S.m("s")
        fset(cx, "B0", "s", erm_s)
        fset(cx, f"MXA{par}", "t", S.m("t"))
        if c + 1 < NCH:
            pre_next = emit_ln_prep_pre(S, cx, c + 1)

        def ev_k(r, glo, ghi, blo, mm_t):
            S.op("vector", lambda e, glo=glo, ghi=ghi, blo=blo: e.tensor_scalar(
                sq3[:, glo:ghi], ps_banks(cx, blo, ghi - glo), 0.0, None,
                ALU.max),
                waits=[("t", mm_t), fget(cx, "B2")], inc="v")
            S.op("vector", lambda e, glo=glo, ghi=ghi: e.tensor_tensor(
                sq3[:, glo:ghi], sq3[:, glo:ghi], sq3[:, glo:ghi], ALU.mult),
                inc="v")
            for i in range(ghi - glo):
                cx["bankfree"][blo + i] = ("v", S.m("v"))
        emit_proj(S, cx, 1, mxb, ("v", mixk_v), ev_k)
        sq_v = S.m("v")
        fset(cx, "B2", "v", sq_v)
        fset(cx, f"MXB{par}", "t", S.m("t"))

        def ev_v(r, glo, ghi, blo, mm_t):
            S.op("scalar", lambda e, glo=glo, ghi=ghi, blo=blo: e.activation(
                b1r[:, glo:ghi], ps_banks(cx, blo, ghi - glo), AF.Copy),
                waits=[("t", mm_t), fget(cx, "B1")], inc="s")
            for i in range(ghi - glo):
                cx["bankfree"][blo + i] = ("s", S.m("s"))
        emit_proj(S, cx, 2, sq3, ("v", sq_v), ev_v)
        vv_s = S.m("s")
        fset(cx, "B1", "s", vv_s)
        fset(cx, "B2", "t", S.m("t"))

        if c + 1 < NCH:
            pr_next = emit_ln_prep_stats(S, cx, c + 1, site, pre=pre_next)

        S.op("vector", lambda e: e.tensor_scalar(b0r, b0r, 1.0, None, ALU.add),
             waits=[("s", erm_s)], inc="v")
        d_v = S.m("v")
        S.op("scalar", lambda e: e.activation(b2f, b0r, AF.Ln),
             waits=[("v", d_v), fget(cx, "B2")], inc="s")
        S.op("scalar", lambda e: e.activation(b0r, b2f, AF.Exp, scale=-1.0),
             inc="s")
        rec_s = S.m("s")
        S.op("vector", lambda e: e.tensor_tensor(b1r, b1r, b0r, ALU.mult),
             waits=[("s", rec_s), ("s", vv_s)], inc="v")
        xr = r3(XRES[:], NCT)[:, :, c * CH:(c + 1) * CH]
        S.op("vector", lambda e, xr=xr: e.tensor_tensor(xr, xr, b1r, ALU.add),
             waits=list(cx["xres_ready"]), inc="v")
        done_v = S.m("v")
        fset(cx, "B0", "v", done_v)
        fset(cx, "B1", "v", done_v)
        fset(cx, "B2", "s", rec_s)
        cx["xres_ready"] = [("v", done_v)]
        if c + 1 < NCH:
            preps.append(emit_ln_prep_finish(S, cx, pr_next, mixes))


def emit_attn(S, cx, p):
    PS, RWB, PAR, PARB = cx["PS"], cx["RWB"], cx["PAR"], cx["PARB"]
    B0, B1, B2, B3 = cx["B0"], cx["B1"], cx["B2"], cx["B3"]
    ARENA, XRES = cx["ARENA"], cx["XRES"]
    ksp, vtsp, qsp = cx["ksp"], cx["vtsp"], cx["qsp"]
    poff = p * NCH

    waits = [("t", S.m("t"))]
    for m, w in enumerate([cx["wqT"], cx["wkT"], cx["wvT"], cx["woT"]]):
        srcw = w.rearrange("(ct p) o -> p ct o", p=128)
        dst = r3(ARENA[:], 4)[:, m].rearrange("p (ct o) -> p ct o", ct=NCT)
        S.op("gpsimd", lambda e, dst=dst, srcw=srcw: e.dma_start(dst, srcw),
             waits=waits if m == 0 else [], inc="gd", inc_n=16)
    cx["arena_ready"] = ("gd", S.m("gd"))

    rwb3 = r3(RWB[:], NCT)
    stg = B2[:].bitcast(BF16)[:, 0:8 * 2 * 130]
    stg4 = stg.rearrange("p (pr tt w) -> p pr tt w", pr=8, tt=2)

    S.op("vector", lambda e: e.memset(
        stg.rearrange("p (b w) -> p b w", b=16, w=130)[:, :, 64:130:65], 1.0),
        waits=[fget(cx, "B2")], inc="v")
    ones_v = S.m("v")
    preps = [emit_ln_prep(S, cx, 0, None, [])]
    for c in range(NCH):
        if c + 1 < NCH:
            preps.append(emit_ln_prep(S, cx, c + 1, None, []))
        P = preps[c]
        par = P["par"]
        xh3 = P["xh"]
        xhat_v = P["xhat_v"]
        gcol = p * HALF + c * CH

        def ev_q(r, glo, ghi, blo, mm_t):
            for g in range(glo, ghi):
                bq = PAR[:, pcol("bq") + g:pcol("bq") + g + 1]
                bank = PS[:, (blo + g - glo) * 512:(blo + g - glo) * 512 + CH]
                S.op("scalar", lambda e, g=g, bank=bank, bq=bq: e.activation(
                    rwb3[:, g], bank, AF.Identity, bias=bq),
                    waits=[("t", mm_t), fget(cx, "RWB")], inc="s")
                cx["bankfree"][blo + g - glo] = ("s", S.m("s"))
        emit_proj(S, cx, 0, xh3, ("v", xhat_v), ev_q)
        q_s = S.m("s")
        S.op("sync", lambda e, gcol=gcol: e.dma_start(
            qsp.rearrange("(ct p) t -> p ct t", p=128)[:, :, gcol:gcol + CH],
            rwb3), waits=[("s", q_s)], inc="sd", inc_n=16)
        fset(cx, "RWB", "sd", S.m("sd"))

        def ev_k(r, glo, ghi, blo, mm_t):
            for g in range(glo, ghi):
                bk = PAR[:, pcol("bk") + g:pcol("bk") + g + 1]
                bank = PS[:, (blo + g - glo) * 512:(blo + g - glo) * 512 + CH]
                S.op("scalar", lambda e, g=g, bank=bank, bk=bk: e.activation(
                    rwb3[:, g], bank, AF.Identity, bias=bk),
                    waits=[("t", mm_t), fget(cx, "RWB")], inc="s")
                cx["bankfree"][blo + g - glo] = ("s", S.m("s"))
        emit_proj(S, cx, 1, xh3, ("v", xhat_v), ev_k)
        k_s = S.m("s")
        S.op("sync", lambda e, gcol=gcol: e.dma_start(
            ksp.rearrange("(ct p) t -> p ct t", p=128)[:, :, gcol:gcol + CH],
            rwb3), waits=[("s", k_s)], inc="sd", inc_n=16)
        fset(cx, "RWB", "sd", S.m("sd"))

        wv3 = r3(ARENA[:], 4)[:, 2].rearrange("p (ct o) -> p ct o", ct=NCT)
        vmm_t = 0
        for tt in range(2):
            for nh in range(2):
                bank = PS[:, nh * 512:nh * 512 + 512]

                def fmm(t, tt=tt, nh=nh, bank=bank, P=P):
                    inst = None
                    for ct in range(NCT):
                        inst = t.matmul(
                            bank,
                            P["xh"][:, ct, tt * 128:tt * 128 + 128],
                            wv3[:, ct, nh * 512:(nh + 1) * 512],
                            start=(ct == 0), stop=(ct == NCT - 1))
                    return inst
                S.op("tensor", fmm, waits=[("v", xhat_v),
                                           cx["bankfree"][nh],
                                           cx["arena_ready"]], inc="t")
            vmm_t = S.m("t")

            def ev(e, tt=tt):
                src_ap = PS[:, 0:1024].rearrange(
                    "p (pr h w) -> p pr h w", pr=8, h=2, w=64)
                dst_ap = stg4[:, :, tt, :].rearrange(
                    "p pr (h z) -> p pr h z", h=2, z=65)[:, :, :, 0:64]
                return e.activation(dst_ap, src_ap, AF.Copy)
            S.op("scalar", ev, waits=[("t", vmm_t), ("v", ones_v),
                                      fget(cx, "B2")], inc="s")
            cx["bankfree"][0] = ("s", S.m("s"))
            cx["bankfree"][1] = ("s", S.m("s"))
        fset(cx, f"XHAT{par}", "t", vmm_t)
        vstg_s = S.m("s")
        dst = vtsp.rearrange("p (pr tt w) -> p pr tt w", pr=8, tt=32,
                             w=130)[:, :, 2 * (poff + c):2 * (poff + c) + 2, :]
        S.op("sync", lambda e, dst=dst: e.dma_start(dst, stg4),
             waits=[("s", vstg_s)], inc="sd", inc_n=16)
        fset(cx, "B2", "sd", S.m("sd"))
    spills_sd = S.m("sd")

    kslot = [B0[:].bitcast(BF16), B1[:].bitcast(BF16)]
    vslot = [B2[:].bitcast(BF16), B3[:].bitcast(BF16)]
    ptbuf = cx["XHATp"][0][:]
    slot_free = {("k", 0): [fget(cx, "B0")], ("k", 1): [fget(cx, "B1")],
                 ("v", 0): [fget(cx, "B2")], ("v", 1): [fget(cx, "B3")]}
    cx["free"]["PT0"] = fget(cx, "XHAT0")
    cx["free"]["PT1"] = fget(cx, "XHAT0")
    cx["free"]["OB0"] = fget(cx, "XHAT0")
    cx["free"]["OB1"] = fget(cx, "XHAT0")
    SQF = cx["SQB"][:].bitcast(F32)
    QB = cx["MXAp"][0]

    for c in range(NCH):
        q0 = p * HALF + c * CH
        nkt = (q0 + CH) // 128
        S.op("sync", lambda e, q0=q0: e.dma_start(
            r3(QB[:], NCT),
            qsp.rearrange("(ct p) t -> p ct t", p=128)[:, :, q0:q0 + CH]),
            waits=[("sd", spills_sd), fget(cx, "MXA0")], inc="sd", inc_n=16)
        qd = S.m("sd")

        for pr in range(8):
            ks = kslot[pr % 2]
            vs = vslot[pr % 2]
            S.op("sync", lambda e, nkt=nkt, ks=ks, pr=pr: e.dma_start(
                ks[:, 0:nkt * 128], ksp[pr * 128:(pr + 1) * 128, 0:nkt * 128]),
                waits=[("sd", spills_sd)] + slot_free[("k", pr % 2)],
                inc="sd", inc_n=16)
            kd = S.m("sd")
            S.op("gpsimd", lambda e, nkt=nkt, vs=vs, pr=pr: e.dma_start(
                vs[:, 0:nkt * 130],
                vtsp.rearrange("p (pr w) -> p pr w", pr=8, w=32 * 130)[
                    :, pr, 0:nkt * 130]),
                waits=[("sd", spills_sd)] + slot_free[("v", pr % 2)],
                inc="gd", inc_n=16)
            vd = ("gd", S.m("gd"))

            def emit_sc(kt):
                sb = [0, 2][kt % 2]

                def fsc(t, kt=kt, sb=sb, ks=ks, pr=pr):
                    t.matmul(PS[:, sb * 512:sb * 512 + CH],
                             ks[0:64, kt * 128:(kt + 1) * 128],
                             r3(QB[:], NCT)[0:64, pr, :],
                             start=True, stop=True, tile_position=(0, 0))
                    return t.matmul(PS[:, (sb + 1) * 512:(sb + 1) * 512 + CH],
                                    ks[64:128, kt * 128:(kt + 1) * 128],
                                    r3(QB[:], NCT)[64:128, pr, :],
                                    start=True, stop=True,
                                    tile_position=(64, 0))
                S.op("tensor", fsc, waits=[("sd", kd), ("sd", qd),
                                           cx["bankfree"][sb],
                                           cx["bankfree"][sb + 1]], inc="t")
                return S.m("t")

            sc_ts = {0: emit_sc(0)}
            for kt in range(nkt):
                sb = [0, 2][kt % 2]
                pt = kt % 2
                ptap = ptbuf[:, pt * 512:pt * 512 + 512]
                sc_t = sc_ts[kt]

                def fex(e, sb=sb, ptap=ptap):
                    src = PS[:, sb * 512:sb * 512 + 1024].rearrange(
                        "p (h w) -> p h w", h=2, w=512)[:, :, 0:CH]
                    return e.activation(
                        ptap.rearrange("p (h w) -> p h w", h=2, w=CH), src,
                        AF.Exp, scale=0.125)
                S.op("scalar", fex, waits=[("t", sc_t), fget(cx, f"PT{pt}")],
                     inc="s")
                ex_s = S.m("s")
                cx["bankfree"][sb] = ("s", ex_s)
                cx["bankfree"][sb + 1] = ("s", ex_s)
                avw = [("s", ex_s), vd]
                if kt * 128 > q0 - 128:
                    def fmask(e, kt=kt, ptap=ptap, q0=q0):
                        i = None
                        for h in range(2):
                            i = e.affine_select(
                                ptap[:, h * CH:(h + 1) * CH],
                                ptap[:, h * CH:(h + 1) * CH],
                                [[1, CH]], ALU.is_ge, 0.0,
                                base=q0 - kt * 128, channel_multiplier=-1)
                        return i
                    S.op("gpsimd", fmask, waits=[("s", ex_s)], inc="g")
                    avw = [("g", S.m("g")), vd]
                if kt + 1 < nkt:
                    sc_ts[kt + 1] = emit_sc(kt + 1)

                def fav(t, kt=kt, vs=vs, ptap=ptap, first=(kt == 0),
                        last=(kt == nkt - 1)):
                    t.matmul(PS[0:64, 4 * 512:4 * 512 + CH],
                             vs[:, kt * 130:kt * 130 + 64],
                             ptap[:, 0:CH], start=first, stop=last)
                    t.matmul(PS[64:128, 5 * 512:5 * 512 + CH],
                             vs[:, kt * 130 + 65:kt * 130 + 129],
                             ptap[:, CH:2 * CH], start=first, stop=last,
                             tile_position=(0, 64))
                    t.matmul(PS[0:1, 5 * 512 + CH:5 * 512 + 2 * CH],
                             vs[:, kt * 130 + 64:kt * 130 + 65],
                             ptap[:, 0:CH], start=first, stop=last)
                    return t.matmul(PS[64:65, 4 * 512 + CH:4 * 512 + 2 * CH],
                                    vs[:, kt * 130 + 129:kt * 130 + 130],
                                    ptap[:, CH:2 * CH], start=first, stop=last,
                                    tile_position=(0, 64))
                avw2 = avw + ([cx["bankfree"][4], cx["bankfree"][5]]
                              if kt == 0 else [])
                S.op("tensor", fav, waits=avw2, inc="t")
                fset(cx, f"PT{pt}", "t", S.m("t"))
            av_t = S.m("t")
            slot_free[("k", pr % 2)] = [("t", av_t)]
            slot_free[("v", pr % 2)] = [("t", av_t)]

            for h in range(2):
                pb = 0 if h == 0 else 64
                avbank = PS[0 + 64 * h:64 + 64 * h,
                            (4 + h) * 512:(4 + h) * 512 + CH]
                den_ap = (PS[0:1, 5 * 512 + CH:5 * 512 + 2 * CH] if h == 0
                          else PS[64:65, 4 * 512 + CH:4 * 512 + 2 * CH])
                rrow = SQF[pb:pb + 1, 0:CH]
                rrow2 = SQF[pb:pb + 1, CH:2 * CH]
                brow = PARB[pb:pb + 1, PB_MU:PB_MU + CH]
                S.op("scalar", lambda e, den_ap=den_ap, rrow=rrow: e.activation(
                    rrow, den_ap, AF.Ln),
                    waits=[("t", av_t), fget(cx, f"DROW{h}")], inc="s")
                S.op("scalar", lambda e, rrow=rrow, rrow2=rrow2: e.activation(
                    rrow2, rrow, AF.Exp, scale=-1.0), inc="s")
                S.op("scalar", lambda e, rrow2=rrow2, brow=brow: e.activation(
                    brow, rrow2, AF.Copy), inc="s")
                row_s = S.m("s")
                bb = 6 + h

                def fbc(e, brow=brow, pb=pb, bb=bb, h=h):
                    ones = PARB[pb:pb + 1, PB_ONESROW:PB_ONESROW + 64]
                    if h == 0:
                        return e.matmul(PS[0:64, bb * 512:bb * 512 + CH],
                                        ones, brow, start=True, stop=True)
                    return e.matmul(PS[64:128, bb * 512:bb * 512 + CH],
                                    ones, brow, start=True, stop=True,
                                    tile_position=(64, 64))
                S.op("tensor", fbc, waits=[("s", row_s), cx["bankfree"][bb]],
                     inc="t")
                bc_t = S.m("t")
                fset(cx, f"DROW{h}", "t", bc_t)
                obuf = ptbuf[0 + 64 * h:64 + 64 * h, 1024:1024 + CH]
                S.op("scalar", lambda e, avbank=avbank, obuf=obuf: e.activation(
                    obuf, avbank, AF.Copy),
                    waits=[("t", av_t), fget(cx, f"OB{h}")], inc="s")
                ob_s = S.m("s")
                dst = rwb3[64 * h:64 * h + 64, pr, :]
                bvap = PAR[64 * h:64 * h + 64,
                           pcol("bv") + pr:pcol("bv") + pr + 1]
                bcread = (PS[0:64, bb * 512:bb * 512 + CH] if h == 0 else
                          PS[64:128, bb * 512:bb * 512 + CH])
                S.op("vector", lambda e, obuf=obuf, dst=dst, bcread=bcread:
                     e.tensor_tensor(dst, obuf, bcread, ALU.mult),
                     waits=[("s", ob_s), ("t", bc_t), fget(cx, "RWB")],
                     inc="v")
                S.op("vector", lambda e, dst=dst, bvap=bvap: e.tensor_scalar(
                    dst, dst, bvap, None, ALU.add), inc="v")
                fin_v = S.m("v")
                fset(cx, f"OB{h}", "v", fin_v)
                cx["bankfree"][bb] = ("v", fin_v)
            cx["bankfree"][4] = ("v", fin_v)
            cx["bankfree"][5] = ("v", fin_v)
        ofin_v = S.m("v")
        fset(cx, "RWB", "v", ofin_v)
        fset(cx, "MXA0", "t", S.m("t"))

        xr = r3(XRES[:], NCT)[:, :, c * CH:(c + 1) * CH]

        def ev_o(r, glo, ghi, blo, mm_t, xr=xr):
            def fn(e, glo=glo, ghi=ghi, blo=blo, xr=xr):
                inst = None
                for g in range(glo, ghi):
                    bank = PS[:, (blo + g - glo) * 512:
                              (blo + g - glo) * 512 + CH]
                    inst = e.scalar_tensor_tensor(
                        xr[:, g], bank, PAR[:, pcol("bo") + g:pcol("bo") + g + 1],
                        xr[:, g], ALU.add, ALU.add)
                return inst
            S.op("vector", fn, waits=[("t", mm_t)] + cx["xres_ready"], inc="v")
            for i in range(ghi - glo):
                cx["bankfree"][blo + i] = ("v", S.m("v"))
        emit_proj(S, cx, 3, rwb3, ("v", ofin_v), ev_o)
        oadd_v = S.m("v")
        fset(cx, "RWB", "t", S.m("t"))
        cx["xres_ready"] = [("v", oadd_v)]

    end_t = S.m("t")
    for nm in ("B0", "B1", "B2", "B3", "XHAT0"):
        fset(cx, nm, "t", end_t)
    fset(cx, "SQB", "s", S.m("s"))


def _prep_inputs(inputs):
    def tr(a):
        return np.ascontiguousarray(np.swapaxes(np.asarray(a), -1, -2))

    g = {k: np.asarray(v) for k, v in inputs.items()}
    com = {}
    com["tmkT"] = tr(g["L0_attn_a_tm_k_w"])
    com["tmvT"] = tr(g["L0_attn_a_tm_v_w"])
    com["tmrT"] = tr(g["L0_attn_a_tm_r_w"])
    com["tmoT"] = tr(g["L0_attn_a_tm_o_w"])
    com["cmkT"] = tr(g["L0_attn_a_cm_k_w"])
    com["cmvT"] = tr(g["L0_attn_a_cm_v_w"])
    com["cmrT"] = tr(g["L0_attn_a_cm_r_w"])
    wqkv = g["L0_attn_b_W_qkv"]
    com["wqT"] = tr(wqkv[0:1024])
    com["wkT"] = tr(wqkv[1024:2048])
    com["wvT"] = tr(wqkv[2048:3072])
    com["woT"] = tr(g["L0_attn_b_W_o"])
    com["frT"] = tr(g["L0_ffn_W_r"])
    com["fkT"] = tr(g["L0_ffn_W_k"])
    com["fvT"] = tr(g["L0_ffn_W_v"])

    def col8(v):
        return np.ascontiguousarray(np.asarray(v).reshape(8, 128).T)

    cols = {}
    for l in range(NL):
        cols[f"raw_ln1w{l}"] = col8(g["L0_attn_a_ln1_w"][l])
        cols[f"raw_ln2w{l}"] = col8(g["L0_attn_a_ln2_w"][l])
        cols[f"raw_mk{l}"] = col8(g["L0_attn_a_tm_mix_k"][l, 0, 0])
        cols[f"raw_mv{l}"] = col8(g["L0_attn_a_tm_mix_v"][l, 0, 0])
        cols[f"raw_mr{l}"] = col8(g["L0_attn_a_tm_mix_r"][l, 0, 0])
        cols[f"raw_cmk{l}"] = col8(g["L0_attn_a_cm_mix_k"][l, 0, 0])
        cols[f"raw_cmr{l}"] = col8(g["L0_attn_a_cm_mix_r"][l, 0, 0])
        cols[f"raw_td{l}"] = col8(g["L0_attn_a_time_decay"][l])
        cols[f"raw_tf{l}"] = col8(g["L0_attn_a_time_first"][l])
    cols["raw_mur"] = col8(g["L0_ffn_mu_r"])
    cols["raw_muk"] = col8(g["L0_ffn_mu_k"])
    cols["lnfw"] = col8(g["L0_attn_a_lnf_w"])
    bqkv = g["L0_attn_b_b_qkv"]
    cols["bq"] = col8(bqkv[0:1024])
    cols["bk"] = col8(bqkv[1024:2048])
    cols["bv"] = col8(bqkv[2048:3072])
    cols["bo"] = col8(g["L0_attn_b_b_o"])
    vec = np.concatenate([cols[n] for n in VEC_ORDER], axis=1).astype(np.float32)
    com["vecs"] = np.ascontiguousarray(vec)

    x = g["L0_attn_a_x"]
    per_core = []
    for b in range(8):
        m = dict(com)
        m["xt"] = np.ascontiguousarray(x[b].T.astype(np.float32))
        per_core.append(m)
    return per_core


def kernel(**inputs):
    in_maps = _prep_inputs(inputs)
    nc = build_program()
    res = run_bass_kernel_spmd(nc, in_maps, core_ids=list(range(8)))
    outs = [np.ascontiguousarray(np.asarray(r["out"]).T) for r in res.results]
    return np.stack(outs, axis=0).astype(np.float32)


# version 15
# speedup vs baseline: 1.2610x; 1.0377x over previous
import numpy as np
import concourse.bass as bass
import concourse.mybir as mybir
from concourse.bass_utils import run_bass_kernel_spmd

F32 = mybir.dt.float32
BF16 = mybir.dt.bfloat16
AF = mybir.ActivationFunctionType
ALU = mybir.AluOpType

C = 1024
NCT = 8
T = 4096
HALF = 2048
CH = 256
NCH = 8
FFND = 4096
NFT = 32
NL = 4
HEADS = 16
DH = 64
EPS = 1e-5
INV_C = 1.0 / C


SERIALIZE = False
SER_SAME_ENGINES = ("vector", "scalar", "gpsimd")
DETECT_RACES = False
FFN_TRUNC = None

ENG_SEM = {"tensor": "t", "vector": "v", "scalar": "s", "gpsimd": "g",
           "sync": "sd"}


class Sched:
    ENGINES = ("tensor", "vector", "scalar", "gpsimd", "sync")

    def __init__(self, nc):
        self.nc = nc
        self.ops = {e: [] for e in self.ENGINES}
        self.sem = {}
        self.count = {}
        self.seen = {e: {} for e in self.ENGINES}

    def add_sem(self, name, handle):
        self.sem[name] = handle
        self.count[name] = 0
        for e in self.ENGINES:
            self.seen[e][name] = 0

    def op(self, eng, fn, waits=(), inc=None, inc_n=1):
        for s, v in waits:
            assert v <= self.count[s], f"wait {s}>={v} but only {self.count[s]} emitted"
        if SERIALIZE:
            waits = list(waits) + [(s, self.count[s]) for s in self.count
                                   if s != ENG_SEM[eng]]
            if inc is None:
                inc = ENG_SEM[eng]
                inc_n = 16 if inc in ("sd", "gd") else 1
        elif eng in SER_SAME_ENGINES:
            own = ENG_SEM[eng]
            waits = list(waits) + [(own, self.count[own])]
            if inc is None:
                inc = own
                inc_n = 16 if inc in ("sd", "gd") else 1
        if inc is not None:
            self.count[inc] += inc_n
        sems = self.sem
        seen = self.seen[eng]
        eff = {}
        for s, v in waits:
            if v > seen[s] and v > eff.get(s, 0):
                eff[s] = v
        for s, v in eff.items():
            seen[s] = v
        waits = tuple(eff.items())

        def closure(e, fn=fn, waits=waits, inc=inc, inc_n=inc_n):
            for s, v in waits:
                e.wait_ge(sems[s], v)
            inst = fn(e)
            if inc is not None and inst is not None:
                inst.then_inc(sems[inc], inc_n)
        self.ops[eng].append(closure)
        return self.count[inc] if inc is not None else None

    def m(self, sem):
        return self.count[sem]


def r3(ap, nb):
    return ap.rearrange("p (b w) -> p b w", b=nb)


def _par_map():
    m = {}
    col = 0

    def add(name, n=8):
        nonlocal col
        m[name] = col
        col += n
    for l in range(NL):
        for nm in ("Ak", "Bk", "Av", "Bv", "Ar", "Br",
                   "Ack", "Bck", "Acr", "Bcr", "adec", "Eu"):
            add(f"{nm}{l}")
    for nm in ("fAr", "fBr", "fAk", "fBk", "lnfw", "bq", "bk", "bo", "bv"):
        add(nm)
    add("G", NL * 8)
    add("H", NL * 8)
    add("halo", 10 * 8)
    add("epsc", 8)
    add("onesf", 128)
    assert col <= 768, col
    return m, col


PMAP, PAR_USED = _par_map()

VEC_ORDER = ([f"raw_{nm}{l}" for l in range(NL)
              for nm in ("ln1w", "ln2w", "mk", "mv", "mr", "cmk", "cmr", "td", "tf")]
             + ["raw_mur", "raw_muk", "lnfw", "bq", "bk", "bo", "bv"])
VEC_COLS = len(VEC_ORDER) * 8

PB_RSTD = 0
PB_MU = CH
PB_ONESROW = 4 * CH
PB_ONESCOL = 4 * CH + 128
PARB_COLS = 4 * CH + 132


def pcol(name, l=None):
    return PMAP[name if l is None else f"{name}{l}"]


def build_program(debug_stage=None):
    nc = bass.Bass("TRN2", target_bir_lowering=False,
                   detect_race_conditions=DETECT_RACES)

    xt = nc.dram_tensor("xt", [C, T], F32, kind="ExternalInput").ap()
    out = nc.dram_tensor("out", [C, T], F32, kind="ExternalOutput").ap()
    tmkT = nc.dram_tensor("tmkT", [NL, C, C], F32, kind="ExternalInput").ap()
    tmvT = nc.dram_tensor("tmvT", [NL, C, C], F32, kind="ExternalInput").ap()
    tmrT = nc.dram_tensor("tmrT", [NL, C, C], F32, kind="ExternalInput").ap()
    tmoT = nc.dram_tensor("tmoT", [NL, C, C], F32, kind="ExternalInput").ap()
    cmkT = nc.dram_tensor("cmkT", [NL, C, FFND], F32, kind="ExternalInput").ap()
    cmvT = nc.dram_tensor("cmvT", [NL, FFND, C], F32, kind="ExternalInput").ap()
    cmrT = nc.dram_tensor("cmrT", [NL, C, C], F32, kind="ExternalInput").ap()
    wqT = nc.dram_tensor("wqT", [C, C], F32, kind="ExternalInput").ap()
    wkT = nc.dram_tensor("wkT", [C, C], F32, kind="ExternalInput").ap()
    wvT = nc.dram_tensor("wvT", [C, C], F32, kind="ExternalInput").ap()
    woT = nc.dram_tensor("woT", [C, C], F32, kind="ExternalInput").ap()
    frT = nc.dram_tensor("frT", [C, C], F32, kind="ExternalInput").ap()
    fkT = nc.dram_tensor("fkT", [C, C], F32, kind="ExternalInput").ap()
    fvT = nc.dram_tensor("fvT", [C, C], F32, kind="ExternalInput").ap()
    vecs = nc.dram_tensor("vecs", [128, VEC_COLS], F32, kind="ExternalInput").ap()
    cmk_s = nc.dram_tensor("cmk_s", [NL, C, FFND], BF16, kind="Internal").ap()
    cmv_s = nc.dram_tensor("cmv_s", [NL, FFND, C], BF16, kind="Internal").ap()
    ksp = nc.dram_tensor("ksp", [C, T], BF16, kind="Internal").ap()
    vtsp = nc.dram_tensor("vtsp", [128, 8 * 32 * 130], BF16, kind="Internal").ap()
    qsp = nc.dram_tensor("qsp", [C, T], BF16, kind="Internal").ap()
    bcscr = nc.dram_tensor("bcscr", [2, 2 * CH], BF16, kind="Internal").ap()

    import contextlib
    with contextlib.ExitStack() as stack:
        ent = stack.enter_context
        XRES = ent(nc.sbuf_tensor("XRES", [128, NCT * HALF], F32))
        ARENA = ent(nc.sbuf_tensor("ARENA", [128, 4 * NCT * 1024], BF16))
        B0 = ent(nc.sbuf_tensor("B0", [128, NCT * CH], F32))
        B1 = ent(nc.sbuf_tensor("B1", [128, NCT * CH], F32))
        B2 = ent(nc.sbuf_tensor("B2", [128, 2080], F32))
        B3 = ent(nc.sbuf_tensor("B3", [128, 2080], F32))
        XHATp = [ent(nc.sbuf_tensor(f"XHAT{i}", [128, NCT * (CH + 1)], BF16))
                 for i in range(2)]
        RWB = ent(nc.sbuf_tensor("RWB", [128, NCT * CH], BF16))
        SQB = ent(nc.sbuf_tensor("SQB", [128, NCT * CH], BF16))
        MXAp = [ent(nc.sbuf_tensor(f"MXA{i}", [128, NCT * CH], BF16))
                for i in range(2)]
        MXBp = [ent(nc.sbuf_tensor(f"MXB{i}", [128, NCT * CH], BF16))
                for i in range(2)]
        MXCp = [ent(nc.sbuf_tensor(f"MXC{i}", [128, NCT * CH], BF16))
                for i in range(2)]
        BCB = ent(nc.sbuf_tensor("BCB", [128, 2 * CH], BF16))
        PAR = ent(nc.sbuf_tensor("PAR", [128, PAR_USED], F32))
        PARB = ent(nc.sbuf_tensor("PARB", [128, PARB_COLS], BF16))
        PS = ent(nc.psum_tensor("PS", [128, 4096], F32))
        sem_t = ent(nc.semaphore("sem_t"))
        sem_v = ent(nc.semaphore("sem_v"))
        sem_s = ent(nc.semaphore("sem_s"))
        sem_g = ent(nc.semaphore("sem_g"))
        sem_gd = ent(nc.semaphore("sem_gd"))
        sem_sd = ent(nc.semaphore("sem_sd"))
        sem_ad = ent(nc.semaphore("sem_ad"))
        blk = ent(nc.Block())

        S = Sched(nc)
        for nm, h in [("t", sem_t), ("v", sem_v), ("s", sem_s),
                      ("g", sem_g), ("gd", sem_gd), ("sd", sem_sd),
                      ("ad", sem_ad)]:
            S.add_sem(nm, h)

        cx = dict(
            xt=xt, out=out, tmkT=tmkT, tmvT=tmvT, tmrT=tmrT, tmoT=tmoT,
            cmkT=cmkT, cmvT=cmvT, cmrT=cmrT, wqT=wqT, wkT=wkT, wvT=wvT,
            woT=woT, frT=frT, fkT=fkT, fvT=fvT, vecs=vecs,
            cmk_s=cmk_s, cmv_s=cmv_s, ksp=ksp, vtsp=vtsp, qsp=qsp,
            bcscr=bcscr,
            XRES=XRES, ARENA=ARENA, B0=B0, B1=B1, B2=B2, B3=B3,
            XHATp=XHATp, RWB=RWB, SQB=SQB, MXAp=MXAp, MXBp=MXBp,
            MXCp=MXCp, BCB=BCB, PAR=PAR, PARB=PARB, PS=PS,
            free={},
            bankfree=[("v", 0)] * 8,
        )
        emit_all(S, cx, debug_stage)

        @blk.tensor
        def _(e):
            for f in S.ops["tensor"]:
                f(e)

        @blk.vector
        def _(e):
            for f in S.ops["vector"]:
                f(e)

        @blk.scalar
        def _(e):
            for f in S.ops["scalar"]:
                f(e)

        @blk.gpsimd
        def _(e):
            for f in S.ops["gpsimd"]:
                f(e)

        @blk.sync
        def _(e):
            for f in S.ops["sync"]:
                f(e)

    return nc



def fget(cx, name):
    return cx["free"].get(name, ("v", 0))


def fset(cx, name, sem, val):
    cx["free"][name] = (sem, val)


def emit_all(S, cx, debug_stage):
    prologue(S, cx)
    passes = (0,)
    if debug_stage and debug_stage.startswith("1"):
        debug_stage = debug_stage[1:]
    else:
        passes = (0, 1)
    for p in passes:
        pass_begin(S, cx, p)
        src0 = list(cx["xres_ready"])
        for c in range(NCH):
            done_v = emit_ln_fp32_inplace(S, cx, c, src0)
        cx["xres_ready"] = [("v", done_v)]
        if debug_stage == "h0":
            pass_store(S, cx, p)
            continue
        stop = emit_rwkv_stack(S, cx, p, debug_stage)
        if stop:
            pass_store(S, cx, p)
            continue
        emit_lnf_add(S, cx, p)
        if debug_stage == "x1":
            pass_store(S, cx, p)
            continue
        emit_ffn(S, cx, p, site=8)
        if debug_stage == "x2":
            pass_store(S, cx, p)
            continue
        emit_attn(S, cx, p)
        if debug_stage == "x3":
            pass_store(S, cx, p)
            continue
        emit_ffn(S, cx, p, site=9)
        pass_store(S, cx, p)

    S.op("sync", lambda e: None, waits=[("sd", S.m("sd"))])


def prologue(S, cx):
    PAR, PARB, B0 = cx["PAR"], cx["PARB"], cx["B0"]

    guards = [("XRES", cx["XRES"]), ("ARENA", cx["ARENA"]),
              ("B0", cx["B0"]), ("B1", cx["B1"]), ("B2", cx["B2"]),
              ("B3", cx["B3"]), ("XHAT0", cx["XHATp"][0]),
              ("XHAT1", cx["XHATp"][1]), ("RWB", cx["RWB"]),
              ("SQB", cx["SQB"]), ("MXA0", cx["MXAp"][0]),
              ("MXA1", cx["MXAp"][1]), ("MXB0", cx["MXBp"][0]),
              ("MXB1", cx["MXBp"][1]), ("MXC0", cx["MXCp"][0]),
              ("MXC1", cx["MXCp"][1]), ("BCB", cx["BCB"]),
              ("PAR", cx["PAR"]), ("PARB", cx["PARB"])]
    for _, buf in guards:
        S.op("vector", lambda e, buf=buf: e.memset(buf[:], 0.0), inc="v")
    guard_v = S.m("v")
    for name, _ in guards:
        fset(cx, name, "v", guard_v)
    cx["guard_v"] = guard_v

    S.op("sync", lambda e: e.dma_start(B0[:, 0:VEC_COLS], cx["vecs"]),
         waits=[("v", guard_v)], inc="sd", inc_n=16)
    lded = S.m("sd")

    def rawc(name, l=None):
        i = VEC_ORDER.index(name if l is None else f"{name}{l}")
        return B0[:, i * 8:(i + 1) * 8]

    def c(name, l=None):
        col = pcol(name, l)
        return PAR[:, col:col + 8]

    for pp in (0, 32, 64):
        S.op("vector", lambda e, pp=pp: e.memset(
            PARB[pp:pp + 1, PB_ONESROW:PB_ONESROW + 128], 1.0), inc="v")
    S.op("vector", lambda e: e.memset(PARB[:, PB_ONESCOL:PB_ONESCOL + 1], 1.0),
         inc="v")
    of = PMAP["onesf"]
    S.op("vector", lambda e: e.memset(PAR[0:1, of:of + 128], 1.0), inc="v")
    S.op("vector", lambda e: e.memset(PAR[32:33, of:of + 128], 1.0), inc="v")
    ec = PMAP["epsc"]
    S.op("vector", lambda e: e.memset(PAR[:, ec:ec + 1], EPS), inc="v")
    S.op("vector", lambda e: e.memset(PAR[:, PMAP["G"]:PMAP["G"] + NL * 8], 0.0),
         inc="v")
    S.op("vector", lambda e: e.memset(PAR[:, PMAP["H"]:PMAP["H"] + NL * 8], 0.0),
         inc="v")
    S.op("vector", lambda e: e.memset(PAR[:, PMAP["halo"]:PMAP["halo"] + 80], 0.0),
         inc="v")

    for l in range(NL):
        for (a, b, mix, w) in [("Ak", "Bk", "raw_mk", "raw_ln1w"),
                               ("Av", "Bv", "raw_mv", "raw_ln1w"),
                               ("Ar", "Br", "raw_mr", "raw_ln1w"),
                               ("Ack", "Bck", "raw_cmk", "raw_ln2w"),
                               ("Acr", "Bcr", "raw_cmr", "raw_ln2w")]:
            S.op("vector", lambda e, a=a, mix=mix, w=w, l=l: e.tensor_tensor(
                c(a, l), rawc(mix, l), rawc(w, l), ALU.mult),
                waits=[("sd", lded)], inc="v")
            S.op("vector", lambda e, a=a, b=b, w=w, l=l: e.tensor_tensor(
                c(b, l), rawc(w, l), c(a, l), ALU.subtract), inc="v")
        S.op("scalar", lambda e, l=l: e.activation(
            c("adec", l), rawc("raw_td", l), AF.Exp), waits=[("sd", lded)],
            inc="s")
        S.op("scalar", lambda e, l=l: e.activation(
            c("adec", l), c("adec", l), AF.Exp, scale=-1.0), inc="s")
        S.op("scalar", lambda e, l=l: e.activation(
            c("Eu", l), rawc("raw_tf", l), AF.Exp, scale=-1.0), inc="s")
    for (a, b, raw) in [("fAr", "fBr", "raw_mur"), ("fAk", "fBk", "raw_muk")]:
        S.op("vector", lambda e, a=a, raw=raw: e.tensor_copy(c(a), rawc(raw)),
             waits=[("sd", lded)], inc="v")
        S.op("vector", lambda e, b=b, raw=raw: e.tensor_scalar(
            c(b), rawc(raw), -1.0, 1.0, ALU.mult, ALU.add), inc="v")
    for nm in ("lnfw", "bq", "bk", "bo", "bv"):
        S.op("vector", lambda e, nm=nm: e.tensor_copy(c(nm), rawc(nm)),
             waits=[("sd", lded)], inc="v")
    fset(cx, "B0", "v", S.m("v"))
    cx["params_v"] = S.m("v")
    cx["params_s"] = S.m("s")

    cx["cmprep"] = {}
    for l in range(NL):
        S.op("gpsimd", lambda e, l=l: e.dma_start(cx["cmk_s"][l], cx["cmkT"][l]),
             inc="gd", inc_n=16)
        S.op("gpsimd", lambda e, l=l: e.dma_start(cx["cmv_s"][l], cx["cmvT"][l]),
             inc="gd", inc_n=16)
        cx["cmprep"][l] = S.m("gd")


def pass_begin(S, cx, p):
    XRES = cx["XRES"]
    cx["pass"] = p
    src = cx["xt"].rearrange("(ct p) t -> p ct t", p=128)[:, :, p * HALF:(p + 1) * HALF]
    waits = [fget(cx, "XRES")]
    S.op("sync", lambda e: e.dma_start(r3(XRES[:], NCT), src),
         waits=waits, inc="sd", inc_n=16)
    cx["xres_ready"] = [("sd", S.m("sd"))]


def pass_store(S, cx, p):
    XRES = cx["XRES"]
    dst = cx["out"].rearrange("(ct p) t -> p ct t", p=128)[:, :, p * HALF:(p + 1) * HALF]
    S.op("sync", lambda e: e.dma_start(dst, r3(XRES[:], NCT)),
         waits=list(cx["xres_ready"]), inc="sd", inc_n=16)
    fset(cx, "XRES", "sd", S.m("sd"))


def emit_stats_pre(S, cx, src3, src_waits, par):
    SQB = cx["SQB"]
    XHAT = cx["XHATp"][par]
    xhv = r3(XHAT[:, 0:NCT * CH], NCT)
    S.op("scalar", lambda e: e.activation(xhv, src3, AF.Copy),
         waits=list(src_waits) + [fget(cx, f"XHAT{par}")], inc="s")
    cast_s = S.m("s")
    S.op("vector", lambda e: e.tensor_tensor(r3(SQB[:], NCT), xhv, xhv,
                                             ALU.mult),
         waits=[("s", cast_s), fget(cx, "SQB")], inc="v")
    return dict(cast_s=cast_s, sq_v=S.m("v"))


def emit_stats(S, cx, src3, src_waits, pbank, pre=None):
    PS, SQB, PARB = cx["PS"], cx["SQB"], cx["PARB"]
    par = pbank - 6
    XHAT = cx["XHATp"][par]
    SQF = SQB[:].bitcast(F32)
    bank = PS[:, pbank * 512:(pbank + 1) * 512]

    if pre is None:
        pre = emit_stats_pre(S, cx, src3, src_waits, par)
    cast_s, sq_v = pre["cast_s"], pre["sq_v"]

    onescol = PARB[:, PB_ONESCOL:PB_ONESCOL + 1]

    def fsum(t):
        inst = None
        for ct in range(NCT):
            inst = t.matmul(bank[0:1, 0:CH], onescol,
                            r3(XHAT[:, 0:NCT * CH], NCT)[:, ct],
                            start=(ct == 0), stop=(ct == NCT - 1))
        for ct in range(NCT):
            inst = t.matmul(bank[0:1, CH:2 * CH], onescol,
                            r3(SQB[:], NCT)[:, ct],
                            start=(ct == 0), stop=(ct == NCT - 1))
        return inst
    S.op("tensor", fsum, waits=[("s", cast_s), ("v", sq_v),
                                cx["bankfree"][pbank],
                                ("v", cx["params_v"]), ("s", cx["params_s"])],
         inc="t")
    stats_t = S.m("t")
    fset(cx, f"XHAT{par}", "t", stats_t)
    fset(cx, "SQB", "t", stats_t)

    mu_f = SQF[0:1, 0:CH]
    var_f = SQF[0:1, CH:2 * CH]
    rstd_f = SQF[0:1, 2 * CH:3 * CH]
    scr_f = SQF[0:1, 3 * CH:4 * CH]
    S.op("vector", lambda e: e.tensor_scalar(mu_f, bank[0:1, 0:CH], INV_C,
                                             None, ALU.mult),
         waits=[("t", stats_t)], inc="v")
    S.op("vector", lambda e: e.tensor_scalar(var_f, bank[0:1, CH:2 * CH],
                                             INV_C, None, ALU.mult), inc="v")
    S.op("vector", lambda e: e.tensor_tensor(scr_f, mu_f, mu_f, ALU.mult),
         inc="v")
    S.op("vector", lambda e: e.tensor_tensor(var_f, var_f, scr_f,
                                             ALU.subtract), inc="v")
    fin_v = S.m("v")

    epsap = cx["PAR"][0:1, PMAP["epsc"]:PMAP["epsc"] + 1]
    S.op("scalar", lambda e: e.activation(scr_f, var_f, AF.Ln, bias=epsap),
         waits=[("v", fin_v)], inc="s")
    S.op("scalar", lambda e: e.activation(rstd_f, scr_f, AF.Exp, scale=-0.5),
         inc="s")
    S.op("scalar", lambda e: e.activation(
        PARB[0:1, par * 2 * CH + PB_RSTD:par * 2 * CH + PB_RSTD + CH],
        rstd_f, AF.Copy), inc="s")
    S.op("scalar", lambda e: e.activation(
        PARB[0:1, par * 2 * CH + PB_MU:par * 2 * CH + PB_MU + CH],
        mu_f, AF.Copy), inc="s")
    rows_s = S.m("s")
    fset(cx, "SQB", "s", rows_s)
    return dict(rows_s=rows_s, fin_v=fin_v, pbank=pbank, par=par,
                mu_f=mu_f, rstd_f=rstd_f)


def emit_bcast(S, cx, st, fp32=False):
    PS, PARB, PAR = cx["PS"], cx["PARB"], cx["PAR"]
    par = st["par"]
    bank = PS[:, st["pbank"] * 512:(st["pbank"] + 1) * 512]
    if fp32:
        lhs_r = PAR[0:1, PMAP["onesf"]:PMAP["onesf"] + 128]
        lhs_m = lhs_r
        rstd_src = st["rstd_f"]
        mu_src = st["mu_f"]
        w = [("s", st["rows_s"]), ("v", st["fin_v"])]
    else:
        lhs_r = PARB[0:1, PB_ONESROW:PB_ONESROW + 128]
        lhs_m = lhs_r
        rstd_src = PARB[0:1, PB_RSTD + par * CH:PB_RSTD + par * CH + CH]
        mu_src = PARB[0:1, PB_MU + par * CH:PB_MU + par * CH + CH]
        w = [("s", st["rows_s"])]

    def fbc(e):
        e.matmul(bank[:, 0:CH], lhs_r, rstd_src, start=True, stop=True)
        return e.matmul(bank[:, CH:2 * CH], lhs_m, mu_src, start=True,
                        stop=True)
    S.op("tensor", fbc, waits=w, inc="t")
    return S.m("t")


def emit_ln_fp32_inplace(S, cx, c, src_waits):
    XRES, PS = cx["XRES"], cx["PS"]
    par = c % 2
    pbank = 6 + par
    tmp = [cx["B0"], cx["B1"]][par]
    tname = ["B0", "B1"][par]
    xr = r3(XRES[:], NCT)[:, :, c * CH:(c + 1) * CH]
    st = emit_stats(S, cx, xr, src_waits, pbank)
    bc_t = emit_bcast(S, cx, st, fp32=True)
    fset(cx, "SQB", "t", bc_t)
    bank = PS[:, pbank * 512:(pbank + 1) * 512]
    mu_bc = bank[:, CH:2 * CH].unsqueeze(1).broadcast_to([128, NCT, CH])
    rstd_bc = bank[:, 0:CH].unsqueeze(1).broadcast_to([128, NCT, CH])
    b0r = r3(tmp[:, 0:NCT * CH], NCT)
    S.op("vector", lambda e: e.tensor_tensor(b0r, xr, mu_bc, ALU.subtract),
         waits=[("t", bc_t), fget(cx, tname)], inc="v")
    S.op("vector", lambda e: e.tensor_tensor(xr, b0r, rstd_bc, ALU.mult),
         inc="v")
    done_v = S.m("v")
    cx["bankfree"][pbank] = ("v", done_v)
    fset(cx, tname, "v", done_v)
    return done_v


def emit_ln_prep_pre(S, cx, c):
    xr = r3(cx["XRES"][:], NCT)[:, :, c * CH:(c + 1) * CH]
    return emit_stats_pre(S, cx, xr, cx["xres_ready"], c % 2)


def emit_ln_prep_stats(S, cx, c, site, pre=None):
    XRES, PARB = cx["XRES"], cx["PARB"]
    par = c % 2
    pbank = 6 + par
    XHAT = cx["XHATp"][par]
    BCB = cx["BCB"]
    bcscr = cx["bcscr"]
    xr = r3(XRES[:], NCT)[:, :, c * CH:(c + 1) * CH]
    st = emit_stats(S, cx, xr, cx["xres_ready"], pbank, pre=pre)
    cx["bankfree"][pbank] = ("v", st["fin_v"])
    S.op("sync", lambda e: e.dma_start(
        bcscr[par:par + 1, :], PARB[0:1, par * 2 * CH:par * 2 * CH + 2 * CH]),
        waits=[("s", st["rows_s"])], inc="sd", inc_n=16)
    st_sd = S.m("sd")
    S.op("sync", lambda e: e.dma_start(
        BCB[:], bcscr[par:par + 1, :].broadcast_to([128, 2 * CH])),
        waits=[fget(cx, "BCB"), ("sd", st_sd)], inc="sd", inc_n=16)
    bcb_w = ("sd", S.m("sd"))
    return dict(par=par, site=site, bcb_w=bcb_w,
                xh=r3(XHAT[:, 0:NCT * CH], NCT), xflat=XHAT[:, 0:NCT * CH],
                mix={})


def emit_ln_prep_finish(S, cx, rec, mixes):
    PAR, RWB = cx["PAR"], cx["RWB"]
    par, site, xh = rec["par"], rec["site"], rec["xh"]
    BCB = cx["BCB"]
    mu_bc = BCB[:, CH:2 * CH].unsqueeze(1).broadcast_to([128, NCT, CH])
    rstd_bc = BCB[:, 0:CH].unsqueeze(1).broadcast_to([128, NCT, CH])
    S.op("vector", lambda e: e.tensor_tensor(xh, xh, mu_bc, ALU.subtract),
         waits=[rec["bcb_w"]], inc="v")
    S.op("vector", lambda e: e.tensor_tensor(xh, xh, rstd_bc, ALU.mult),
         inc="v")
    xhat_v = S.m("v")
    fset(cx, "BCB", "v", xhat_v)
    rec["xhat_v"] = xhat_v
    if site is not None:
        hc = PMAP["halo"] + site * 8
        S.op("vector", lambda e: e.tensor_copy(
            RWB[:, 1:NCT * CH], rec["xflat"][:, 0:NCT * CH - 1]),
            waits=[fget(cx, "RWB")], inc="v")
        S.op("vector", lambda e: e.tensor_copy(
            r3(RWB[:], NCT)[:, :, 0:1], PAR[:, hc:hc + 8].bitcast(BF16)
            [:, 0:NCT].rearrange("p (b w) -> p b w", b=NCT)), inc="v")
        S.op("vector", lambda e: e.tensor_copy(
            PAR[:, hc:hc + 8].bitcast(BF16)[:, 0:NCT]
            .rearrange("p (b w) -> p b w", b=NCT),
            xh[:, :, CH - 1:CH]), inc="v")
    sh3 = r3(RWB[:], NCT)
    slot = {"A": "MXAp", "B": "MXBp", "C": "MXCp"}
    for (Aname, Bname, l, which) in mixes:
        dst = r3(cx[slot[which]][par][:], NCT)
        mv = emit_mix(S, cx, Aname, Bname, l, dst, f"MX{which}{par}", xh, sh3)
        rec["mix"][which] = (dst, mv)
    if mixes:
        fset(cx, "RWB", "v", S.m("v"))
    return rec


def emit_ln_prep(S, cx, c, site, mixes):
    return emit_ln_prep_finish(S, cx, emit_ln_prep_stats(S, cx, c, site),
                               mixes)


def emit_mix(S, cx, Aname, Bname, l, dst, dst_name, xh, sh3):
    PAR = cx["PAR"]
    SC = r3(cx["SQB"][:], NCT)
    Ac = pcol(Aname, l) if l is not None else pcol(Aname)
    Bc = pcol(Bname, l) if l is not None else pcol(Bname)
    A_bc = r3(PAR[:, Ac:Ac + 8], NCT).broadcast_to([128, NCT, CH])
    B_bc = r3(PAR[:, Bc:Bc + 8], NCT).broadcast_to([128, NCT, CH])
    S.op("vector", lambda e: e.tensor_tensor(dst, sh3, B_bc, ALU.mult),
         waits=[fget(cx, dst_name)], inc="v")
    S.op("vector", lambda e: e.tensor_tensor(SC, xh, A_bc, ALU.mult),
         waits=[fget(cx, "SQB")], inc="v")
    S.op("vector", lambda e: e.tensor_tensor(dst, dst, SC, ALU.add), inc="v")
    mix_v = S.m("v")
    fset(cx, "SQB", "v", mix_v)
    return mix_v


def emit_lnf_add(S, cx, p):
    XRES, PAR, PS = cx["XRES"], cx["PAR"], cx["PS"]
    lnfw = pcol("lnfw")
    src0 = list(cx["xres_ready"])
    for c in range(NCH):
        par = c % 2
        pbank = 6 + par
        tmp = [cx["B0"], cx["B1"]][par]
        tname = ["B0", "B1"][par]
        xbuf = [cx["B3"], cx["B2"]][par]
        xname = ["B3", "B2"][par]
        xr = r3(XRES[:], NCT)[:, :, c * CH:(c + 1) * CH]
        st = emit_stats(S, cx, xr, src0, pbank)
        bc_t = emit_bcast(S, cx, st, fp32=True)
        fset(cx, "SQB", "t", bc_t)
        bank = PS[:, pbank * 512:(pbank + 1) * 512]
        mu_bc = bank[:, CH:2 * CH].unsqueeze(1).broadcast_to([128, NCT, CH])
        rstd_bc = bank[:, 0:CH].unsqueeze(1).broadcast_to([128, NCT, CH])
        src = cx["xt"].rearrange("(ct p) t -> p ct t", p=128)[
            :, :, p * HALF + c * CH: p * HALF + (c + 1) * CH]
        b3r = r3(xbuf[:, 0:NCT * CH], NCT)
        S.op("sync", lambda e, src=src, b3r=b3r: e.dma_start(b3r, src),
             waits=[fget(cx, xname)], inc="sd", inc_n=16)
        xin_d = S.m("sd")
        b0r = r3(tmp[:, 0:NCT * CH], NCT)
        S.op("vector", lambda e, b0r=b0r, xr=xr, mu_bc=mu_bc: e.tensor_tensor(
            b0r, xr, mu_bc, ALU.subtract),
            waits=[("t", bc_t), fget(cx, tname)], inc="v")
        S.op("vector", lambda e, b0r=b0r, rstd_bc=rstd_bc: e.tensor_tensor(
            b0r, b0r, rstd_bc, ALU.mult), inc="v")
        bcd = S.m("v")
        cx["bankfree"][pbank] = ("v", bcd)

        def fn(e, b0r=b0r, xr=xr, b3r=b3r):
            inst = None
            for ct in range(NCT):
                inst = e.scalar_tensor_tensor(
                    xr[:, ct], b0r[:, ct], PAR[:, lnfw + ct:lnfw + ct + 1],
                    b3r[:, ct], ALU.mult, ALU.add)
            return inst
        S.op("vector", fn, waits=[("sd", xin_d)], inc="v")
        done_v = S.m("v")
        fset(cx, tname, "v", done_v)
        fset(cx, xname, "v", done_v)
    cx["xres_ready"] = [("v", done_v)]


PROJ_ROUNDS = [(0, 3, 0), (3, 6, 3), (6, 8, 0)]


def ps_banks(cx, blo, n):
    return r3(cx["PS"][:, blo * 512:(blo + n) * 512], n)[:, :, 0:CH]


def emit_proj(S, cx, warena_m, rhs3, rhs_wait, evac, wa3=None):
    ARENA, PS = cx["ARENA"], cx["PS"]
    if wa3 is None:
        wa3 = r3(ARENA[:], 4)[:, warena_m].rearrange("p (ct o) -> p ct o",
                                                     ct=NCT)
    for r, (glo, ghi, blo) in enumerate(PROJ_ROUNDS):
        def fn(t, glo=glo, ghi=ghi, blo=blo):
            inst = None
            for g in range(glo, ghi):
                bank = PS[:, (blo + g - glo) * 512:(blo + g - glo) * 512 + CH]
                for ct in range(NCT):
                    inst = t.matmul(bank, wa3[:, ct, g * 128:(g + 1) * 128],
                                    rhs3[:, ct], start=(ct == 0),
                                    stop=(ct == NCT - 1))
            return inst
        waits = [rhs_wait, cx["arena_ready"]] + [cx["bankfree"][blo + i]
                                                 for i in range(ghi - glo)]
        S.op("tensor", fn, waits=waits, inc="t")
        evac(r, glo, ghi, blo, S.m("t"))


def _dmp(S, cx, region, src, waits, cast=False, nct=NCT):
    dst = cx["out"].rearrange("(ct p) t -> p ct t", p=128)[
        :, 0:nct, region * CH:(region + 1) * CH]
    eng, inc = ("gpsimd", "gd") if cast else ("sync", "sd")
    S.op(eng, lambda e, dst=dst, src=src: e.dma_start(dst, src),
         waits=waits, inc=inc, inc_n=16)


def emit_rwkv_stack(S, cx, p, debug_stage):
    for l in range(NL):
        emit_rwkv_tm(S, cx, p, l)
        if debug_stage == f"L{l}tm":
            return True
        emit_rwkv_cm(S, cx, p, l)
        if debug_stage == f"L{l}":
            return True
    return False


def load_tm_arena(S, cx, l):
    ARENA = cx["ARENA"]
    waits = [("t", S.m("t"))]
    for m, w in enumerate([cx["tmkT"], cx["tmvT"], cx["tmrT"], cx["tmoT"]]):
        srcw = w[l].rearrange("(ct p) o -> p ct o", p=128)
        dst = r3(ARENA[:], 4)[:, m].rearrange("p (ct o) -> p ct o", ct=NCT)
        S.op("gpsimd", lambda e, dst=dst, srcw=srcw: e.dma_start(dst, srcw),
             waits=waits if m == 0 else [], inc="gd", inc_n=16)
    cx["arena_ready"] = ("gd", S.m("gd"))


def emit_rwkv_tm(S, cx, p, l):
    PS, PAR = cx["PS"], cx["PAR"]
    B0, B1, B2, B3 = cx["B0"], cx["B1"], cx["B2"], cx["B3"]
    XRES = cx["XRES"]
    load_tm_arena(S, cx, l)

    b0r = r3(B0[:], NCT)
    b1r = r3(B1[:], NCT)
    b2r = r3(B2[:, 0:NCT * (CH + 1)], NCT)
    b3r = r3(B3[:, 0:NCT * (CH + 1)], NCT)
    rw = r3(cx["RWB"][:], NCT)

    mixes = [("Ak", "Bk", l, "A"), ("Av", "Bv", l, "B"), ("Ar", "Br", l, "C")]
    preps = [emit_ln_prep(S, cx, 0, l, mixes)]
    for c in range(NCH):
        P = preps[c]
        par = P["par"]
        mxa, mixk_v = P["mix"]["A"]
        mxb, mixv_v = P["mix"]["B"]
        mxc, mixr_v = P["mix"]["C"]

        def ev_ek(r, glo, ghi, blo, mm_t):
            S.op("scalar", lambda e, glo=glo, ghi=ghi, blo=blo: e.activation(
                b0r[:, glo:ghi], ps_banks(cx, blo, ghi - glo), AF.Exp),
                waits=[("t", mm_t), fget(cx, "B0")], inc="s")
            for i in range(ghi - glo):
                cx["bankfree"][blo + i] = ("s", S.m("s"))
        emit_proj(S, cx, 0, mxa, ("v", mixk_v), ev_ek)
        ek_s = S.m("s")
        fset(cx, "B0", "s", ek_s)
        fset(cx, f"MXA{par}", "t", S.m("t"))
        if c + 1 < NCH:
            pre_next = emit_ln_prep_pre(S, cx, c + 1)

        def ev_v(r, glo, ghi, blo, mm_t):
            S.op("scalar", lambda e, glo=glo, ghi=ghi, blo=blo: e.activation(
                b1r[:, glo:ghi], ps_banks(cx, blo, ghi - glo), AF.Copy),
                waits=[("t", mm_t), fget(cx, "B1")], inc="s")
            for i in range(ghi - glo):
                cx["bankfree"][blo + i] = ("s", S.m("s"))
        emit_proj(S, cx, 1, mxb, ("v", mixv_v), ev_v)
        v_s = S.m("s")
        fset(cx, "B1", "s", v_s)
        fset(cx, f"MXB{par}", "t", S.m("t"))

        def ev_r(r, glo, ghi, blo, mm_t):
            S.op("scalar", lambda e, glo=glo, ghi=ghi, blo=blo: e.activation(
                b2r[:, glo:ghi, 0:CH], ps_banks(cx, blo, ghi - glo), AF.Exp,
                scale=-1.0),
                waits=[("t", mm_t), fget(cx, "B2")], inc="s")
            for i in range(ghi - glo):
                cx["bankfree"][blo + i] = ("s", S.m("s"))

        S.op("vector", lambda e: e.tensor_tensor(b1r, b0r, b1r, ALU.mult),
             waits=[("s", ek_s), ("s", v_s)], inc="v")
        c1_v = S.m("v")
        gcol = PMAP["G"] + l * 8
        hcol = PMAP["H"] + l * 8
        S.op("vector", lambda e: e.tensor_copy(
            b2r[:, :, 0:1], r3(PAR[:, gcol:gcol + 8], NCT)),
            waits=[fget(cx, "B2")], inc="v")
        S.op("vector", lambda e: e.tensor_copy(
            b3r[:, :, 0:1], r3(PAR[:, hcol:hcol + 8], NCT)),
            waits=[fget(cx, "B3")], inc="v")

        def scans(e):
            inst = None
            for ct in range(NCT):
                a_ap = PAR[:, pcol("adec", l) + ct: pcol("adec", l) + ct + 1]
                inst = e.tensor_tensor_scan(
                    b2r[:, ct, 1:1 + CH], a_ap.broadcast_to([128, CH]),
                    b1r[:, ct], PAR[:, gcol + ct:gcol + ct + 1],
                    ALU.mult, ALU.add)
            return inst
        S.op("vector", scans, waits=[("v", c1_v)], inc="v")

        def scand(e):
            inst = None
            for ct in range(NCT):
                a_ap = PAR[:, pcol("adec", l) + ct: pcol("adec", l) + ct + 1]
                inst = e.tensor_tensor_scan(
                    b3r[:, ct, 1:1 + CH], a_ap.broadcast_to([128, CH]),
                    b0r[:, ct], PAR[:, hcol + ct:hcol + ct + 1],
                    ALU.mult, ALU.add)
            return inst
        S.op("vector", scand, inc="v")
        S.op("vector", lambda e: e.tensor_copy(
            r3(PAR[:, gcol:gcol + 8], NCT), b2r[:, :, CH:CH + 1]), inc="v")
        S.op("vector", lambda e: e.tensor_copy(
            r3(PAR[:, hcol:hcol + 8], NCT), b3r[:, :, CH:CH + 1]), inc="v")

        def num(e):
            inst = None
            for ct in range(NCT):
                eu = PAR[:, pcol("Eu", l) + ct: pcol("Eu", l) + ct + 1]
                inst = e.scalar_tensor_tensor(b1r[:, ct], b2r[:, ct, 0:CH], eu,
                                              b1r[:, ct], ALU.mult, ALU.add)
            return inst
        S.op("vector", num, inc="v")
        num_v = S.m("v")

        def den(e):
            inst = None
            for ct in range(NCT):
                eu = PAR[:, pcol("Eu", l) + ct: pcol("Eu", l) + ct + 1]
                inst = e.scalar_tensor_tensor(b0r[:, ct], b3r[:, ct, 0:CH], eu,
                                              b0r[:, ct], ALU.mult, ALU.add)
            return inst
        S.op("vector", den, inc="v")
        den_v = S.m("v")
        fset(cx, "B2", "v", num_v)
        fset(cx, "B3", "v", den_v)

        emit_proj(S, cx, 2, mxc, ("v", mixr_v), ev_r)
        esm_s = S.m("s")
        fset(cx, "B2", "s", esm_s)
        fset(cx, f"MXC{par}", "t", S.m("t"))

        if c + 1 < NCH:
            pr_next = emit_ln_prep_stats(S, cx, c + 1, l, pre=pre_next)

        S.op("vector", lambda e: e.tensor_scalar(
            b2r[:, :, 0:CH], b2r[:, :, 0:CH], 1.0, None, ALU.add),
            waits=[("s", esm_s)], inc="v")
        S.op("vector", lambda e: e.tensor_tensor(
            b0r, b0r, b2r[:, :, 0:CH], ALU.mult),
            waits=[("v", den_v)], inc="v")
        d2_v = S.m("v")
        fset(cx, "B2", "v", d2_v)
        S.op("scalar", lambda e: e.activation(b3r[:, :, 0:CH], b0r, AF.Ln),
             waits=[("v", d2_v)], inc="s")
        S.op("scalar", lambda e: e.activation(b0r, b3r[:, :, 0:CH], AF.Exp,
                                              scale=-1.0), inc="s")
        rec_s = S.m("s")
        fset(cx, "B3", "s", rec_s)
        S.op("vector", lambda e: e.tensor_tensor(rw, b1r, b0r, ALU.mult),
             waits=[("s", rec_s), ("v", num_v), fget(cx, "RWB")], inc="v")
        rw_v = S.m("v")
        fset(cx, "RWB", "v", rw_v)
        fset(cx, "B0", "v", rw_v)
        fset(cx, "B1", "v", rw_v)

        xr = r3(XRES[:], NCT)[:, :, c * CH:(c + 1) * CH]

        def ev_o(r, glo, ghi, blo, mm_t, xr=xr):
            S.op("vector", lambda e, glo=glo, ghi=ghi, blo=blo, xr=xr:
                 e.tensor_tensor(xr[:, glo:ghi], xr[:, glo:ghi],
                                 ps_banks(cx, blo, ghi - glo), ALU.add),
                 waits=[("t", mm_t)] + cx["xres_ready"], inc="v")
            for i in range(ghi - glo):
                cx["bankfree"][blo + i] = ("v", S.m("v"))
        emit_proj(S, cx, 3, rw, ("v", rw_v), ev_o)
        oadd_v = S.m("v")
        fset(cx, "RWB", "t", S.m("t"))
        cx["xres_ready"] = [("v", oadd_v)]
        if c + 1 < NCH:
            preps.append(emit_ln_prep_finish(S, cx, pr_next, mixes))


def emit_rwkv_cm(S, cx, p, l):
    PS, PAR = cx["PS"], cx["PAR"]
    B0, B1, B2, B3 = cx["B0"], cx["B1"], cx["B2"], cx["B3"]
    ARENA, XRES = cx["ARENA"], cx["XRES"]

    cmr_dst = ARENA[:, 24576:32768].rearrange("p (ct o) -> p ct o", ct=NCT)
    cmr3 = ARENA[:, 24576:32768].rearrange("p (ct o) -> p ct o", ct=NCT)
    tm_done_t = S.m("t")
    S.op("gpsimd", lambda e: e.dma_start(
        cmr_dst, cx["cmrT"][l].rearrange("(ct p) o -> p ct o", p=128)),
        waits=[("t", tm_done_t)], inc="gd", inc_n=16)
    cmr_gd = S.m("gd")
    for nm in ("CKSLOT0", "CKSLOT1", "CKSLOT2", "CKSLOT3", "CV0", "CV1"):
        fset(cx, nm, "t", tm_done_t)

    b0r = r3(B0[:], NCT)
    b1r = r3(B1[:], NCT)
    b2r = r3(B2[:, 0:NCT * CH], NCT)
    sq_a = cx["B2"][:, 0:NCT * CH].bitcast(BF16)
    sq_b = cx["B3"][:, 0:NCT * CH].bitcast(BF16)

    def sq_slice(s):
        buf = sq_a if s < 4 else sq_b
        return r3(buf[:, (s % 4) * 1024:(s % 4) * 1024 + 1024], 4)

    prep = cx["cmprep"][l]
    fset(cx, "SQ", "s", S.m("s"))
    mixes = [("Ack", "Bck", l, "A"), ("Acr", "Bcr", l, "C")]
    preps = [emit_ln_prep(S, cx, 0, 4 + l, mixes)]
    for c in range(NCH):
        P = preps[c]
        par = P["par"]
        mxa, mixk_v = P["mix"]["A"]
        mxc, mixr_v = P["mix"]["C"]

        last_sq_v = 0
        for h in range(16):
            slot = h % 4
            slot3 = r3(ARENA[:, slot * 2048:(slot + 1) * 2048], NCT)
            srcw = cx["cmk_s"][l].rearrange("(ct p) f -> p ct f", p=128)[
                :, :, h * 256:(h + 1) * 256]
            dq, dsem = [("sync", "sd"), ("scalar", "ad")][h % 2]
            S.op(dq, lambda e, slot3=slot3, srcw=srcw: e.dma_start(
                slot3, srcw),
                waits=[("gd", prep), fget(cx, f"CKSLOT{slot}")],
                inc=dsem, inc_n=16)
            sl_sd = (dsem, S.m(dsem))
            bp = 2 * (h % 3)

            def fkk(t, slot3=slot3, bp=bp, mxa=mxa):
                inst = None
                for fi in range(2):
                    bank = PS[:, (bp + fi) * 512:(bp + fi) * 512 + CH]
                    for ct in range(NCT):
                        inst = t.matmul(bank,
                                        slot3[:, ct, fi * 128:(fi + 1) * 128],
                                        mxa[:, ct], start=(ct == 0),
                                        stop=(ct == NCT - 1))
                return inst
            S.op("tensor", fkk, waits=[sl_sd, ("v", mixk_v),
                                       cx["bankfree"][bp],
                                       cx["bankfree"][bp + 1]], inc="t")
            kk_t = S.m("t")
            fset(cx, f"CKSLOT{slot}", "t", kk_t)
            buf = sq_a if h < 8 else sq_b
            sqv = r3(buf[:, (h % 8) * 512:(h % 8) * 512 + 512], 2)

            def evk(e, sqv=sqv, bp=bp):
                src = r3(PS[:, bp * 512:(bp + 2) * 512], 2)[:, :, 0:CH]
                return e.tensor_scalar(sqv, src, 0.0, None, ALU.max)
            S.op("vector", evk, waits=[("t", kk_t), fget(cx, "SQ")], inc="v")
            S.op("vector", lambda e, sqv=sqv: e.tensor_tensor(
                sqv, sqv, sqv, ALU.mult), inc="v")
            last_sq_v = S.m("v")
            cx["bankfree"][bp] = ("v", last_sq_v)
            cx["bankfree"][bp + 1] = ("v", last_sq_v)

        if c + 1 < NCH:
            pre_next = emit_ln_prep_pre(S, cx, c + 1)
            pr_next = emit_ln_prep_stats(S, cx, c + 1, 4 + l, pre=pre_next)
            preps.append(emit_ln_prep_finish(S, cx, pr_next, mixes))

        def ev_rr(r, glo, ghi, blo, mm_t):
            S.op("scalar", lambda e, glo=glo, ghi=ghi, blo=blo: e.activation(
                b0r[:, glo:ghi], ps_banks(cx, blo, ghi - glo), AF.Exp,
                scale=-1.0),
                waits=[("t", mm_t), fget(cx, "B0")], inc="s")
            for i in range(ghi - glo):
                cx["bankfree"][blo + i] = ("s", S.m("s"))
        cx["arena_ready"] = ("gd", cmr_gd)
        emit_proj(S, cx, None, mxc, ("v", mixr_v), ev_rr, wa3=cmr3)
        e2m_s = S.m("s")
        fset(cx, "B0", "s", e2m_s)
        fset(cx, f"MXC{par}", "t", S.m("t"))
        dwait1, dwait2 = [], []
        if cx.get("dbgcm") == (l, c):
            _dmp(S, cx, 9, b0r, [("s", e2m_s)])
            dwait2 = [("sd", S.m("sd"))]
        S.op("vector", lambda e: e.tensor_scalar(b0r, b0r, 1.0, None, ALU.add),
             waits=[("s", e2m_s)] + dwait2, inc="v")
        d_v = S.m("v")

        def cv_slot(sl):
            return ARENA[:, 8192 + sl * 8192:8192 + (sl + 1) * 8192]\
                .rearrange("p (ft o) -> p ft o", ft=8)

        def cv_load(s):
            sl = s % 2
            srcw = cx["cmv_s"][l][s * 1024:(s + 1) * 1024].rearrange(
                "(ft p) o -> p ft o", p=128)
            dq, dsem = [("gpsimd", "gd"), ("sync", "sd"), ("scalar", "ad"),
                        ("gpsimd", "gd")][s]
            S.op(dq, lambda e, sl=sl, srcw=srcw: e.dma_start(
                cv_slot(sl), srcw),
                waits=[("gd", prep), fget(cx, f"CV{sl}")], inc=dsem, inc_n=16)
            return (dsem, S.m(dsem))
        cv_sd = {0: cv_load(0), 1: cv_load(1)}
        for s in range(4):
            slot3 = cv_slot(s % 2)
            for fi in range(8):
                ft = s * 8 + fi
                buf = sq_a if ft < 16 else sq_b
                ft_sq = buf[:, (ft % 16) * CH:(ft % 16) * CH + CH]

                def fvv(t, slot3=slot3, fi=fi, ft=ft, ft_sq=ft_sq):
                    inst = None
                    for g in range(8):
                        inst = t.matmul(
                            PS[:, g * 512:g * 512 + CH],
                            slot3[:, fi, g * 128:(g + 1) * 128],
                            ft_sq, start=(ft == 0), stop=(ft == NFT - 1))
                    return inst
                wv = [cv_sd[s], ("v", last_sq_v)]
                if ft == 0:
                    wv += [cx["bankfree"][i] for i in range(8)]
                S.op("tensor", fvv, waits=wv, inc="t")
            fset(cx, f"CV{s % 2}", "t", S.m("t"))
            if s + 2 < 4:
                cv_sd[s + 2] = cv_load(s + 2)
        vv_t = S.m("t")
        S.op("scalar", lambda e: e.activation(
            b1r[:, 6:8], r3(PS[:, 6 * 512:8 * 512], 2)[:, :, 0:CH], AF.Copy),
            waits=[("t", vv_t), fget(cx, "B1")], inc="s")
        ev67_s = S.m("s")
        cx["bankfree"][6] = ("s", ev67_s)
        cx["bankfree"][7] = ("s", ev67_s)
        S.op("scalar", lambda e: e.activation(
            b1r[:, 0:6], r3(PS[:, 0:6 * 512], 6)[:, :, 0:CH], AF.Copy),
            inc="s")
        vv_s = S.m("s")
        for i in range(6):
            cx["bankfree"][i] = ("s", vv_s)
        fset(cx, "B1", "s", vv_s)
        fset(cx, "B2", "t", vv_t)
        fset(cx, "B3", "t", vv_t)

        if cx.get("dbgcm") == (l, c):
            _dmp(S, cx, 8, b1r, [("s", vv_s)])
            dwait1 = [("sd", S.m("sd"))]
            _dmp(S, cx, 12, r3(sq_a[:, 0:2048], NCT), [("v", last_sq_v)],
                 cast=True)
            _dmp(S, cx, 10, mxa, [("v", mixk_v)], cast=True)
            _dmp(S, cx, 11, mxc, [("v", mixr_v)], cast=True)
            _dmp(S, cx, 13, P["xh"], [("v", P["xhat_v"])], cast=True)

        S.op("scalar", lambda e: e.activation(b2r, b0r, AF.Ln),
             waits=[("v", d_v), fget(cx, "B2")], inc="s")
        S.op("scalar", lambda e: e.activation(b0r, b2r, AF.Exp, scale=-1.0),
             inc="s")
        rec_s = S.m("s")
        S.op("vector", lambda e: e.tensor_tensor(b1r, b1r, b0r, ALU.mult),
             waits=[("s", rec_s), ("s", vv_s)] + dwait1, inc="v")
        xr = r3(XRES[:], NCT)[:, :, c * CH:(c + 1) * CH]
        S.op("vector", lambda e, xr=xr: e.tensor_tensor(xr, xr, b1r, ALU.add),
             waits=list(cx["xres_ready"]), inc="v")
        done_v = S.m("v")
        fset(cx, "B0", "v", done_v)
        fset(cx, "B1", "v", done_v)
        fset(cx, "B2", "s", rec_s)
        fset(cx, "B3", "s", rec_s)
        fset(cx, "SQ", "s", rec_s)
        fset(cx, f"MXA{par}", "t", S.m("t"))
        cx["xres_ready"] = [("v", done_v)]


def emit_ffn(S, cx, p, site):
    PS = cx["PS"]
    B0, B1, B2 = cx["B0"], cx["B1"], cx["B2"]
    ARENA, XRES = cx["ARENA"], cx["XRES"]
    waits = [("t", S.m("t"))]
    for m, w in enumerate([cx["frT"], cx["fkT"], cx["fvT"]]):
        srcw = w.rearrange("(ct p) o -> p ct o", p=128)
        dst = r3(ARENA[:], 4)[:, m].rearrange("p (ct o) -> p ct o", ct=NCT)
        S.op("gpsimd", lambda e, dst=dst, srcw=srcw: e.dma_start(dst, srcw),
             waits=waits if m == 0 else [], inc="gd", inc_n=16)
    cx["arena_ready"] = ("gd", S.m("gd"))

    b0r = r3(B0[:], NCT)
    b1r = r3(B1[:], NCT)
    b2f = r3(B2[:, 0:NCT * CH], NCT)
    sq3 = r3(B2[:, 0:NCT * CH].bitcast(BF16)[:, 0:NCT * CH], NCT)

    mixes = [("fAr", "fBr", None, "A"), ("fAk", "fBk", None, "B")]
    preps = [emit_ln_prep(S, cx, 0, site, mixes)]
    for c in range(NCH):
        P = preps[c]
        par = P["par"]
        mxa, mixr_v = P["mix"]["A"]
        mxb, mixk_v = P["mix"]["B"]

        def ev_r(r, glo, ghi, blo, mm_t):
            S.op("scalar", lambda e, glo=glo, ghi=ghi, blo=blo: e.activation(
                b0r[:, glo:ghi], ps_banks(cx, blo, ghi - glo), AF.Exp,
                scale=-1.0),
                waits=[("t", mm_t), fget(cx, "B0")], inc="s")
            for i in range(ghi - glo):
                cx["bankfree"][blo + i] = ("s", S.m("s"))
        emit_proj(S, cx, 0, mxa, ("v", mixr_v), ev_r)
        erm_s = S.m("s")
        fset(cx, "B0", "s", erm_s)
        fset(cx, f"MXA{par}", "t", S.m("t"))
        if c + 1 < NCH:
            pre_next = emit_ln_prep_pre(S, cx, c + 1)

        def ev_k(r, glo, ghi, blo, mm_t):
            S.op("vector", lambda e, glo=glo, ghi=ghi, blo=blo: e.tensor_scalar(
                sq3[:, glo:ghi], ps_banks(cx, blo, ghi - glo), 0.0, None,
                ALU.max),
                waits=[("t", mm_t), fget(cx, "B2")], inc="v")
            S.op("vector", lambda e, glo=glo, ghi=ghi: e.tensor_tensor(
                sq3[:, glo:ghi], sq3[:, glo:ghi], sq3[:, glo:ghi], ALU.mult),
                inc="v")
            for i in range(ghi - glo):
                cx["bankfree"][blo + i] = ("v", S.m("v"))
        emit_proj(S, cx, 1, mxb, ("v", mixk_v), ev_k)
        sq_v = S.m("v")
        fset(cx, "B2", "v", sq_v)
        fset(cx, f"MXB{par}", "t", S.m("t"))

        def ev_v(r, glo, ghi, blo, mm_t):
            S.op("scalar", lambda e, glo=glo, ghi=ghi, blo=blo: e.activation(
                b1r[:, glo:ghi], ps_banks(cx, blo, ghi - glo), AF.Copy),
                waits=[("t", mm_t), fget(cx, "B1")], inc="s")
            for i in range(ghi - glo):
                cx["bankfree"][blo + i] = ("s", S.m("s"))
        emit_proj(S, cx, 2, sq3, ("v", sq_v), ev_v)
        vv_s = S.m("s")
        fset(cx, "B1", "s", vv_s)
        fset(cx, "B2", "t", S.m("t"))

        if c + 1 < NCH:
            pr_next = emit_ln_prep_stats(S, cx, c + 1, site, pre=pre_next)

        S.op("vector", lambda e: e.tensor_scalar(b0r, b0r, 1.0, None, ALU.add),
             waits=[("s", erm_s)], inc="v")
        d_v = S.m("v")
        S.op("scalar", lambda e: e.activation(b2f, b0r, AF.Ln),
             waits=[("v", d_v), fget(cx, "B2")], inc="s")
        S.op("scalar", lambda e: e.activation(b0r, b2f, AF.Exp, scale=-1.0),
             inc="s")
        rec_s = S.m("s")
        S.op("vector", lambda e: e.tensor_tensor(b1r, b1r, b0r, ALU.mult),
             waits=[("s", rec_s), ("s", vv_s)], inc="v")
        xr = r3(XRES[:], NCT)[:, :, c * CH:(c + 1) * CH]
        S.op("vector", lambda e, xr=xr: e.tensor_tensor(xr, xr, b1r, ALU.add),
             waits=list(cx["xres_ready"]), inc="v")
        done_v = S.m("v")
        fset(cx, "B0", "v", done_v)
        fset(cx, "B1", "v", done_v)
        fset(cx, "B2", "s", rec_s)
        cx["xres_ready"] = [("v", done_v)]
        if c + 1 < NCH:
            preps.append(emit_ln_prep_finish(S, cx, pr_next, mixes))


def emit_attn(S, cx, p):
    PS, RWB, PAR, PARB = cx["PS"], cx["RWB"], cx["PAR"], cx["PARB"]
    B0, B1, B2, B3 = cx["B0"], cx["B1"], cx["B2"], cx["B3"]
    ARENA, XRES = cx["ARENA"], cx["XRES"]
    ksp, vtsp, qsp = cx["ksp"], cx["vtsp"], cx["qsp"]
    poff = p * NCH

    waits = [("t", S.m("t"))]
    for m, w in enumerate([cx["wqT"], cx["wkT"], cx["wvT"], cx["woT"]]):
        srcw = w.rearrange("(ct p) o -> p ct o", p=128)
        dst = r3(ARENA[:], 4)[:, m].rearrange("p (ct o) -> p ct o", ct=NCT)
        S.op("gpsimd", lambda e, dst=dst, srcw=srcw: e.dma_start(dst, srcw),
             waits=waits if m == 0 else [], inc="gd", inc_n=16)
    cx["arena_ready"] = ("gd", S.m("gd"))

    rwb3 = r3(RWB[:], NCT)
    stg = B2[:].bitcast(BF16)[:, 0:8 * 2 * 130]
    stg4 = stg.rearrange("p (pr tt w) -> p pr tt w", pr=8, tt=2)

    S.op("vector", lambda e: e.memset(
        stg.rearrange("p (b w) -> p b w", b=16, w=130)[:, :, 64:130:65], 1.0),
        waits=[fget(cx, "B2")], inc="v")
    ones_v = S.m("v")
    preps = [emit_ln_prep(S, cx, 0, None, [])]
    for c in range(NCH):
        if c + 1 < NCH:
            preps.append(emit_ln_prep(S, cx, c + 1, None, []))
        P = preps[c]
        par = P["par"]
        xh3 = P["xh"]
        xhat_v = P["xhat_v"]
        gcol = p * HALF + c * CH

        def ev_q(r, glo, ghi, blo, mm_t):
            for g in range(glo, ghi):
                bq = PAR[:, pcol("bq") + g:pcol("bq") + g + 1]
                bank = PS[:, (blo + g - glo) * 512:(blo + g - glo) * 512 + CH]
                S.op("scalar", lambda e, g=g, bank=bank, bq=bq: e.activation(
                    rwb3[:, g], bank, AF.Identity, bias=bq),
                    waits=[("t", mm_t), fget(cx, "RWB")], inc="s")
                cx["bankfree"][blo + g - glo] = ("s", S.m("s"))
        emit_proj(S, cx, 0, xh3, ("v", xhat_v), ev_q)
        q_s = S.m("s")
        S.op("sync", lambda e, gcol=gcol: e.dma_start(
            qsp.rearrange("(ct p) t -> p ct t", p=128)[:, :, gcol:gcol + CH],
            rwb3), waits=[("s", q_s)], inc="sd", inc_n=16)
        fset(cx, "RWB", "sd", S.m("sd"))

        def ev_k(r, glo, ghi, blo, mm_t):
            for g in range(glo, ghi):
                bk = PAR[:, pcol("bk") + g:pcol("bk") + g + 1]
                bank = PS[:, (blo + g - glo) * 512:(blo + g - glo) * 512 + CH]
                S.op("scalar", lambda e, g=g, bank=bank, bk=bk: e.activation(
                    rwb3[:, g], bank, AF.Identity, bias=bk),
                    waits=[("t", mm_t), fget(cx, "RWB")], inc="s")
                cx["bankfree"][blo + g - glo] = ("s", S.m("s"))
        emit_proj(S, cx, 1, xh3, ("v", xhat_v), ev_k)
        k_s = S.m("s")
        S.op("sync", lambda e, gcol=gcol: e.dma_start(
            ksp.rearrange("(ct p) t -> p ct t", p=128)[:, :, gcol:gcol + CH],
            rwb3), waits=[("s", k_s)], inc="sd", inc_n=16)
        fset(cx, "RWB", "sd", S.m("sd"))

        wv3 = r3(ARENA[:], 4)[:, 2].rearrange("p (ct o) -> p ct o", ct=NCT)
        vmm_t = 0
        for tt in range(2):
            for nh in range(2):
                bank = PS[:, nh * 512:nh * 512 + 512]

                def fmm(t, tt=tt, nh=nh, bank=bank, P=P):
                    inst = None
                    for ct in range(NCT):
                        inst = t.matmul(
                            bank,
                            P["xh"][:, ct, tt * 128:tt * 128 + 128],
                            wv3[:, ct, nh * 512:(nh + 1) * 512],
                            start=(ct == 0), stop=(ct == NCT - 1))
                    return inst
                S.op("tensor", fmm, waits=[("v", xhat_v),
                                           cx["bankfree"][nh],
                                           cx["arena_ready"]], inc="t")
            vmm_t = S.m("t")

            def ev(e, tt=tt):
                src_ap = PS[:, 0:1024].rearrange(
                    "p (pr h w) -> p pr h w", pr=8, h=2, w=64)
                dst_ap = stg4[:, :, tt, :].rearrange(
                    "p pr (h z) -> p pr h z", h=2, z=65)[:, :, :, 0:64]
                return e.activation(dst_ap, src_ap, AF.Copy)
            S.op("scalar", ev, waits=[("t", vmm_t), ("v", ones_v),
                                      fget(cx, "B2")], inc="s")
            cx["bankfree"][0] = ("s", S.m("s"))
            cx["bankfree"][1] = ("s", S.m("s"))
        fset(cx, f"XHAT{par}", "t", vmm_t)
        vstg_s = S.m("s")
        dst = vtsp.rearrange("p (pr tt w) -> p pr tt w", pr=8, tt=32,
                             w=130)[:, :, 2 * (poff + c):2 * (poff + c) + 2, :]
        S.op("sync", lambda e, dst=dst: e.dma_start(dst, stg4),
             waits=[("s", vstg_s)], inc="sd", inc_n=16)
        fset(cx, "B2", "sd", S.m("sd"))
    spills_sd = S.m("sd")

    kslot = [B0[:].bitcast(BF16), B1[:].bitcast(BF16)]
    vslot = [B2[:].bitcast(BF16), B3[:].bitcast(BF16)]
    ptbuf = cx["XHATp"][0][:]
    slot_free = {("k", 0): [fget(cx, "B0")], ("k", 1): [fget(cx, "B1")],
                 ("v", 0): [fget(cx, "B2")], ("v", 1): [fget(cx, "B3")]}
    cx["free"]["PT0"] = fget(cx, "XHAT0")
    cx["free"]["PT1"] = fget(cx, "XHAT0")
    cx["free"]["OB0"] = fget(cx, "XHAT0")
    cx["free"]["OB1"] = fget(cx, "XHAT0")
    SQF = cx["SQB"][:].bitcast(F32)
    QB = cx["MXAp"][0]

    for c in range(NCH):
        q0 = p * HALF + c * CH
        nkt = (q0 + CH) // 128
        S.op("sync", lambda e, q0=q0: e.dma_start(
            r3(QB[:], NCT),
            qsp.rearrange("(ct p) t -> p ct t", p=128)[:, :, q0:q0 + CH]),
            waits=[("sd", spills_sd), fget(cx, "MXA0")], inc="sd", inc_n=16)
        qd = S.m("sd")

        for pr in range(8):
            ks = kslot[pr % 2]
            vs = vslot[pr % 2]
            S.op("sync", lambda e, nkt=nkt, ks=ks, pr=pr: e.dma_start(
                ks[:, 0:nkt * 128], ksp[pr * 128:(pr + 1) * 128, 0:nkt * 128]),
                waits=[("sd", spills_sd)] + slot_free[("k", pr % 2)],
                inc="sd", inc_n=16)
            kd = S.m("sd")
            S.op("gpsimd", lambda e, nkt=nkt, vs=vs, pr=pr: e.dma_start(
                vs[:, 0:nkt * 130],
                vtsp.rearrange("p (pr w) -> p pr w", pr=8, w=32 * 130)[
                    :, pr, 0:nkt * 130]),
                waits=[("sd", spills_sd)] + slot_free[("v", pr % 2)],
                inc="gd", inc_n=16)
            vd = ("gd", S.m("gd"))

            def emit_sc(kt):
                sb = [0, 2][kt % 2]

                def fsc(t, kt=kt, sb=sb, ks=ks, pr=pr):
                    t.matmul(PS[:, sb * 512:sb * 512 + CH],
                             ks[0:64, kt * 128:(kt + 1) * 128],
                             r3(QB[:], NCT)[0:64, pr, :],
                             start=True, stop=True, tile_position=(0, 0))
                    return t.matmul(PS[:, (sb + 1) * 512:(sb + 1) * 512 + CH],
                                    ks[64:128, kt * 128:(kt + 1) * 128],
                                    r3(QB[:], NCT)[64:128, pr, :],
                                    start=True, stop=True,
                                    tile_position=(64, 0))
                S.op("tensor", fsc, waits=[("sd", kd), ("sd", qd),
                                           cx["bankfree"][sb],
                                           cx["bankfree"][sb + 1]], inc="t")
                return S.m("t")

            sc_ts = {0: emit_sc(0)}
            for kt in range(nkt):
                sb = [0, 2][kt % 2]
                pt = kt % 2
                ptap = ptbuf[:, pt * 512:pt * 512 + 512]
                sc_t = sc_ts[kt]

                def fex(e, sb=sb, ptap=ptap):
                    src = PS[:, sb * 512:sb * 512 + 1024].rearrange(
                        "p (h w) -> p h w", h=2, w=512)[:, :, 0:CH]
                    return e.activation(
                        ptap.rearrange("p (h w) -> p h w", h=2, w=CH), src,
                        AF.Exp, scale=0.125)
                S.op("scalar", fex, waits=[("t", sc_t), fget(cx, f"PT{pt}")],
                     inc="s")
                ex_s = S.m("s")
                cx["bankfree"][sb] = ("s", ex_s)
                cx["bankfree"][sb + 1] = ("s", ex_s)
                avw = [("s", ex_s), vd]
                if kt * 128 > q0 - 128:
                    def fmask(e, kt=kt, ptap=ptap, q0=q0):
                        i = None
                        for h in range(2):
                            i = e.affine_select(
                                ptap[:, h * CH:(h + 1) * CH],
                                ptap[:, h * CH:(h + 1) * CH],
                                [[1, CH]], ALU.is_ge, 0.0,
                                base=q0 - kt * 128, channel_multiplier=-1)
                        return i
                    S.op("gpsimd", fmask, waits=[("s", ex_s)], inc="g")
                    avw = [("g", S.m("g")), vd]
                if kt + 1 < nkt:
                    sc_ts[kt + 1] = emit_sc(kt + 1)

                def fav(t, kt=kt, vs=vs, ptap=ptap, first=(kt == 0),
                        last=(kt == nkt - 1)):
                    t.matmul(PS[0:64, 4 * 512:4 * 512 + CH],
                             vs[:, kt * 130:kt * 130 + 64],
                             ptap[:, 0:CH], start=first, stop=last)
                    t.matmul(PS[64:128, 5 * 512:5 * 512 + CH],
                             vs[:, kt * 130 + 65:kt * 130 + 129],
                             ptap[:, CH:2 * CH], start=first, stop=last,
                             tile_position=(0, 64))
                    t.matmul(PS[0:1, 5 * 512 + CH:5 * 512 + 2 * CH],
                             vs[:, kt * 130 + 64:kt * 130 + 65],
                             ptap[:, 0:CH], start=first, stop=last)
                    return t.matmul(PS[64:65, 4 * 512 + CH:4 * 512 + 2 * CH],
                                    vs[:, kt * 130 + 129:kt * 130 + 130],
                                    ptap[:, CH:2 * CH], start=first, stop=last,
                                    tile_position=(0, 64))
                avw2 = avw + ([cx["bankfree"][4], cx["bankfree"][5]]
                              if kt == 0 else [])
                S.op("tensor", fav, waits=avw2, inc="t")
                fset(cx, f"PT{pt}", "t", S.m("t"))
            av_t = S.m("t")
            slot_free[("k", pr % 2)] = [("t", av_t)]
            slot_free[("v", pr % 2)] = [("t", av_t)]

            for h in range(2):
                pb = 0 if h == 0 else 64
                avbank = PS[0 + 64 * h:64 + 64 * h,
                            (4 + h) * 512:(4 + h) * 512 + CH]
                den_ap = (PS[0:1, 5 * 512 + CH:5 * 512 + 2 * CH] if h == 0
                          else PS[64:65, 4 * 512 + CH:4 * 512 + 2 * CH])
                rrow = SQF[pb:pb + 1, 0:CH]
                rrow2 = SQF[pb:pb + 1, CH:2 * CH]
                brow = PARB[pb:pb + 1, PB_MU:PB_MU + CH]
                S.op("scalar", lambda e, den_ap=den_ap, rrow=rrow: e.activation(
                    rrow, den_ap, AF.Ln),
                    waits=[("t", av_t), fget(cx, f"DROW{h}")], inc="s")
                S.op("scalar", lambda e, rrow=rrow, rrow2=rrow2: e.activation(
                    rrow2, rrow, AF.Exp, scale=-1.0), inc="s")
                S.op("scalar", lambda e, rrow2=rrow2, brow=brow: e.activation(
                    brow, rrow2, AF.Copy), inc="s")
                row_s = S.m("s")
                bb = 6 + h

                def fbc(e, brow=brow, pb=pb, bb=bb, h=h):
                    ones = PARB[pb:pb + 1, PB_ONESROW:PB_ONESROW + 64]
                    if h == 0:
                        return e.matmul(PS[0:64, bb * 512:bb * 512 + CH],
                                        ones, brow, start=True, stop=True)
                    return e.matmul(PS[64:128, bb * 512:bb * 512 + CH],
                                    ones, brow, start=True, stop=True,
                                    tile_position=(64, 64))
                S.op("tensor", fbc, waits=[("s", row_s), cx["bankfree"][bb]],
                     inc="t")
                bc_t = S.m("t")
                fset(cx, f"DROW{h}", "t", bc_t)
                obuf = ptbuf[0 + 64 * h:64 + 64 * h, 1024:1024 + CH]
                S.op("scalar", lambda e, avbank=avbank, obuf=obuf: e.activation(
                    obuf, avbank, AF.Copy),
                    waits=[("t", av_t), fget(cx, f"OB{h}")], inc="s")
                ob_s = S.m("s")
                dst = rwb3[64 * h:64 * h + 64, pr, :]
                bvap = PAR[64 * h:64 * h + 64,
                           pcol("bv") + pr:pcol("bv") + pr + 1]
                bcread = (PS[0:64, bb * 512:bb * 512 + CH] if h == 0 else
                          PS[64:128, bb * 512:bb * 512 + CH])
                S.op("vector", lambda e, obuf=obuf, dst=dst, bcread=bcread:
                     e.tensor_tensor(dst, obuf, bcread, ALU.mult),
                     waits=[("s", ob_s), ("t", bc_t), fget(cx, "RWB")],
                     inc="v")
                S.op("vector", lambda e, dst=dst, bvap=bvap: e.tensor_scalar(
                    dst, dst, bvap, None, ALU.add), inc="v")
                fin_v = S.m("v")
                fset(cx, f"OB{h}", "v", fin_v)
                cx["bankfree"][bb] = ("v", fin_v)
            cx["bankfree"][4] = ("v", fin_v)
            cx["bankfree"][5] = ("v", fin_v)
        ofin_v = S.m("v")
        fset(cx, "RWB", "v", ofin_v)
        fset(cx, "MXA0", "t", S.m("t"))

        xr = r3(XRES[:], NCT)[:, :, c * CH:(c + 1) * CH]

        def ev_o(r, glo, ghi, blo, mm_t, xr=xr):
            def fn(e, glo=glo, ghi=ghi, blo=blo, xr=xr):
                inst = None
                for g in range(glo, ghi):
                    bank = PS[:, (blo + g - glo) * 512:
                              (blo + g - glo) * 512 + CH]
                    inst = e.scalar_tensor_tensor(
                        xr[:, g], bank, PAR[:, pcol("bo") + g:pcol("bo") + g + 1],
                        xr[:, g], ALU.add, ALU.add)
                return inst
            S.op("vector", fn, waits=[("t", mm_t)] + cx["xres_ready"], inc="v")
            for i in range(ghi - glo):
                cx["bankfree"][blo + i] = ("v", S.m("v"))
        emit_proj(S, cx, 3, rwb3, ("v", ofin_v), ev_o)
        oadd_v = S.m("v")
        fset(cx, "RWB", "t", S.m("t"))
        cx["xres_ready"] = [("v", oadd_v)]

    end_t = S.m("t")
    for nm in ("B0", "B1", "B2", "B3", "XHAT0"):
        fset(cx, nm, "t", end_t)
    fset(cx, "SQB", "s", S.m("s"))


def _prep_inputs(inputs):
    def tr(a):
        return np.ascontiguousarray(np.swapaxes(np.asarray(a), -1, -2))

    g = {k: np.asarray(v) for k, v in inputs.items()}
    com = {}
    com["tmkT"] = tr(g["L0_attn_a_tm_k_w"])
    com["tmvT"] = tr(g["L0_attn_a_tm_v_w"])
    com["tmrT"] = tr(g["L0_attn_a_tm_r_w"])
    com["tmoT"] = tr(g["L0_attn_a_tm_o_w"])
    com["cmkT"] = tr(g["L0_attn_a_cm_k_w"])
    com["cmvT"] = tr(g["L0_attn_a_cm_v_w"])
    com["cmrT"] = tr(g["L0_attn_a_cm_r_w"])
    wqkv = g["L0_attn_b_W_qkv"]
    com["wqT"] = tr(wqkv[0:1024])
    com["wkT"] = tr(wqkv[1024:2048])
    com["wvT"] = tr(wqkv[2048:3072])
    com["woT"] = tr(g["L0_attn_b_W_o"])
    com["frT"] = tr(g["L0_ffn_W_r"])
    com["fkT"] = tr(g["L0_ffn_W_k"])
    com["fvT"] = tr(g["L0_ffn_W_v"])

    def col8(v):
        return np.ascontiguousarray(np.asarray(v).reshape(8, 128).T)

    cols = {}
    for l in range(NL):
        cols[f"raw_ln1w{l}"] = col8(g["L0_attn_a_ln1_w"][l])
        cols[f"raw_ln2w{l}"] = col8(g["L0_attn_a_ln2_w"][l])
        cols[f"raw_mk{l}"] = col8(g["L0_attn_a_tm_mix_k"][l, 0, 0])
        cols[f"raw_mv{l}"] = col8(g["L0_attn_a_tm_mix_v"][l, 0, 0])
        cols[f"raw_mr{l}"] = col8(g["L0_attn_a_tm_mix_r"][l, 0, 0])
        cols[f"raw_cmk{l}"] = col8(g["L0_attn_a_cm_mix_k"][l, 0, 0])
        cols[f"raw_cmr{l}"] = col8(g["L0_attn_a_cm_mix_r"][l, 0, 0])
        cols[f"raw_td{l}"] = col8(g["L0_attn_a_time_decay"][l])
        cols[f"raw_tf{l}"] = col8(g["L0_attn_a_time_first"][l])
    cols["raw_mur"] = col8(g["L0_ffn_mu_r"])
    cols["raw_muk"] = col8(g["L0_ffn_mu_k"])
    cols["lnfw"] = col8(g["L0_attn_a_lnf_w"])
    bqkv = g["L0_attn_b_b_qkv"]
    cols["bq"] = col8(bqkv[0:1024])
    cols["bk"] = col8(bqkv[1024:2048])
    cols["bv"] = col8(bqkv[2048:3072])
    cols["bo"] = col8(g["L0_attn_b_b_o"])
    vec = np.concatenate([cols[n] for n in VEC_ORDER], axis=1).astype(np.float32)
    com["vecs"] = np.ascontiguousarray(vec)

    x = g["L0_attn_a_x"]
    per_core = []
    for b in range(8):
        m = dict(com)
        m["xt"] = np.ascontiguousarray(x[b].T.astype(np.float32))
        per_core.append(m)
    return per_core


def kernel(**inputs):
    in_maps = _prep_inputs(inputs)
    nc = build_program()
    res = run_bass_kernel_spmd(nc, in_maps, core_ids=list(range(8)))
    outs = [np.ascontiguousarray(np.asarray(r["out"]).T) for r in res.results]
    return np.stack(outs, axis=0).astype(np.float32)
